# version 01
# baseline (speedup 1.0000x reference)
import sys
sys.path.insert(0, '/opt/trn_rl_repo')
import numpy as np
import ml_dtypes
from contextlib import ExitStack

import concourse.bass as bass
import concourse.mybir as mybir
from concourse.bass_utils import run_bass_kernel_spmd

F32 = mybir.dt.float32
BF16 = mybir.dt.bfloat16
AF = mybir.ActivationFunctionType
ALU = mybir.AluOpType
AX = mybir.AxisListType

S = 8192
D = 1024
NKT = 8
H = 16
DH = 64
W = 256
NW = S // W
DI2 = 5460
DI = 2730
NL = 6
ATILES = [(i * 128, 128) for i in range(21)] + [(2688, 42)]
NAT = len(ATILES)

bfnp = ml_dtypes.bfloat16


def _build_consts():
    t = np.arange(S, dtype=np.float64)
    half = np.arange(0, DH, 2, dtype=np.float64)
    inv_freq = 1.0 / 10000.0 ** (half / DH)
    freqs = t[:, None] * inv_freq[None, :]
    freqs = np.concatenate([freqs, freqs], -1)
    base_scale = (half + 0.4 * DH) / (1.4 * DH)
    power = (t - S // 2) / (W // 2)
    scale = base_scale[None, :] ** power[:, None]
    scale = np.concatenate([scale, scale], -1)
    cos = np.cos(freqs); sin = np.sin(freqs)
    sgn = np.concatenate([-np.ones(32), np.ones(32)])[None, :]
    consts = {
        "cs_q": (8.0 * cos * scale).astype(np.float32),
        "ss_q": (8.0 * sin * scale * sgn).astype(np.float32),
        "cs_k": (cos / scale).astype(np.float32),
        "ss_k": (sin / scale * sgn).astype(np.float32),
    }
    kl = np.arange(128)[:, None]; ql = np.arange(128)[None, :]
    consts["triA"] = (ql <= kl).astype(bfnp)
    consts["triB"] = (ql >= kl).astype(bfnp)
    consts["ident"] = np.eye(128, dtype=bfnp)
    cv = np.zeros((128, 8), np.float32)
    cv[:, 0] = 1e-5; cv[:, 1] = -0.5; cv[:, 2] = -1.0; cv[:, 3] = -2.0
    cv[:, 4] = 1.0 / D; cv[:, 5] = 0.25; cv[:, 6] = 1.0
    consts["cvals"] = cv
    return consts


def _col(v):
    return np.ascontiguousarray(v.reshape(-1, 128).T)


def _colpad(v, n):
    out = np.zeros(n * 128, v.dtype)
    out[: v.shape[0]] = v
    return _col(out)


class Sem:
    def __init__(self, es, nc, name):
        self.h = es.enter_context(nc.semaphore(name))
        self.n = 0

    def inc(self, inst, k=1):
        inst.then_inc(self.h, k)
        self.n += k
        return inst


def wge(eng, sem, lag=0):
    v = sem.n - lag
    if v > 0:
        eng.wait_ge(sem.h, v)


class Prog:
    def __init__(self, nc, es, block):
        self.nc = nc
        self.es = es
        self.block = block
        self.sem = {}
        for name in ["LD", "LD2", "LD3", "MM", "EV", "AC", "ST"]:
            self.sem[name] = Sem(es, nc, name)
        self.cv = None

    def s(self, name):
        return self.sem[name]

    def c(self, i):
        return self.cv[:, i:i + 1]

    def c1(self, i):
        return self.cv[0:1, i:i + 1]

    def barrier(self):
        for eng_open in [self.block.sync, self.block.tensor, self.block.vector,
                         self.block.scalar, self.block.gpsimd]:
            def _b(eng):
                for s in self.sem.values():
                    wge(eng, s)
            eng_open(_b)


def mm_group(tensor, out_ap, pairs, sem=None, k=1):
    inst = None
    n = len(pairs)
    for i, (l, r) in enumerate(pairs):
        inst = tensor.matmul(out_ap, l, r, start=(i == 0), stop=(i == n - 1))
    if sem is not None:
        sem.inc(inst, k)
    return inst


def rr8(ap, inner):
    return ap.rearrange("p (k t) -> p k t", t=inner)


def drview(dram, T0, T1):
    return dram.rearrange("(k p) t -> p k t", p=128)[:, :, T0:T1]


def emit_ln(p, d, bufs, T, g_ap=None, b_ap=None, make_f32=True, make_bf=False):
    block = p.block
    MM, EV, AC = p.s("MM"), p.s("EV"), p.s("AC")
    t_f, tb, sqb, rows = bufs["t_f"], bufs["tb"], bufs["sqb"], bufs["rows"]
    onesc, onesf = bufs["onesc"], bufs["onesf"]

    def s_cast(scalar):
        wge(scalar, EV)
        AC.inc(scalar.copy(tb[:, 0:NKT * T], t_f[:, 0:NKT * T]))
    block.scalar(s_cast)

    def v_sq(vector):
        wge(vector, AC)
        EV.inc(vector.tensor_mul(sqb[:, 0:NKT * T], tb[:, 0:NKT * T], tb[:, 0:NKT * T]))
    block.vector(v_sq)

    def t_rows(tensor):
        wge(tensor, EV)
        mm_group(tensor, bufs["ps_sum"],
                 [(onesc[:], tb[:, k * T:(k + 1) * T]) for k in range(NKT)], MM)
        mm_group(tensor, bufs["ps_sumsq"],
                 [(onesc[:], sqb[:, k * T:(k + 1) * T]) for k in range(NKT)], MM)
    block.tensor(t_rows)

    def s_r1(scalar):
        wge(scalar, MM)
        AC.inc(scalar.activation(rows[:, 0:T], bufs["ps_sum"], AF.Identity, scale=p.c1(4)))
        wge(scalar, AC)
        AC.inc(scalar.activation(rows[:, T:2 * T], bufs["ps_sumsq"], AF.Identity, scale=p.c1(4)))
    block.scalar(s_r1)

    def v_r(vector):
        wge(vector, AC)
        EV.inc(vector.tensor_mul(rows[:, 2 * T:3 * T], rows[:, 0:T], rows[:, 0:T]))
        wge(vector, EV)
        EV.inc(vector.tensor_sub(rows[:, 2 * T:3 * T], rows[:, T:2 * T], rows[:, 2 * T:3 * T]))
    block.vector(v_r)

    def s_r2(scalar):
        wge(scalar, EV)
        AC.inc(scalar.activation(rows[:, 3 * T:4 * T], rows[:, 2 * T:3 * T], AF.Ln, bias=p.c1(0)))
        wge(scalar, AC)
        AC.inc(scalar.activation(rows[:, 3 * T:4 * T], rows[:, 3 * T:4 * T], AF.Exp, scale=p.c1(1)))
        wge(scalar, AC)
        AC.inc(scalar.activation(rows[:, 4 * T:5 * T], rows[:, 0:T], AF.Identity, scale=p.c1(2)))
    block.scalar(s_r2)

    def t_bc(tensor):
        wge(tensor, AC)
        mm_group(tensor, bufs["ps_bcmu"], [(onesf[:], rows[:, 4 * T:5 * T])], MM)
        mm_group(tensor, bufs["ps_bcr"], [(onesf[:], rows[:, 3 * T:4 * T])], MM)
    block.tensor(t_bc)

    for k in range(NKT):
        def v_ap(vector, k=k):
            wge(vector, MM)
            EV.inc(vector.tensor_add(t_f[:, k * T:(k + 1) * T], t_f[:, k * T:(k + 1) * T],
                                     bufs["ps_bcmu"]))
            wge(vector, EV)
            if g_ap is None:
                EV.inc(vector.tensor_mul(bufs["outf"][:, k * T:(k + 1) * T],
                                         t_f[:, k * T:(k + 1) * T], bufs["ps_bcr"]))
            else:
                EV.inc(vector.tensor_mul(t_f[:, k * T:(k + 1) * T],
                                         t_f[:, k * T:(k + 1) * T], bufs["ps_bcr"]))
                wge(vector, EV)
                EV.inc(vector.tensor_scalar(bufs["outb"][:, k * T:(k + 1) * T],
                                            t_f[:, k * T:(k + 1) * T],
                                            g_ap[:, k:k + 1], b_ap[:, k:k + 1],
                                            op0=ALU.mult, op1=ALU.add))
        block.vector(v_ap)

        if g_ap is None and make_bf:
            def s_cb(scalar, k=k):
                wge(scalar, EV)
                AC.inc(scalar.copy(bufs["outb"][:, k * T:(k + 1) * T],
                                   bufs["outf"][:, k * T:(k + 1) * T]))
            block.scalar(s_cb)


def emit_s0(p, d):
    nc, block = p.nc, p.block
    p.barrier()
    LD, MM, EV, AC, ST = (p.s(x) for x in ["LD", "MM", "EV", "AC", "ST"])
    es = ExitStack()
    with es:
        wq_sb = es.enter_context(nc.sbuf_tensor("s0_wq", [128, NKT * 1024], BF16))
        wk_sb = es.enter_context(nc.sbuf_tensor("s0_wk", [128, NKT * 1024], BF16))
        wv_sb = es.enter_context(nc.sbuf_tensor("s0_wv", [128, NKT * 1024], BF16))
        bq_sb = es.enter_context(nc.sbuf_tensor("s0_bq", [128, 8], F32))
        bkm2_sb = es.enter_context(nc.sbuf_tensor("s0_bkm2", [128, 8], F32))
        bvbc_sb = es.enter_context(nc.sbuf_tensor("s0_bvbc", [128, 1024], F32))
        onesc = es.enter_context(nc.sbuf_tensor("s0_ones", [128, 1], BF16))
        xb = [es.enter_context(nc.sbuf_tensor(f"s0_xb{i}", [128, NKT * 512], BF16)) for i in range(2)]
        qt = es.enter_context(nc.sbuf_tensor("s0_qt", [128, NKT * 512], BF16))
        kt = es.enter_context(nc.sbuf_tensor("s0_kt", [128, NKT * 512], BF16))
        sq = es.enter_context(nc.sbuf_tensor("s0_sq", [128, NKT * 512], BF16))
        vt = es.enter_context(nc.sbuf_tensor("s0_vt", [128, 1024], BF16))
        rowq = es.enter_context(nc.sbuf_tensor("s0_rowq", [1, 512], BF16))
        rowk = es.enter_context(nc.sbuf_tensor("s0_rowk", [1, 512], BF16))
        rone = es.enter_context(nc.sbuf_tensor("s0_rone", [1, 512], BF16))
        psA = es.enter_context(nc.psum_tensor("s0_psA", [128, 512], F32))
        psB = es.enter_context(nc.psum_tensor("s0_psB", [128, 512], F32))
        psV = es.enter_context(nc.psum_tensor("s0_psV", [128, 512], F32))
        psR = es.enter_context(nc.psum_tensor("s0_psR", [1, 512], F32))
        psR2 = es.enter_context(nc.psum_tensor("s0_psR2", [1, 512], F32))

        def gp_loads(gpsimd):
            for w_sb, w_dram in [(wq_sb, d["wq"]), (wk_sb, d["wk"]), (wv_sb, d["wv"])]:
                LD.inc(gpsimd.dma_start(
                    out=rr8(w_sb, 1024),
                    in_=w_dram.rearrange("(k p) o -> p k o", p=128)), 16)
        block.gpsimd(gp_loads)

        def sy_loads(sync):
            LD.inc(sync.dma_start(out=bq_sb[:], in_=d["bq_pt"]), 16)
            LD.inc(sync.dma_start(out=bkm2_sb[:], in_=d["bk_pt"]), 16)
            LD.inc(sync.dma_start(out=bvbc_sb[:], in_=d["bv_bc"]), 16)
        block.sync(sy_loads)

        def v_init(vector):
            wge(vector, LD)
            EV.inc(vector.memset(onesc[:], 1.0))
            EV.inc(vector.memset(rone[:], 1.0))
            wge(vector, EV)
            EV.inc(vector.tensor_scalar_mul(bkm2_sb[:], bkm2_sb[:], -2.0))
        block.vector(v_init)

        def sy_aug(sync):
            wge(sync, EV)
            for blk in range(16):
                ST.inc(sync.dma_start(out=d["qaug"][0:1, blk * 512:(blk + 1) * 512], in_=rone[:]), 16)
                ST.inc(sync.dma_start(out=d["kaug"][1:2, blk * 512:(blk + 1) * 512], in_=rone[:]), 16)
        block.sync(sy_aug)

        def load_blk(gpsimd, blk):
            LD.inc(gpsimd.dma_start(
                out=rr8(xb[blk % 2], 512),
                in_=drview(d["xT"], blk * 512, (blk + 1) * 512)), 16)

        block.gpsimd(lambda g: load_blk(g, 0))

        for blk in range(16):
            x_sb = xb[blk % 2]
            if blk + 1 < 16:
                def gp(g, blk=blk):
                    wge(g, MM)
                    load_blk(g, blk + 1)
                block.gpsimd(gp)

            for proj in range(2):
                w_sb = [wq_sb, wk_sb][proj]
                dst = [d["qt8"], d["kts8"]][proj]
                b_ap = [bq_sb, bkm2_sb][proj]
                scale_ap = [1.0, p.c(3)][proj]
                tgt_sb = [qt, kt][proj]
                for ot in range(8):
                    ps = psA if ot % 2 == 0 else psB
                    def t_mm(tensor, ot=ot, ps=ps, w_sb=w_sb, x_sb=x_sb):
                        wge(tensor, LD)
                        wge(tensor, AC)
                        mm_group(tensor, ps[:],
                                 [(w_sb[:, k * 1024 + ot * 128: k * 1024 + ot * 128 + 128],
                                   x_sb[:, k * 512:(k + 1) * 512]) for k in range(NKT)], MM)
                    block.tensor(t_mm)

                    def s_ev(scalar, ot=ot, ps=ps, b_ap=b_ap, scale_ap=scale_ap, tgt_sb=tgt_sb):
                        wge(scalar, MM)
                        wge(scalar, ST)
                        AC.inc(scalar.activation(tgt_sb[:, ot * 512:(ot + 1) * 512], ps[:],
                                                 AF.Identity, bias=b_ap[:, ot:ot + 1], scale=scale_ap))
                    block.scalar(s_ev)

                    def v_sq(vector, ot=ot, tgt_sb=tgt_sb):
                        wge(vector, AC)
                        EV.inc(vector.tensor_mul(sq[:, ot * 512:(ot + 1) * 512],
                                                 tgt_sb[:, ot * 512:(ot + 1) * 512],
                                                 tgt_sb[:, ot * 512:(ot + 1) * 512]))
                    block.vector(v_sq)

                def sy_st(sync, dst=dst, tgt_sb=tgt_sb, blk=blk):
                    wge(sync, AC)
                    ST.inc(sync.dma_start(
                        out=drview(dst, blk * 512, (blk + 1) * 512),
                        in_=rr8(tgt_sb, 512)), 16)
                block.sync(sy_st)

                psrow = psR if proj == 0 else psR2
                def t_row(tensor, psrow=psrow):
                    wge(tensor, EV)
                    mm_group(tensor, psrow[:],
                             [(onesc[:], sq[:, k * 512:(k + 1) * 512]) for k in range(NKT)], MM)
                block.tensor(t_row)

                rowsb = [rowq, rowk][proj]
                rsc_ap = [1.0, p.c1(5)][proj]
                def s_row(scalar, psrow=psrow, rowsb=rowsb, rsc_ap=rsc_ap):
                    wge(scalar, MM)
                    wge(scalar, ST)
                    AC.inc(scalar.activation(rowsb[:], psrow[:], AF.Identity, scale=rsc_ap))
                block.scalar(s_row)

                aug_dst = d["qaug"][1:2] if proj == 0 else d["kaug"][0:1]
                def sy_row(sync, aug_dst=aug_dst, rowsb=rowsb, blk=blk):
                    wge(sync, AC)
                    ST.inc(sync.dma_start(out=aug_dst[:, blk * 512:(blk + 1) * 512], in_=rowsb[:]), 16)
                block.sync(sy_row)

            for tt in range(4):
                for oc in range(2):
                    def t_v(tensor, tt=tt, oc=oc, x_sb=x_sb):
                        wge(tensor, EV)
                        mm_group(tensor, psV[:],
                                 [(x_sb[:, k * 512 + tt * 128: k * 512 + tt * 128 + 128],
                                   wv_sb[:, k * 1024 + oc * 512: k * 1024 + oc * 512 + 512])
                                  for k in range(NKT)], MM)
                    block.tensor(t_v)

                    def v_ev(vector, oc=oc):
                        wge(vector, MM)
                        wge(vector, ST)
                        EV.inc(vector.tensor_add(vt[:, oc * 512:(oc + 1) * 512], psV[:],
                                                 bvbc_sb[:, oc * 512:(oc + 1) * 512]))
                    block.vector(v_ev)

                def sy_v(sync, tt=tt, blk=blk):
                    wge(sync, EV)
                    ST.inc(sync.dma_start(
                        out=d["vb"][blk * 512 + tt * 128: blk * 512 + tt * 128 + 128, :],
                        in_=vt[:]), 16)
                block.sync(sy_v)

        def sy_fin(sync):
            wge(sync, ST)
        block.sync(sy_fin)


def emit_s1(p, d):
    nc, block = p.nc, p.block
    p.barrier()
    LD, LD2, LD3, MM, EV, AC, ST = (p.s(x) for x in ["LD", "LD2", "LD3", "MM", "EV", "AC", "ST"])
    es = ExitStack()
    with es:
        wo_sb = es.enter_context(nc.sbuf_tensor("s1_wo", [128, NKT * 1024], BF16))
        bo_sb = es.enter_context(nc.sbuf_tensor("s1_bo", [128, 8], F32))
        onesc = es.enter_context(nc.sbuf_tensor("s1_ones", [128, 1], BF16))
        onesf = es.enter_context(nc.sbuf_tensor("s1_onesf", [1, 128], F32))
        q8 = [es.enter_context(nc.sbuf_tensor(f"s1_q8_{i}", [128, NKT * 512], BF16)) for i in range(2)]
        qa2 = [es.enter_context(nc.sbuf_tensor(f"s1_qa2_{i}", [2, 512], BF16)) for i in range(2)]
        ktile = [es.enter_context(nc.sbuf_tensor(f"s1_kt{i}", [128, NKT * 128], BF16)) for i in range(2)]
        ka2 = [es.enter_context(nc.sbuf_tensor(f"s1_ka2_{i}", [2, 128], BF16)) for i in range(2)]
        fcol = es.enter_context(nc.sbuf_tensor("s1_fcol", [128, 64 * 512], BF16))
        etmp = [es.enter_context(nc.sbuf_tensor(f"s1_etmp{i}", [128, 512], F32)) for i in range(2)]
        vcol = es.enter_context(nc.sbuf_tensor("s1_vcol", [128, 64 * 128], BF16))
        x_sb = es.enter_context(nc.sbuf_tensor("s1_x", [128, NKT * 512], F32))
        attnT = es.enter_context(nc.sbuf_tensor("s1_attnT", [128, NKT * 512], BF16))
        rden = es.enter_context(nc.sbuf_tensor("s1_rden", [1, 512], F32))
        rb = es.enter_context(nc.sbuf_tensor("s1_rb", [128, 512], F32))
        tb = es.enter_context(nc.sbuf_tensor("s1_tb", [128, NKT * 512], BF16))
        rows = es.enter_context(nc.sbuf_tensor("s1_rows", [1, 5 * 512], F32))
        x1f = es.enter_context(nc.sbuf_tensor("s1_x1f", [128, NKT * 512], F32))
        x1b = es.enter_context(nc.sbuf_tensor("s1_x1b", [128, NKT * 512], BF16))
        psD = [es.enter_context(nc.psum_tensor(f"s1_psD{i}", [128, 512], F32)) for i in range(2)]
        psDen = es.enter_context(nc.psum_tensor("s1_psDen", [1, 512], F32))
        psO = [es.enter_context(nc.psum_tensor(f"s1_psO{i}", [128, 512], F32)) for i in range(2)]
        psBCa = es.enter_context(nc.psum_tensor("s1_psBCa", [128, 512], F32))
        psBCb = es.enter_context(nc.psum_tensor("s1_psBCb", [128, 512], F32))

        ln_bufs = dict(t_f=x_sb, tb=tb, sqb=attnT, rows=rows, onesc=onesc, onesf=onesf,
                       ps_sum=psDen[:], ps_sumsq=psBCa[0:1, :], ps_bcmu=psBCa[:],
                       ps_bcr=psBCb[:], outf=x1f, outb=x1b)

        def gp0(gpsimd):
            LD.inc(gpsimd.dma_start(
                out=rr8(wo_sb, 1024),
                in_=d["wo"].rearrange("(k p) o -> p k o", p=128)), 16)
        block.gpsimd(gp0)

        def sy0(sync):
            LD.inc(sync.dma_start(out=bo_sb[:], in_=d["bo_pt"]), 16)
        block.sync(sy0)

        def v0(vector):
            wge(vector, LD)
            EV.inc(vector.memset(onesc[:], 1.0))
            EV.inc(vector.memset(onesf[:], 1.0))
        block.vector(v0)

        def load_chunk(sync, c):
            i = c % 2
            LD2.inc(sync.dma_start(out=rr8(q8[i], 512),
                                   in_=drview(d["qt8"], c * 512, (c + 1) * 512)), 16)
            LD2.inc(sync.dma_start(out=qa2[i][:], in_=d["qaug"][:, c * 512:(c + 1) * 512]), 16)

        block.sync(lambda s: load_chunk(s, 0))

        def load_ktile(sync, kt_i):
            i = kt_i % 2
            LD.inc(sync.dma_start(out=rr8(ktile[i], 128),
                                  in_=drview(d["kts8"], kt_i * 128, (kt_i + 1) * 128)), 16)
            LD.inc(sync.dma_start(out=ka2[i][:], in_=d["kaug"][:, kt_i * 128:(kt_i + 1) * 128]), 16)

        def load_vcol(gpsimd, dt):
            LD3.inc(gpsimd.dma_start(
                out=vcol.rearrange("p (k c) -> p k c", k=64),
                in_=d["vb"].rearrange("(k p) c -> p k c", p=128)[:, :, dt * 128:(dt + 1) * 128]), 16)

        for c in range(16):
            ci = c % 2
            if c + 1 < 16:
                def syn(s, c=c):
                    wge(s, MM)
                    load_chunk(s, c + 1)
                block.sync(syn)

            def syx(s, c=c):
                wge(s, EV)
                wge(s, AC)
                LD2.inc(s.dma_start(out=rr8(x_sb, 512),
                                    in_=drview(d["xT"], c * 512, (c + 1) * 512)), 16)
            block.sync(syx)

            def sy_k0(s):
                wge(s, MM)
                load_ktile(s, 0)
            block.sync(sy_k0)

            for kt_i in range(64):
                kti = kt_i % 2
                if kt_i + 1 < 64:
                    def syk(s, kt_i=kt_i):
                        wge(s, MM)
                        load_ktile(s, kt_i + 1)
                    block.sync(syk)

                ps = psD[kt_i % 2]
                def t_d(tensor, kti=kti, ps=ps, ci=ci):
                    wge(tensor, LD)
                    wge(tensor, LD2)
                    wge(tensor, AC, lag=1)
                    pairs = [(ktile[kti][:, k * 128:(k + 1) * 128], q8[ci][:, k * 512:(k + 1) * 512])
                             for k in range(NKT)]
                    pairs.append((ka2[kti][:], qa2[ci][:]))
                    mm_group(tensor, ps[:], pairs, MM)
                block.tensor(t_d)

                def s_e(scalar, ps=ps, kt_i=kt_i):
                    wge(scalar, MM)
                    wge(scalar, AC)
                    et = etmp[kt_i % 2]
                    AC.inc(scalar.activation(et[:], ps[:], AF.Exp, scale=p.c(2)))
                    wge(scalar, AC)
                    AC.inc(scalar.activation(fcol[:, kt_i * 512:(kt_i + 1) * 512], et[:], AF.Exp))
                block.scalar(s_e)

            def t_den(tensor):
                wge(tensor, AC)
                mm_group(tensor, psDen[:],
                         [(onesc[:], fcol[:, k * 512:(k + 1) * 512]) for k in range(64)], MM)
            block.tensor(t_den)

            def v_den(vector):
                wge(vector, MM)
                EV.inc(vector.reciprocal(rden[:], psDen[:]))
            block.vector(v_den)

            def t_bc(tensor):
                wge(tensor, EV)
                mm_group(tensor, psBCa[:], [(onesf[:], rden[:])], MM)
            block.tensor(t_bc)

            def s_bc(scalar):
                wge(scalar, MM)
                AC.inc(scalar.copy(rb[:], psBCa[:]))
            block.scalar(s_bc)

            def gp_v0(g):
                wge(g, MM)
                load_vcol(g, 0)
            block.gpsimd(gp_v0)
            for dt in range(8):
                ps = psO[dt % 2]
                def t_o(tensor, dt=dt, ps=ps):
                    wge(tensor, LD3)
                    wge(tensor, EV, lag=0)
                    mm_group(tensor, ps[:],
                             [(vcol[:, k * 128:(k + 1) * 128], fcol[:, k * 512:(k + 1) * 512])
                              for k in range(64)], MM)
                block.tensor(t_o)
                if dt + 1 < 8:
                    def gpv(g, dt=dt):
                        wge(g, MM)
                        load_vcol(g, dt + 1)
                    block.gpsimd(gpv)

                def v_o(vector, dt=dt, ps=ps):
                    wge(vector, MM)
                    wge(vector, AC)
                    EV.inc(vector.tensor_mul(attnT[:, dt * 512:(dt + 1) * 512], ps[:], rb[:]))
                block.vector(v_o)

            if "attn_dbg" in d:
                def sy_dbg(sync, c=c):
                    wge(sync, EV)
                    ST.inc(sync.dma_start(out=drview(d["attn_dbg"], c * 512, (c + 1) * 512),
                                          in_=rr8(attnT, 512)), 16)
                block.sync(sy_dbg)

            for ot in range(8):
                ps = psD[ot % 2]
                def t_w(tensor, ot=ot, ps=ps):
                    wge(tensor, EV)
                    mm_group(tensor, ps[:],
                             [(wo_sb[:, k * 1024 + ot * 128: k * 1024 + ot * 128 + 128],
                               attnT[:, k * 512:(k + 1) * 512]) for k in range(NKT)], MM)
                block.tensor(t_w)

                def s_w(scalar, ot=ot, ps=ps):
                    wge(scalar, MM)
                    wge(scalar, AC)
                    AC.inc(scalar.activation(x1f[:, ot * 512:(ot + 1) * 512], ps[:],
                                             AF.Identity, bias=bo_sb[:, ot:ot + 1]))
                block.scalar(s_w)

                def v_w(vector, ot=ot):
                    wge(vector, AC)
                    wge(vector, LD2)
                    EV.inc(vector.tensor_add(x_sb[:, ot * 512:(ot + 1) * 512],
                                             x1f[:, ot * 512:(ot + 1) * 512],
                                             x_sb[:, ot * 512:(ot + 1) * 512]))
                block.vector(v_w)

            emit_ln(p, d, ln_bufs, 512, make_bf=True)

            def sy_out(sync, c=c):
                wge(sync, EV)
                wge(sync, AC)
                ST.inc(sync.dma_start(out=drview(d["x1f"], c * 512, (c + 1) * 512),
                                      in_=rr8(x1f, 512)), 16)
                ST.inc(sync.dma_start(out=drview(d["x1b"], c * 512, (c + 1) * 512),
                                      in_=rr8(x1b, 512)), 16)
            block.sync(sy_out)

        def sy_fin(sync):
            wge(sync, ST)
        block.sync(sy_fin)


def emit_attn(p, d, li, xin_f, xin_b, xout_f, xout_b):
    nc, block = p.nc, p.block
    p.barrier()
    LD, LD2, MM, EV, AC, ST = (p.s(x) for x in ["LD", "LD2", "MM", "EV", "AC", "ST"])
    es = ExitStack()
    pre = f"a{li}_"
    with es:
        wqkv_sb = es.enter_context(nc.sbuf_tensor(pre + "wqkv", [128, NKT * 3072], BF16))
        woi_sb = es.enter_context(nc.sbuf_tensor(pre + "woi", [128, NKT * 1024], BF16))
        qs_sb = es.enter_context(nc.sbuf_tensor(pre + "qs", [128, DH], F32))
        ks_sb = es.enter_context(nc.sbuf_tensor(pre + "ks", [128, DH], F32))
        qsw_sb = es.enter_context(nc.sbuf_tensor(pre + "qsw", [128, DH], F32))
        ksw_sb = es.enter_context(nc.sbuf_tensor(pre + "ksw", [128, DH], F32))
        g_sb = es.enter_context(nc.sbuf_tensor(pre + "g", [128, 8], F32))
        lb_sb = es.enter_context(nc.sbuf_tensor(pre + "lb", [128, 8], F32))
        ident = es.enter_context(nc.sbuf_tensor(pre + "id", [128, 128], BF16))
        triA = es.enter_context(nc.sbuf_tensor(pre + "triA", [128, 128], BF16))
        triB = es.enter_context(nc.sbuf_tensor(pre + "triB", [128, 128], BF16))
        tab = es.enter_context(nc.sbuf_tensor(pre + "tab", [128, 4 * 2 * DH], F32))
        x_sb = es.enter_context(nc.sbuf_tensor(pre + "x", [128, NKT * 257], F32))
        xb_sb = es.enter_context(nc.sbuf_tensor(pre + "xb", [128, NKT * 257], BF16))
        q_tm = es.enter_context(nc.sbuf_tensor(pre + "qtm", [128, 1024], BF16))
        k_tm = es.enter_context(nc.sbuf_tensor(pre + "ktm", [128, 1024], BF16))
        qn_sb = es.enter_context(nc.sbuf_tensor(pre + "qn", [128, 1024], BF16))
        tmp1 = es.enter_context(nc.sbuf_tensor(pre + "tmp1", [128, 1024], BF16))
        tmp2 = es.enter_context(nc.sbuf_tensor(pre + "tmp2", [128, 1024], BF16))
        qrot = es.enter_context(nc.sbuf_tensor(pre + "qrot", [128, 1024], BF16))
        krot = es.enter_context(nc.sbuf_tensor(pre + "krot", [128, 1024], BF16))
        ssq = es.enter_context(nc.sbuf_tensor(pre + "ssq", [128, 32], F32))
        rr_ = es.enter_context(nc.sbuf_tensor(pre + "rr", [128, 32], F32))
        csq = es.enter_context(nc.sbuf_tensor(pre + "csq", [128, 4 * DH], F32))
        qT = es.enter_context(nc.sbuf_tensor(pre + "qT", [64, H * 256], BF16))
        kT = [es.enter_context(nc.sbuf_tensor(pre + f"kT{i}", [64, H * 256], BF16)) for i in range(2)]
        vA = [es.enter_context(nc.sbuf_tensor(pre + f"vA{i}", [128, 2 * H * 65], BF16)) for i in range(2)]
        emT = es.enter_context(nc.sbuf_tensor(pre + "emT", [128, 2 * 768], BF16))
        attnout = es.enter_context(nc.sbuf_tensor(pre + "ao", [128, 2 * 1024], BF16))
        f_sb = es.enter_context(nc.sbuf_tensor(pre + "f", [128, 16], F32))
        aT = es.enter_context(nc.sbuf_tensor(pre + "aT", [128, NKT * 256], BF16))
        xo = [es.enter_context(nc.sbuf_tensor(pre + f"xo{i}", [128, NKT * 258], F32)) for i in range(2)]
        tb_a = es.enter_context(nc.sbuf_tensor(pre + "tba", [128, NKT * 256], BF16))
        sqb_a = es.enter_context(nc.sbuf_tensor(pre + "sqba", [128, NKT * 256], BF16))
        xob = es.enter_context(nc.sbuf_tensor(pre + "xob", [128, NKT * 256], BF16))
        xn_sb = es.enter_context(nc.sbuf_tensor(pre + "xn", [128, NKT * 256], BF16))
        u_sb = es.enter_context(nc.sbuf_tensor(pre + "u", [128, 256], F32))
        rows = es.enter_context(nc.sbuf_tensor(pre + "rows", [1, 5 * 256], F32))
        onesc = es.enter_context(nc.sbuf_tensor(pre + "onesc", [128, 1], BF16))
        onesf = es.enter_context(nc.sbuf_tensor(pre + "onesf", [1, 128], F32))
        psA = [es.enter_context(nc.psum_tensor(pre + f"psA{i}", [128, 512], F32)) for i in range(2)]
        psT = es.enter_context(nc.psum_tensor(pre + "psT", [128, 1024], BF16))
        psS = [es.enter_context(nc.psum_tensor(pre + f"psS{i}", [128, 512], F32)) for i in range(2)]
        psG = [es.enter_context(nc.psum_tensor(pre + f"psG{i}", [128, 512], F32)) for i in range(2)]

        def gp0(gpsimd):
            LD.inc(gpsimd.dma_start(out=rr8(wqkv_sb, 3072),
                                    in_=d["wqkv"][li].rearrange("(k p) o -> p k o", p=128)), 16)
            LD.inc(gpsimd.dma_start(out=rr8(woi_sb, 1024),
                                    in_=d["wo_i"][li].rearrange("(k p) o -> p k o", p=128)), 16)
        block.gpsimd(gp0)

        def sy0(sync):
            LD.inc(sync.dma_start(out=qs_sb[:], in_=d["qs_bc"][li]), 16)
            LD.inc(sync.dma_start(out=ks_sb[:], in_=d["ks_bc"][li]), 16)
            LD.inc(sync.dma_start(out=qsw_sb[:], in_=d["qs_sw"][li]), 16)
            LD.inc(sync.dma_start(out=ksw_sb[:], in_=d["ks_sw"][li]), 16)
            LD.inc(sync.dma_start(out=g_sb[:], in_=d["g_pt"][li]), 16)
            LD.inc(sync.dma_start(out=lb_sb[:], in_=d["lb_pt"][li]), 16)
            LD.inc(sync.dma_start(out=ident[:], in_=d["ident"]), 16)
            LD.inc(sync.dma_start(out=triA[:], in_=d["triA"]), 16)
            LD.inc(sync.dma_start(out=triB[:], in_=d["triB"]), 16)
        block.sync(sy0)

        def v0(vector):
            wge(vector, LD)
            EV.inc(vector.memset(onesc[:], 1.0))
            EV.inc(vector.memset(onesf[:], 1.0))
            for i in range(2):
                EV.inc(vector.memset(
                    vA[i].rearrange("p (c h e) -> p c h e", c=2, h=H)[:, :, :, 64:65], 1.0))
            EV.inc(vector.memset(rr8(x_sb, 257)[:, :, 0:1], 0.0))
            EV.inc(vector.memset(rr8(xb_sb, 257)[:, :, 0:1], 0.0))
            EV.inc(vector.memset(rr8(xo[0], 258)[:, :, 0:1], 0.0))
        block.vector(v0)

        ln_bufs = dict(t_f=None, tb=tb_a, sqb=sqb_a, rows=rows, onesc=onesc, onesf=onesf,
                       ps_sum=psS[0][0:1, 0:256], ps_sumsq=psS[1][0:1, 0:256],
                       ps_bcmu=psG[0][:, 0:256], ps_bcr=psG[1][:, 0:256],
                       outf=None, outb=xn_sb)

        for w in range(NW):
            w0 = w * W
            cw = w % 2
            pw = 1 - cw

            def sy_x(sync, w=w, w0=w0):
                wge(sync, MM)
                wge(sync, EV)
                wge(sync, AC)
                if w == 0:
                    LD2.inc(sync.dma_start(out=rr8(x_sb, 257)[:, :, 1:257],
                                           in_=drview(xin_f, 0, 256)), 16)
                    LD2.inc(sync.dma_start(out=rr8(xb_sb, 257)[:, :, 1:257],
                                           in_=drview(xin_b, 0, 256)), 16)
                else:
                    LD2.inc(sync.dma_start(out=rr8(x_sb, 257),
                                           in_=drview(xin_f, w0 - 1, w0 + 256)), 16)
                    LD2.inc(sync.dma_start(out=rr8(xb_sb, 257),
                                           in_=drview(xin_b, w0 - 1, w0 + 256)), 16)
                for ti, nm in enumerate(["cs_q", "ss_q", "cs_k", "ss_k"]):
                    LD2.inc(sync.dma_start(
                        out=tab.rearrange("p (ti tt e) -> p ti tt e", ti=4, tt=2)[:, ti],
                        in_=d[nm].rearrange("(tt p) e -> p tt e", p=128)[:, 2 * w:2 * w + 2, :]), 16)
            block.sync(sy_x)

            for tt in range(2):
                def tview(kt):
                    off = (1 if kt < 4 else 0) + tt * 128
                    return xb_sb[:, kt * 257 + off: kt * 257 + off + 128]

                for ch in range(6):
                    ps = psA[ch % 2]
                    def t_qkv(tensor, ps=ps, ch=ch, tt=tt):
                        wge(tensor, LD)
                        wge(tensor, LD2)
                        wge(tensor, AC)
                        wge(tensor, EV)
                        mm_group(tensor, ps[:],
                                 [(tview(k), wqkv_sb[:, k * 3072 + ch * 512: k * 3072 + ch * 512 + 512])
                                  for k in range(NKT)], MM)
                    block.tensor(t_qkv)

                    if ch < 2:
                        def s_eq(scalar, ps=ps, ch=ch):
                            wge(scalar, MM)
                            AC.inc(scalar.copy(q_tm[:, ch * 512:(ch + 1) * 512], ps[:]))
                        block.scalar(s_eq)
                    elif ch < 4:
                        def s_ek(scalar, ps=ps, ch=ch):
                            wge(scalar, MM)
                            AC.inc(scalar.copy(k_tm[:, (ch - 2) * 512:(ch - 1) * 512], ps[:]))
                        block.scalar(s_ek)
                    else:
                        def v_evv(vector, ps=ps, ch=ch, tt=tt, cw=cw):
                            wge(vector, MM)
                            h0 = (ch - 4) * 8
                            EV.inc(vector.tensor_copy(
                                vA[cw].rearrange("p (c h e) -> p c h e", c=2, h=H)[:, tt, h0:h0 + 8, 0:64],
                                ps.rearrange("p (h e) -> p h e", e=64)))
                        block.vector(v_evv)

                def v_ss(vector, tt=tt):
                    wge(vector, AC)
                    EV.inc(vector.tensor_mul(tmp1[:], q_tm[:], q_tm[:]))
                    wge(vector, EV)
                    EV.inc(vector.tensor_reduce(ssq[:, 0:16],
                                                tmp1.rearrange("p (h e) -> p h e", e=64),
                                                axis=AX.X, op=ALU.add))
                    wge(vector, EV)
                    EV.inc(vector.tensor_mul(tmp1[:], k_tm[:], k_tm[:]))
                    wge(vector, EV)
                    EV.inc(vector.tensor_reduce(ssq[:, 16:32],
                                                tmp1.rearrange("p (h e) -> p h e", e=64),
                                                axis=AX.X, op=ALU.add))
                block.vector(v_ss)

                def s_rr(scalar):
                    wge(scalar, EV)
                    AC.inc(scalar.activation(rr_[:], ssq[:], AF.Ln))
                    wge(scalar, AC)
                    AC.inc(scalar.activation(rr_[:], rr_[:], AF.Exp, scale=p.c(1)))
                block.scalar(s_rr)

                def v_rrmin(vector):
                    wge(vector, AC)
                    EV.inc(vector.tensor_scalar_min(rr_[:], rr_[:], 1.0e12))
                block.vector(v_rrmin)

                def v_tab(vector, tt=tt):
                    wge(vector, LD2)
                    wge(vector, EV)
                    t3 = tab.rearrange("p (ti tt e) -> p ti tt e", ti=4, tt=2)
                    c4 = csq.rearrange("p (ti e) -> p ti e", ti=4)
                    EV.inc(vector.tensor_mul(c4[:, 0], t3[:, 0, tt], qs_sb[:]))
                    EV.inc(vector.tensor_mul(c4[:, 1], t3[:, 1, tt], qsw_sb[:]))
                    EV.inc(vector.tensor_mul(c4[:, 2], t3[:, 2, tt], ks_sb[:]))
                    EV.inc(vector.tensor_mul(c4[:, 3], t3[:, 3, tt], ksw_sb[:]))
                block.vector(v_tab)

                for which in range(2):
                    src = q_tm if which == 0 else k_tm
                    dst = qrot if which == 0 else krot
                    rr_sl = rr_[:, 0:16] if which == 0 else rr_[:, 16:32]
                    cs_i, ss_i = (0, 1) if which == 0 else (2, 3)

                    def v_rot(vector, src=src, dst=dst, rr_sl=rr_sl, cs_i=cs_i, ss_i=ss_i):
                        wge(vector, EV)
                        c4 = csq.rearrange("p (ti e) -> p ti e", ti=4)
                        h3 = lambda t: t.rearrange("p (h e) -> p h e", e=64)
                        h4 = lambda t: t.rearrange("p (h a c) -> p h a c", a=2, c=32)
                        EV.inc(vector.tensor_mul(
                            h3(qn_sb), h3(src),
                            rr_sl.rearrange("p (h e) -> p h e", e=1).broadcast_to((128, 16, 64))))
                        wge(vector, EV)
                        EV.inc(vector.tensor_mul(
                            h3(tmp1), h3(qn_sb),
                            c4[:, cs_i].rearrange("p (o e) -> p o e", o=1).broadcast_to((128, 16, 64))))
                        wge(vector, EV)
                        ss_t = c4[:, ss_i].rearrange("p (a c) -> p a c", a=2)
                        EV.inc(vector.tensor_mul(
                            h4(tmp2)[:, :, 0, :], h4(qn_sb)[:, :, 1, :],
                            ss_t[:, 0:1, :].broadcast_to((128, 16, 32))))
                        wge(vector, EV)
                        EV.inc(vector.tensor_mul(
                            h4(tmp2)[:, :, 1, :], h4(qn_sb)[:, :, 0, :],
                            ss_t[:, 1:2, :].broadcast_to((128, 16, 32))))
                        wge(vector, EV)
                        EV.inc(vector.tensor_add(dst[:], tmp1[:], tmp2[:]))
                    block.vector(v_rot)

                for which in range(2):
                    src = qrot if which == 0 else krot
                    dstT = qT if which == 0 else kT[cw]
                    for h in range(H):
                        slot = h % 8
                        def t_tr(tensor, src=src, h=h, slot=slot):
                            wge(tensor, EV)
                            wge(tensor, AC)
                            MM.inc(tensor.transpose(psT[0:64, slot * 128:(slot + 1) * 128],
                                                    src[:, h * 64:(h + 1) * 64], ident[:]))
                        block.tensor(t_tr)

                        def s_trev(scalar, dstT=dstT, h=h, slot=slot, tt=tt):
                            wge(scalar, MM)
                            AC.inc(scalar.copy(dstT[0:64, h * 256 + tt * 128: h * 256 + tt * 128 + 128],
                                               psT[0:64, slot * 128:(slot + 1) * 128]))
                        block.scalar(s_trev)

            for h in range(H):
                hs = h % 2
                sim_ps = psS[h % 2]
                def t_sim(tensor, h=h, sim_ps=sim_ps, w=w, cw=cw, pw=pw):
                    wge(tensor, AC)
                    wge(tensor, EV)
                    qh = qT[0:64, h * 256: h * 256 + 256]
                    if w > 0:
                        tensor.matmul(sim_ps[:, 0:128], kT[pw][0:64, h * 256: h * 256 + 128],
                                      qh[:, 0:128], start=True, stop=True)
                        tensor.matmul(sim_ps[:, 128:384], kT[pw][0:64, h * 256 + 128: h * 256 + 256],
                                      qh, start=True, stop=True)
                    tensor.matmul(psG[h % 2][:, 0:256], kT[cw][0:64, h * 256: h * 256 + 128],
                                  qh, start=True, stop=True)
                    MM.inc(tensor.matmul(psG[h % 2][:, 256:384], kT[cw][0:64, h * 256 + 128: h * 256 + 256],
                                         qh[:, 128:256], start=True, stop=True))
                block.tensor(t_sim)

                def s_exp(scalar, h=h, hs=hs, w=w):
                    wge(scalar, MM)
                    wge(scalar, EV)
                    base = hs * 768
                    if w > 0:
                        AC.inc(scalar.activation(emT[:, base: base + 384], psS[h % 2][:, 0:384], AF.Exp))
                        wge(scalar, AC)
                    AC.inc(scalar.activation(emT[:, base + 384: base + 768], psG[h % 2][:, 0:384], AF.Exp))
                block.scalar(s_exp)

                def v_mask(vector, h=h, hs=hs, w=w):
                    wge(vector, AC)
                    base = hs * 768
                    if w > 0:
                        EV.inc(vector.tensor_mul(
                            emT.rearrange("p (s c) -> p s c", c=128)[:, (base // 128):(base // 128) + 3:2, :],
                            emT.rearrange("p (s c) -> p s c", c=128)[:, (base // 128):(base // 128) + 3:2, :],
                            triA.rearrange("p (o c) -> p o c", o=1).broadcast_to((128, 2, 128))))
                        wge(vector, EV)
                    EV.inc(vector.tensor_mul(
                        emT.rearrange("p (s c) -> p s c", c=128)[:, (base // 128) + 3:(base // 128) + 6:2, :],
                        emT.rearrange("p (s c) -> p s c", c=128)[:, (base // 128) + 3:(base // 128) + 6:2, :],
                        triB.rearrange("p (o c) -> p o c", o=1).broadcast_to((128, 2, 128))))
                block.vector(v_mask)

                g4 = h % 4
                for qt in range(2):
                    ps_att = psA[qt]
                    def t_out(tensor, h=h, hs=hs, qt=qt, g4=g4, ps_att=ps_att, w=w, cw=cw, pw=pw):
                        wge(tensor, EV)
                        base = hs * 768
                        col = g4 * 65
                        pairs = []
                        va_c = vA[cw].rearrange("p (c h e) -> p c h e", c=2, h=H)
                        va_p = vA[pw].rearrange("p (c h e) -> p c h e", c=2, h=H)
                        if qt == 0:
                            if w > 0:
                                pairs.append((emT[:, base + 0: base + 128], va_p[:, 0, h, :]))
                                pairs.append((emT[:, base + 128: base + 256], va_p[:, 1, h, :]))
                            pairs.append((emT[:, base + 384: base + 512], va_c[:, 0, h, :]))
                        else:
                            if w > 0:
                                pairs.append((emT[:, base + 256: base + 384], va_p[:, 1, h, :]))
                            pairs.append((emT[:, base + 512: base + 640], va_c[:, 0, h, :]))
                            pairs.append((emT[:, base + 640: base + 768], va_c[:, 1, h, :]))
                        mm_group(tensor, ps_att[:, col: col + 65], pairs, MM)
                    block.tensor(t_out)

                if g4 == 3:
                    gidx = h // 4
                    for qt in range(2):
                        def v_nrm(vector, gidx=gidx, qt=qt):
                            wge(vector, MM)
                            pa = psA[qt][:, 0:260].rearrange("p (j e) -> p j e", e=65)
                            EV.inc(vector.reciprocal(f_sb[:, gidx * 4:(gidx + 1) * 4],
                                                     pa[:, 0:4, 64]))
                            wge(vector, EV)
                            ao = attnout.rearrange("p (q h e) -> p q h e", q=2, e=64)
                            EV.inc(vector.tensor_mul(
                                ao[:, qt, gidx * 4:(gidx + 1) * 4, :],
                                pa[:, 0:4, 0:64],
                                f_sb.rearrange("p (j e) -> p j e", e=1)[:, gidx * 4:(gidx + 1) * 4, :]
                                .broadcast_to((128, 4, 64))))
                        block.vector(v_nrm)

            for qt in range(2):
                for ct in range(8):
                    slot = ct % 8
                    def t_tr2(tensor, qt=qt, ct=ct, slot=slot):
                        wge(tensor, EV)
                        wge(tensor, AC)
                        MM.inc(tensor.transpose(psT[:, slot * 128:(slot + 1) * 128],
                                                attnout[:, qt * 1024 + ct * 128: qt * 1024 + ct * 128 + 128],
                                                ident[:]))
                    block.tensor(t_tr2)

                    def s_tr2(scalar, qt=qt, ct=ct, slot=slot):
                        wge(scalar, MM)
                        AC.inc(scalar.copy(aT[:, ct * 256 + qt * 128: ct * 256 + qt * 128 + 128],
                                           psT[:, slot * 128:(slot + 1) * 128]))
                    block.scalar(s_tr2)

            def v_pc(vector, w=w, cw=cw, pw=pw):
                wge(vector, ST)
                if w > 0:
                    wge(vector, EV)
                    EV.inc(vector.tensor_copy(rr8(xo[cw], 258)[:, :, 0:1],
                                              rr8(xo[pw], 258)[:, :, 256:257]))
            block.vector(v_pc)

            if "aT_dbg" in d:
                def sy_adbg(sync, w0=w0):
                    wge(sync, AC)
                    ST.inc(sync.dma_start(out=drview(d["aT_dbg"], w0, w0 + 256),
                                          in_=rr8(aT, 256)), 16)
                block.sync(sy_adbg)

            for ot in range(8):
                ps = psA[ot % 2]
                def t_wo(tensor, ot=ot, ps=ps):
                    wge(tensor, AC)
                    wge(tensor, EV)
                    mm_group(tensor, ps[:, 0:256],
                             [(woi_sb[:, k * 1024 + ot * 128: k * 1024 + ot * 128 + 128],
                               aT[:, k * 256:(k + 1) * 256]) for k in range(NKT)], MM)
                block.tensor(t_wo)

                def v_wo(vector, ot=ot, ps=ps, cw=cw):
                    wge(vector, MM)
                    wge(vector, LD2)
                    EV.inc(vector.tensor_add(xo[cw][:, ot * 258 + 1: ot * 258 + 257],
                                             ps[:, 0:256],
                                             x_sb[:, ot * 257 + 1: ot * 257 + 257]))
                block.vector(v_wo)

            def s_casts(scalar, cw=cw):
                wge(scalar, EV)
                AC.inc(scalar.copy(rr8(tb_a, 256)[:, 0:4, :],
                                   rr8(xo[cw], 258)[:, 0:4, 1:257]))
                wge(scalar, AC)
                AC.inc(scalar.copy(rr8(tb_a, 256)[:, 4:8, :],
                                   rr8(xo[cw], 258)[:, 4:8, 0:256]))
                wge(scalar, AC)
                AC.inc(scalar.copy(rr8(xob, 256), rr8(xo[cw], 258)[:, :, 1:257]))
            block.scalar(s_casts)

            def v_sqa(vector):
                wge(vector, AC)
                EV.inc(vector.tensor_mul(sqb_a[:], tb_a[:], tb_a[:]))
            block.vector(v_sqa)

            def t_lnrows(tensor):
                wge(tensor, EV)
                mm_group(tensor, psS[0][0:1, 0:256],
                         [(onesc[:], tb_a[:, k * 256:(k + 1) * 256]) for k in range(NKT)], MM)
                mm_group(tensor, psS[1][0:1, 0:256],
                         [(onesc[:], sqb_a[:, k * 256:(k + 1) * 256]) for k in range(NKT)], MM)
            block.tensor(t_lnrows)

            T = 256
            def s_lr1(scalar):
                wge(scalar, MM)
                AC.inc(scalar.activation(rows[:, 0:T], psS[0][0:1, 0:256], AF.Identity, scale=p.c1(4)))
                wge(scalar, AC)
                AC.inc(scalar.activation(rows[:, T:2 * T], psS[1][0:1, 0:256], AF.Identity, scale=p.c1(4)))
            block.scalar(s_lr1)

            def v_lr(vector):
                wge(vector, AC)
                EV.inc(vector.tensor_mul(rows[:, 2 * T:3 * T], rows[:, 0:T], rows[:, 0:T]))
                wge(vector, EV)
                EV.inc(vector.tensor_sub(rows[:, 2 * T:3 * T], rows[:, T:2 * T], rows[:, 2 * T:3 * T]))
            block.vector(v_lr)

            def s_lr2(scalar):
                wge(scalar, EV)
                AC.inc(scalar.activation(rows[:, 3 * T:4 * T], rows[:, 2 * T:3 * T], AF.Ln, bias=p.c1(0)))
                wge(scalar, AC)
                AC.inc(scalar.activation(rows[:, 3 * T:4 * T], rows[:, 3 * T:4 * T], AF.Exp, scale=p.c1(1)))
                wge(scalar, AC)
                AC.inc(scalar.activation(rows[:, 4 * T:5 * T], rows[:, 0:T], AF.Identity, scale=p.c1(2)))
            block.scalar(s_lr2)

            def t_lnbc(tensor):
                wge(tensor, AC)
                mm_group(tensor, psG[0][:, 0:256], [(onesf[:], rows[:, 4 * T:5 * T])], MM)
                mm_group(tensor, psG[1][:, 0:256], [(onesf[:], rows[:, 3 * T:4 * T])], MM)
            block.tensor(t_lnbc)

            for kk in range(NKT):
                def v_lap(vector, kk=kk, cw=cw):
                    wge(vector, MM)
                    off = (1 if kk < 4 else 0)
                    src = xo[cw][:, kk * 258 + off: kk * 258 + off + 256]
                    EV.inc(vector.tensor_add(u_sb[:], src, psG[0][:, 0:256]))
                    wge(vector, EV)
                    EV.inc(vector.tensor_mul(u_sb[:], u_sb[:], psG[1][:, 0:256]))
                    wge(vector, EV)
                    EV.inc(vector.tensor_scalar(xn_sb[:, kk * 256:(kk + 1) * 256], u_sb[:],
                                                g_sb[:, kk:kk + 1], lb_sb[:, kk:kk + 1],
                                                op0=ALU.mult, op1=ALU.add))
                block.vector(v_lap)

            def sy_st(sync, w=w, w0=w0, cw=cw):
                wge(sync, EV)
                wge(sync, AC)
                ST.inc(sync.dma_start(out=drview(xout_f, w0, w0 + 256),
                                      in_=rr8(xo[cw], 258)[:, :, 1:257]), 16)
                ST.inc(sync.dma_start(out=drview(xout_b, w0, w0 + 256), in_=rr8(xob, 256)), 16)
                ST.inc(sync.dma_start(out=drview(d["xnb"], w0, w0 + 256), in_=rr8(xn_sb, 256)), 16)
            block.sync(sy_st)

        def sy_fin(sync):
            wge(sync, ST)
        block.sync(sy_fin)


def emit_f1(p, d, li):
    nc, block = p.nc, p.block
    p.barrier()
    LD, LD2, MM, EV, AC, ST = (p.s(x) for x in ["LD", "LD2", "MM", "EV", "AC", "ST"])
    es = ExitStack()
    pre = f"f1_{li}_"
    with es:
        w1_sb = es.enter_context(nc.sbuf_tensor(pre + "w1", [128, NKT * DI2], BF16))
        b1a_sb = es.enter_context(nc.sbuf_tensor(pre + "b1a", [128, NAT], F32))
        b1g_sb = es.enter_context(nc.sbuf_tensor(pre + "b1g", [128, NAT], F32))
        xn = [es.enter_context(nc.sbuf_tensor(pre + f"xn{i}", [128, NKT * 512], BF16)) for i in range(2)]
        gg = es.enter_context(nc.sbuf_tensor(pre + "gg", [128, 512], F32))
        h_sb = [es.enter_context(nc.sbuf_tensor(pre + f"h{i}", [128, 512], BF16)) for i in range(2)]
        psa = [es.enter_context(nc.psum_tensor(pre + f"psa{i}", [128, 512], F32)) for i in range(2)]
        psg = [es.enter_context(nc.psum_tensor(pre + f"psg{i}", [128, 512], F32)) for i in range(2)]

        def gp0(gpsimd):
            LD.inc(gpsimd.dma_start(out=rr8(w1_sb, DI2),
                                    in_=d["w1"][li].rearrange("(k p) o -> p k o", p=128)), 16)
        block.gpsimd(gp0)

        def sy0(sync):
            LD.inc(sync.dma_start(out=b1a_sb[:], in_=d["b1a_pt"][li]), 16)
            LD.inc(sync.dma_start(out=b1g_sb[:], in_=d["b1g_pt"][li]), 16)
        block.sync(sy0)

        def load_blk(sync, blk):
            LD2.inc(sync.dma_start(out=rr8(xn[blk % 2], 512),
                                   in_=drview(d["xnb"], blk * 512, (blk + 1) * 512)), 16)

        block.sync(lambda s: load_blk(s, 0))

        for blk in range(16):
            xs = xn[blk % 2]
            if blk + 1 < 16:
                def syn(s, blk=blk):
                    wge(s, MM)
                    load_blk(s, blk + 1)
                block.sync(syn)

            for at in range(NAT):
                aoff, asz = ATILES[at]
                pa = psa[at % 2]
                pg = psg[at % 2]
                def t_a(tensor, aoff=aoff, asz=asz, pa=pa, pg=pg, xs=xs):
                    wge(tensor, LD)
                    wge(tensor, LD2)
                    wge(tensor, EV, lag=1)
                    wge(tensor, AC, lag=1)
                    mm_group(tensor, pa[0:asz, :],
                             [(w1_sb[:, k * DI2 + aoff: k * DI2 + aoff + asz],
                               xs[:, k * 512:(k + 1) * 512]) for k in range(NKT)], MM)
                    mm_group(tensor, pg[0:asz, :],
                             [(w1_sb[:, k * DI2 + DI + aoff: k * DI2 + DI + aoff + asz],
                               xs[:, k * 512:(k + 1) * 512]) for k in range(NKT)], MM)
                block.tensor(t_a)

                def s_g(scalar, at=at, asz=asz, pg=pg):
                    wge(scalar, MM)
                    wge(scalar, EV)
                    AC.inc(scalar.activation(gg[0:asz, :], pg[0:asz, :], AF.Gelu,
                                             bias=b1g_sb[0:asz, at:at + 1]))
                block.scalar(s_g)

                hb_ = h_sb[at % 2]
                def v_h(vector, at=at, asz=asz, pa=pa, hb_=hb_):
                    wge(vector, AC)
                    wge(vector, ST)
                    EV.inc(vector.scalar_tensor_tensor(hb_[0:asz, :], pa[0:asz, :],
                                                       b1a_sb[0:asz, at:at + 1], gg[0:asz, :],
                                                       op0=ALU.add, op1=ALU.mult))
                block.vector(v_h)

                def sy_h(sync, at=at, asz=asz, blk=blk, hb_=hb_):
                    wge(sync, EV)
                    ST.inc(sync.dma_start(
                        out=d["hb"][at * 128: at * 128 + asz, blk * 512:(blk + 1) * 512],
                        in_=hb_[0:asz, :]), 16)
                block.sync(sy_h)

        def sy_fin(sync):
            wge(sync, ST)
        block.sync(sy_fin)


def emit_f2(p, d, li, xres_f, xout_f, xout_b):
    nc, block = p.nc, p.block
    p.barrier()
    LD, LD2, MM, EV, AC, ST = (p.s(x) for x in ["LD", "LD2", "MM", "EV", "AC", "ST"])
    es = ExitStack()
    pre = f"f2_{li}_"
    TB = 256
    with es:
        w2_sb = es.enter_context(nc.sbuf_tensor(pre + "w2", [128, NAT * 1024], BF16))
        b2_sb = es.enter_context(nc.sbuf_tensor(pre + "b2", [128, 8], F32))
        hs = [es.enter_context(nc.sbuf_tensor(pre + f"hs{i}", [128, NAT * TB], BF16)) for i in range(2)]
        xr = [es.enter_context(nc.sbuf_tensor(pre + f"xr{i}", [128, NKT * TB], F32)) for i in range(2)]
        xof = es.enter_context(nc.sbuf_tensor(pre + "xof", [128, NKT * TB], F32))
        xob = es.enter_context(nc.sbuf_tensor(pre + "xob", [128, NKT * TB], BF16))
        ps2 = [es.enter_context(nc.psum_tensor(pre + f"ps{i}", [128, TB], F32)) for i in range(2)]

        def gp0(gpsimd):
            LD.inc(gpsimd.dma_start(
                out=w2_sb.rearrange("p (k o) -> p k o", k=NAT)[:, 0:21, :],
                in_=d["w2"][li][0:2688].rearrange("(k p) o -> p k o", p=128)), 16)
            LD.inc(gpsimd.dma_start(
                out=w2_sb[0:42, 21 * 1024: 22 * 1024],
                in_=d["w2"][li][2688:2730, :]), 16)
        block.gpsimd(gp0)

        def sy0(sync):
            LD.inc(sync.dma_start(out=b2_sb[:], in_=d["b2_pt"][li]), 16)
        block.sync(sy0)

        def load_blk(sync, gpsimd_unused, blk):
            i = blk % 2
            LD2.inc(sync.dma_start(
                out=hs[i].rearrange("p (k t) -> p k t", k=NAT),
                in_=d["hb"].rearrange("(k p) t -> p k t", p=128)[:, :, blk * TB:(blk + 1) * TB]), 16)
            LD2.inc(sync.dma_start(out=rr8(xr[i], TB),
                                   in_=drview(xres_f, blk * TB, (blk + 1) * TB)), 16)

        block.sync(lambda s: load_blk(s, None, 0))

        for blk in range(32):
            i = blk % 2
            if blk + 1 < 32:
                def syn(s, blk=blk):
                    wge(s, MM)
                    wge(s, EV)
                    load_blk(s, None, blk + 1)
                block.sync(syn)

            for ot in range(8):
                ps = ps2[ot % 2]
                def t_m(tensor, ot=ot, ps=ps, i=i):
                    wge(tensor, LD)
                    wge(tensor, LD2)
                    wge(tensor, EV, lag=1)
                    pairs = []
                    for at in range(NAT):
                        asz = ATILES[at][1]
                        pairs.append((w2_sb[0:asz, at * 1024 + ot * 128: at * 1024 + ot * 128 + 128],
                                      hs[i][0:asz, at * TB:(at + 1) * TB]))
                    mm_group(tensor, ps[:], pairs, MM)
                block.tensor(t_m)

                def v_o(vector, ot=ot, ps=ps, i=i):
                    wge(vector, MM)
                    wge(vector, ST)
                    EV.inc(vector.scalar_tensor_tensor(xof[:, ot * TB:(ot + 1) * TB], ps[:],
                                                       b2_sb[:, ot:ot + 1],
                                                       xr[i][:, ot * TB:(ot + 1) * TB],
                                                       op0=ALU.add, op1=ALU.add))
                block.vector(v_o)

                def s_b(scalar, ot=ot):
                    wge(scalar, EV)
                    AC.inc(scalar.copy(xob[:, ot * TB:(ot + 1) * TB], xof[:, ot * TB:(ot + 1) * TB]))
                block.scalar(s_b)

            def sy_st(sync, blk=blk):
                wge(sync, EV)
                wge(sync, AC)
                ST.inc(sync.dma_start(out=drview(xout_f, blk * TB, (blk + 1) * TB),
                                      in_=rr8(xof, TB)), 16)
                ST.inc(sync.dma_start(out=drview(xout_b, blk * TB, (blk + 1) * TB),
                                      in_=rr8(xob, TB)), 16)
            block.sync(sy_st)

        def sy_fin(sync):
            wge(sync, ST)
        block.sync(sy_fin)


def emit_fin(p, d, xlast_f):
    nc, block = p.nc, p.block
    p.barrier()
    LD, MM, EV, AC, ST = (p.s(x) for x in ["LD", "MM", "EV", "AC", "ST"])
    es = ExitStack()
    with es:
        onesc = es.enter_context(nc.sbuf_tensor("fin_ones", [128, 1], BF16))
        onesf = es.enter_context(nc.sbuf_tensor("fin_onesf", [1, 128], F32))
        a_sb = es.enter_context(nc.sbuf_tensor("fin_a", [128, NKT * 512], F32))
        t_f = es.enter_context(nc.sbuf_tensor("fin_t", [128, NKT * 512], F32))
        tb = es.enter_context(nc.sbuf_tensor("fin_tb", [128, NKT * 512], BF16))
        sqb = es.enter_context(nc.sbuf_tensor("fin_sqb", [128, NKT * 512], BF16))
        rows = es.enter_context(nc.sbuf_tensor("fin_rows", [1, 5 * 512], F32))
        outf = es.enter_context(nc.sbuf_tensor("fin_outf", [128, NKT * 512], F32))
        psR1 = es.enter_context(nc.psum_tensor("fin_psR1", [1, 512], F32))
        psBCa = es.enter_context(nc.psum_tensor("fin_psBCa", [128, 512], F32))
        psBCb = es.enter_context(nc.psum_tensor("fin_psBCb", [128, 512], F32))

        def v0(vector):
            EV.inc(vector.memset(onesc[:], 1.0))
            EV.inc(vector.memset(onesf[:], 1.0))
        block.vector(v0)

        bufs = dict(t_f=t_f, tb=tb, sqb=sqb, rows=rows, onesc=onesc, onesf=onesf,
                    ps_sum=psR1[:], ps_sumsq=psBCa[0:1, :], ps_bcmu=psBCa[:],
                    ps_bcr=psBCb[:], outf=outf, outb=tb)

        for c in range(16):
            def sy_l(sync, c=c):
                wge(sync, EV)
                wge(sync, AC)
                wge(sync, ST)
                LD.inc(sync.dma_start(out=rr8(t_f, 512),
                                      in_=drview(d["x1f"], c * 512, (c + 1) * 512)), 16)
                LD.inc(sync.dma_start(out=rr8(a_sb, 512),
                                      in_=drview(xlast_f, c * 512, (c + 1) * 512)), 16)
            block.sync(sy_l)

            def v_add(vector):
                wge(vector, LD)
                EV.inc(vector.tensor_add(t_f[:], t_f[:], a_sb[:]))
            block.vector(v_add)

            emit_ln(p, d, bufs, 512, make_bf=False)

            def sy_st(sync, c=c):
                wge(sync, EV)
                ST.inc(sync.dma_start(out=drview(d["outT"], c * 512, (c + 1) * 512),
                                      in_=rr8(outf, 512)), 16)
            block.sync(sy_st)

        def sy_fin(sync):
            wge(sync, ST)
        block.sync(sy_fin)


def build_program(n_layers=NL, outer_only=False, dbg=(), attn_only=False):
    nc = bass.Bass("TRN2", target_bir_lowering=False)
    d = {}
    def scratch(name, shape, dtype):
        kind = "ExternalOutput" if name in dbg else "Internal"
        return nc.dram_tensor(name, shape, dtype, kind=kind).ap()
    d["xT"] = nc.dram_tensor("xT", [D, S], F32, kind="ExternalInput").ap()
    for nm in ["wq", "wk", "wv", "wo"]:
        d[nm] = nc.dram_tensor(nm, [D, D], F32, kind="ExternalInput").ap()
    for nm in ["bq_pt", "bk_pt", "bo_pt"]:
        d[nm] = nc.dram_tensor(nm, [128, 8], F32, kind="ExternalInput").ap()
    d["bv_bc"] = nc.dram_tensor("bv_bc", [128, D], F32, kind="ExternalInput").ap()
    d["wqkv"] = nc.dram_tensor("wqkv", [NL, D, 3072], F32, kind="ExternalInput").ap()
    d["wo_i"] = nc.dram_tensor("wo_i", [NL, D, D], F32, kind="ExternalInput").ap()
    d["w1"] = nc.dram_tensor("w1", [NL, D, DI2], F32, kind="ExternalInput").ap()
    d["w2"] = nc.dram_tensor("w2", [NL, DI, D], F32, kind="ExternalInput").ap()
    d["qs_bc"] = nc.dram_tensor("qs_bc", [NL, 128, DH], F32, kind="ExternalInput").ap()
    d["ks_bc"] = nc.dram_tensor("ks_bc", [NL, 128, DH], F32, kind="ExternalInput").ap()
    d["qs_sw"] = nc.dram_tensor("qs_sw", [NL, 128, DH], F32, kind="ExternalInput").ap()
    d["ks_sw"] = nc.dram_tensor("ks_sw", [NL, 128, DH], F32, kind="ExternalInput").ap()
    d["g_pt"] = nc.dram_tensor("g_pt", [NL, 128, 8], F32, kind="ExternalInput").ap()
    d["lb_pt"] = nc.dram_tensor("lb_pt", [NL, 128, 8], F32, kind="ExternalInput").ap()
    d["b1a_pt"] = nc.dram_tensor("b1a_pt", [NL, 128, NAT], F32, kind="ExternalInput").ap()
    d["b1g_pt"] = nc.dram_tensor("b1g_pt", [NL, 128, NAT], F32, kind="ExternalInput").ap()
    d["b2_pt"] = nc.dram_tensor("b2_pt", [NL, 128, 8], F32, kind="ExternalInput").ap()
    for nm in ["cs_q", "ss_q", "cs_k", "ss_k"]:
        d[nm] = nc.dram_tensor(nm, [S, DH], F32, kind="ExternalInput").ap()
    for nm in ["triA", "triB", "ident"]:
        d[nm] = nc.dram_tensor(nm, [128, 128], BF16, kind="ExternalInput").ap()
    d["qt8"] = scratch("qt8", [D, S], BF16)
    d["kts8"] = scratch("kts8", [D, S], BF16)
    d["qaug"] = scratch("qaug", [2, S], BF16)
    d["kaug"] = scratch("kaug", [2, S], BF16)
    d["vb"] = scratch("vb", [S, D], BF16)
    d["x1f"] = scratch("x1f", [D, S], F32)
    d["x1b"] = scratch("x1b", [D, S], BF16)
    d["xf"] = [scratch(f"xf{i}", [D, S], F32) for i in range(2)]
    d["xb"] = [scratch(f"xb{i}", [D, S], BF16) for i in range(2)]
    d["xnb"] = scratch("xnb", [D, S], BF16)
    d["hb"] = scratch("hb", [NAT * 128, S], BF16)
    d["outT"] = nc.dram_tensor("outT", [D, S], F32, kind="ExternalOutput").ap()

    if "attn_dbg" in dbg:
        d["attn_dbg"] = nc.dram_tensor("attn_dbg", [D, S], BF16, kind="ExternalOutput").ap()
    if "aT_dbg" in dbg:
        d["aT_dbg"] = nc.dram_tensor("aT_dbg", [D, S], BF16, kind="ExternalOutput").ap()
    d["cvals"] = nc.dram_tensor("cvals", [128, 8], F32, kind="ExternalInput").ap()
    es = ExitStack()
    with es:
        block = es.enter_context(nc.Block())
        p = Prog(nc, es, block)
        p.cv = es.enter_context(nc.sbuf_tensor("cvals_sb", [128, 8], F32))

        def _cv_load(sync):
            p.s("LD").inc(sync.dma_start(out=p.cv[:], in_=d["cvals"]), 16)
        block.sync(_cv_load)
        emit_s0(p, d)
        emit_s1(p, d)
        if outer_only:
            emit_fin(p, d, d["x1f"])
            return nc
        cur_f, cur_b = d["x1f"], d["x1b"]
        for li in range(n_layers):
            nxt_f, nxt_b = d["xf"][li % 2], d["xb"][li % 2]
            emit_attn(p, d, li, cur_f, cur_b, nxt_f, nxt_b)
            if attn_only:
                return nc
            emit_f1(p, d, li)
            emit_f2(p, d, li, nxt_f, nxt_f, nxt_b)
            cur_f, cur_b = nxt_f, nxt_b
        emit_fin(p, d, cur_f)
    return nc


_CONSTS = None


def _inputs_for_core(b, args, consts):
    (x, wq, bq, wk, bk, wv, bv, wo, bo, wqkv, qs, ks, wo_i, g, lb, w1, b1, w2, b2) = args
    m = {
        "xT": np.ascontiguousarray(x[b].T),
        "wq": wq, "wk": wk, "wv": wv, "wo": wo,
        "bq_pt": _col(bq), "bk_pt": _col(bk), "bo_pt": _col(bo),
        "bv_bc": np.ascontiguousarray(np.broadcast_to(bv, (128, D))),
        "wqkv": wqkv, "wo_i": wo_i, "w1": w1, "w2": w2,
        "qs_bc": np.ascontiguousarray(np.broadcast_to(qs[:, None, :], (NL, 128, DH))),
        "ks_bc": np.ascontiguousarray(np.broadcast_to(ks[:, None, :], (NL, 128, DH))),
        "qs_sw": np.ascontiguousarray(np.broadcast_to(np.roll(qs, -32, -1)[:, None, :], (NL, 128, DH))),
        "ks_sw": np.ascontiguousarray(np.broadcast_to(np.roll(ks, -32, -1)[:, None, :], (NL, 128, DH))),
        "g_pt": np.stack([_col(g[i]) for i in range(NL)]),
        "lb_pt": np.stack([_col(lb[i]) for i in range(NL)]),
        "b1a_pt": np.stack([_colpad(b1[i, :DI], NAT) for i in range(NL)]),
        "b1g_pt": np.stack([_colpad(b1[i, DI:], NAT) for i in range(NL)]),
        "b2_pt": np.stack([_col(b2[i]) for i in range(NL)]),
    }
    m.update(consts)
    return m


def kernel(L0_attn_x, L0_attn_Wq, L0_attn_bq, L0_attn_Wk, L0_attn_bk, L0_attn_Wv,
           L0_attn_bv, L0_attn_Wo, L0_attn_bo, L0_ffn_Wqkv, L0_ffn_q_scale,
           L0_ffn_k_scale, L0_ffn_Wo, L0_ffn_ln_g, L0_ffn_ln_b, L0_ffn_W1,
           L0_ffn_b1, L0_ffn_W2, L0_ffn_b2):
    global _CONSTS
    if _CONSTS is None:
        _CONSTS = _build_consts()
    args = tuple(np.ascontiguousarray(np.asarray(a, dtype=np.float32)) for a in (
        L0_attn_x, L0_attn_Wq, L0_attn_bq, L0_attn_Wk, L0_attn_bk, L0_attn_Wv,
        L0_attn_bv, L0_attn_Wo, L0_attn_bo, L0_ffn_Wqkv, L0_ffn_q_scale,
        L0_ffn_k_scale, L0_ffn_Wo, L0_ffn_ln_g, L0_ffn_ln_b, L0_ffn_W1,
        L0_ffn_b1, L0_ffn_W2, L0_ffn_b2))
    nc = build_program()
    in_maps = [_inputs_for_core(b, args, _CONSTS) for b in range(8)]
    res = run_bass_kernel_spmd(nc, in_maps, core_ids=list(range(8)))
    out = np.stack([np.ascontiguousarray(np.asarray(r["outT"]).T) for r in res.results])
    return out.astype(np.float32)


# version 03
# speedup vs baseline: 1.5692x; 1.5692x over previous
import sys
sys.path.insert(0, '/opt/trn_rl_repo')
import numpy as np
import ml_dtypes
from contextlib import ExitStack

import concourse.bass as bass
import concourse.mybir as mybir
from concourse.bass_utils import run_bass_kernel_spmd

F32 = mybir.dt.float32
BF16 = mybir.dt.bfloat16
AF = mybir.ActivationFunctionType
ALU = mybir.AluOpType
AX = mybir.AxisListType

S = 8192
D = 1024
NKT = 8
H = 16
DH = 64
W = 256
NW = S // W
DI2 = 5460
DI = 2730
NL = 6
ATILES = [(i * 128, 128) for i in range(21)] + [(2688, 42)]
NAT = len(ATILES)

bfnp = ml_dtypes.bfloat16


def _build_consts():
    t = np.arange(S, dtype=np.float64)
    half = np.arange(0, DH, 2, dtype=np.float64)
    inv_freq = 1.0 / 10000.0 ** (half / DH)
    freqs = t[:, None] * inv_freq[None, :]
    freqs = np.concatenate([freqs, freqs], -1)
    base_scale = (half + 0.4 * DH) / (1.4 * DH)
    power = (t - S // 2) / (W // 2)
    scale = base_scale[None, :] ** power[:, None]
    scale = np.concatenate([scale, scale], -1)
    cos = np.cos(freqs); sin = np.sin(freqs)
    sgn = np.concatenate([-np.ones(32), np.ones(32)])[None, :]
    consts = {
        "cs_q": (8.0 * cos * scale).astype(np.float32),
        "ss_q": (8.0 * sin * scale * sgn).astype(np.float32),
        "cs_k": (cos / scale).astype(np.float32),
        "ss_k": (sin / scale * sgn).astype(np.float32),
    }
    kl = np.arange(128)[:, None]; ql = np.arange(128)[None, :]
    consts["triA"] = (ql <= kl).astype(bfnp)
    consts["triB"] = (ql >= kl).astype(bfnp)
    consts["ident"] = np.eye(128, dtype=bfnp)
    cv = np.zeros((128, 8), np.float32)
    cv[:, 0] = 1e-5; cv[:, 1] = -0.5; cv[:, 2] = -1.0; cv[:, 3] = -2.0
    cv[:, 4] = 1.0 / D; cv[:, 5] = 0.25; cv[:, 6] = 1.0
    consts["cvals"] = cv
    return consts


def _col(v):
    return np.ascontiguousarray(v.reshape(-1, 128).T)


def _colpad(v, n):
    out = np.zeros(n * 128, v.dtype)
    out[: v.shape[0]] = v
    return _col(out)


class Sem:
    def __init__(self, es, nc, name):
        self.h = es.enter_context(nc.semaphore(name))
        self.n = 0

    def inc(self, inst, k=1):
        inst.then_inc(self.h, k)
        self.n += k
        return inst


def wge(eng, sem, lag=0):
    v = sem.n - lag
    if v > 0:
        eng.wait_ge(sem.h, v)


class Prog:
    def __init__(self, nc, es, block):
        self.nc = nc
        self.es = es
        self.block = block
        self.sem = {}
        for name in ["LD", "LD2", "LD3", "MM", "EV", "AC", "ST", "GLD"]:
            self.sem[name] = Sem(es, nc, name)
        self.cv = None

    def s(self, name):
        if name not in self.sem:
            self.sem[name] = Sem(self.es, self.nc, name)
        return self.sem[name]

    def c(self, i):
        return self.cv[:, i:i + 1]

    def c1(self, i):
        return self.cv[0:1, i:i + 1]

    def barrier(self):
        for eng_open in [self.block.sync, self.block.tensor, self.block.vector,
                         self.block.scalar, self.block.gpsimd]:
            def _b(eng):
                for s in self.sem.values():
                    wge(eng, s)
            eng_open(_b)


def mm_group(tensor, out_ap, pairs, sem=None, k=1):
    inst = None
    n = len(pairs)
    for i, (l, r) in enumerate(pairs):
        inst = tensor.matmul(out_ap, l, r, start=(i == 0), stop=(i == n - 1))
    if sem is not None:
        sem.inc(inst, k)
    return inst


def rr8(ap, inner):
    return ap.rearrange("p (k t) -> p k t", t=inner)


def drview(dram, T0, T1):
    return dram.rearrange("(k p) t -> p k t", p=128)[:, :, T0:T1]


def emit_ln(p, d, bufs, T, g_ap=None, b_ap=None, make_f32=True, make_bf=False):
    block = p.block
    MM, EV, AC = p.s("MM"), p.s("EV"), p.s("AC")
    t_f, tb, sqb, rows = bufs["t_f"], bufs["tb"], bufs["sqb"], bufs["rows"]
    onesc, onesf = bufs["onesc"], bufs["onesf"]

    def s_cast(scalar):
        wge(scalar, EV)
        AC.inc(scalar.copy(tb[:, 0:NKT * T], t_f[:, 0:NKT * T]))
    block.scalar(s_cast)

    def v_sq(vector):
        wge(vector, AC)
        EV.inc(vector.tensor_mul(sqb[:, 0:NKT * T], tb[:, 0:NKT * T], tb[:, 0:NKT * T]))
    block.vector(v_sq)

    def t_rows(tensor):
        wge(tensor, EV)
        mm_group(tensor, bufs["ps_sum"],
                 [(onesc[:], tb[:, k * T:(k + 1) * T]) for k in range(NKT)], MM)
        mm_group(tensor, bufs["ps_sumsq"],
                 [(onesc[:], sqb[:, k * T:(k + 1) * T]) for k in range(NKT)], MM)
    block.tensor(t_rows)

    def s_r1(scalar):
        wge(scalar, MM)
        AC.inc(scalar.activation(rows[:, 0:T], bufs["ps_sum"], AF.Identity, scale=p.c1(4)))
        wge(scalar, AC)
        AC.inc(scalar.activation(rows[:, T:2 * T], bufs["ps_sumsq"], AF.Identity, scale=p.c1(4)))
    block.scalar(s_r1)

    def v_r(vector):
        wge(vector, AC)
        EV.inc(vector.tensor_mul(rows[:, 2 * T:3 * T], rows[:, 0:T], rows[:, 0:T]))
        wge(vector, EV)
        EV.inc(vector.tensor_sub(rows[:, 2 * T:3 * T], rows[:, T:2 * T], rows[:, 2 * T:3 * T]))
    block.vector(v_r)

    def s_r2(scalar):
        wge(scalar, EV)
        AC.inc(scalar.activation(rows[:, 3 * T:4 * T], rows[:, 2 * T:3 * T], AF.Ln, bias=p.c1(0)))
        wge(scalar, AC)
        AC.inc(scalar.activation(rows[:, 3 * T:4 * T], rows[:, 3 * T:4 * T], AF.Exp, scale=p.c1(1)))
        wge(scalar, AC)
        AC.inc(scalar.activation(rows[:, 4 * T:5 * T], rows[:, 0:T], AF.Identity, scale=p.c1(2)))
    block.scalar(s_r2)

    def t_bc(tensor):
        wge(tensor, AC)
        mm_group(tensor, bufs["ps_bcmu"], [(onesf[:], rows[:, 4 * T:5 * T])], MM)
        mm_group(tensor, bufs["ps_bcr"], [(onesf[:], rows[:, 3 * T:4 * T])], MM)
    block.tensor(t_bc)

    for k in range(NKT):
        def v_ap(vector, k=k):
            wge(vector, MM)
            EV.inc(vector.tensor_add(t_f[:, k * T:(k + 1) * T], t_f[:, k * T:(k + 1) * T],
                                     bufs["ps_bcmu"]))
            wge(vector, EV)
            if g_ap is None:
                EV.inc(vector.tensor_mul(bufs["outf"][:, k * T:(k + 1) * T],
                                         t_f[:, k * T:(k + 1) * T], bufs["ps_bcr"]))
            else:
                EV.inc(vector.tensor_mul(t_f[:, k * T:(k + 1) * T],
                                         t_f[:, k * T:(k + 1) * T], bufs["ps_bcr"]))
                wge(vector, EV)
                EV.inc(vector.tensor_scalar(bufs["outb"][:, k * T:(k + 1) * T],
                                            t_f[:, k * T:(k + 1) * T],
                                            g_ap[:, k:k + 1], b_ap[:, k:k + 1],
                                            op0=ALU.mult, op1=ALU.add))
        block.vector(v_ap)

        if g_ap is None and make_bf:
            def s_cb(scalar, k=k):
                wge(scalar, EV)
                AC.inc(scalar.copy(bufs["outb"][:, k * T:(k + 1) * T],
                                   bufs["outf"][:, k * T:(k + 1) * T]))
            block.scalar(s_cb)


def emit_s0(p, d):
    nc, block = p.nc, p.block
    p.barrier()
    LD, MM, EV, AC, ST = (p.s(x) for x in ["LD", "MM", "EV", "AC", "ST"])
    es = ExitStack()
    with es:
        wq_sb = es.enter_context(nc.sbuf_tensor("s0_wq", [128, NKT * 1024], BF16))
        wk_sb = es.enter_context(nc.sbuf_tensor("s0_wk", [128, NKT * 1024], BF16))
        wv_sb = es.enter_context(nc.sbuf_tensor("s0_wv", [128, NKT * 1024], BF16))
        bq_sb = es.enter_context(nc.sbuf_tensor("s0_bq", [128, 8], F32))
        bkm2_sb = es.enter_context(nc.sbuf_tensor("s0_bkm2", [128, 8], F32))
        bvbc_sb = es.enter_context(nc.sbuf_tensor("s0_bvbc", [128, 1024], F32))
        onesc = es.enter_context(nc.sbuf_tensor("s0_ones", [128, 1], BF16))
        xb = [es.enter_context(nc.sbuf_tensor(f"s0_xb{i}", [128, NKT * 512], BF16)) for i in range(2)]
        qt = es.enter_context(nc.sbuf_tensor("s0_qt", [128, NKT * 512], BF16))
        kt = es.enter_context(nc.sbuf_tensor("s0_kt", [128, NKT * 512], BF16))
        sq = es.enter_context(nc.sbuf_tensor("s0_sq", [128, NKT * 512], BF16))
        vt = es.enter_context(nc.sbuf_tensor("s0_vt", [128, 1024], BF16))
        rowq = es.enter_context(nc.sbuf_tensor("s0_rowq", [1, 512], BF16))
        rowk = es.enter_context(nc.sbuf_tensor("s0_rowk", [1, 512], BF16))
        rone = es.enter_context(nc.sbuf_tensor("s0_rone", [1, 512], BF16))
        psA = es.enter_context(nc.psum_tensor("s0_psA", [128, 512], F32))
        psB = es.enter_context(nc.psum_tensor("s0_psB", [128, 512], F32))
        psV = es.enter_context(nc.psum_tensor("s0_psV", [128, 512], F32))
        psR = es.enter_context(nc.psum_tensor("s0_psR", [1, 512], F32))
        psR2 = es.enter_context(nc.psum_tensor("s0_psR2", [1, 512], F32))

        def gp_loads(gpsimd):
            GLD = p.s("GLD")
            for w_sb, w_dram in [(wq_sb, d["wq"]), (wk_sb, d["wk"]), (wv_sb, d["wv"])]:
                GLD.inc(gpsimd.dma_start(
                    out=rr8(w_sb, 1024),
                    in_=w_dram.rearrange("(k p) o -> p k o", p=128)), 16)
        block.gpsimd(gp_loads)

        def sy_loads(sync):
            LD.inc(sync.dma_start(out=bq_sb[:], in_=d["bq_pt"]), 16)
            LD.inc(sync.dma_start(out=bkm2_sb[:], in_=d["bk_pt"]), 16)
            LD.inc(sync.dma_start(out=bvbc_sb[:], in_=d["bv_bc"]), 16)
        block.sync(sy_loads)

        def v_init(vector):
            wge(vector, LD)
            EV.inc(vector.memset(onesc[:], 1.0))
            EV.inc(vector.memset(rone[:], 1.0))
            wge(vector, EV)
            EV.inc(vector.tensor_scalar_mul(bkm2_sb[:], bkm2_sb[:], -2.0))
        block.vector(v_init)

        def sy_aug(sync):
            wge(sync, EV)
            for blk in range(16):
                ST.inc(sync.dma_start(out=d["qaug"][0:1, blk * 512:(blk + 1) * 512], in_=rone[:]), 16)
                ST.inc(sync.dma_start(out=d["kaug"][1:2, blk * 512:(blk + 1) * 512], in_=rone[:]), 16)
        block.sync(sy_aug)

        def load_blk(gpsimd, blk):
            p.s("GLD").inc(gpsimd.dma_start(
                out=rr8(xb[blk % 2], 512),
                in_=drview(d["xT"], blk * 512, (blk + 1) * 512)), 16)

        block.gpsimd(lambda g: load_blk(g, 0))

        for blk in range(16):
            x_sb = xb[blk % 2]
            if blk + 1 < 16:
                def gp(g, blk=blk):
                    wge(g, MM)
                    load_blk(g, blk + 1)
                block.gpsimd(gp)

            for proj in range(2):
                w_sb = [wq_sb, wk_sb][proj]
                dst = [d["qt8"], d["kts8"]][proj]
                b_ap = [bq_sb, bkm2_sb][proj]
                scale_ap = [1.0, p.c(3)][proj]
                tgt_sb = [qt, kt][proj]
                for ot in range(8):
                    ps = psA if ot % 2 == 0 else psB
                    def t_mm(tensor, ot=ot, ps=ps, w_sb=w_sb, x_sb=x_sb):
                        wge(tensor, LD)
                        wge(tensor, p.s("GLD"))
                        wge(tensor, AC)
                        mm_group(tensor, ps[:],
                                 [(w_sb[:, k * 1024 + ot * 128: k * 1024 + ot * 128 + 128],
                                   x_sb[:, k * 512:(k + 1) * 512]) for k in range(NKT)], MM)
                    block.tensor(t_mm)

                    def s_ev(scalar, ot=ot, ps=ps, b_ap=b_ap, scale_ap=scale_ap, tgt_sb=tgt_sb):
                        wge(scalar, MM)
                        wge(scalar, ST)
                        AC.inc(scalar.activation(tgt_sb[:, ot * 512:(ot + 1) * 512], ps[:],
                                                 AF.Identity, bias=b_ap[:, ot:ot + 1], scale=scale_ap))
                    block.scalar(s_ev)

                    def v_sq(vector, ot=ot, tgt_sb=tgt_sb):
                        wge(vector, AC)
                        EV.inc(vector.tensor_mul(sq[:, ot * 512:(ot + 1) * 512],
                                                 tgt_sb[:, ot * 512:(ot + 1) * 512],
                                                 tgt_sb[:, ot * 512:(ot + 1) * 512]))
                    block.vector(v_sq)

                def sy_st(sync, dst=dst, tgt_sb=tgt_sb, blk=blk):
                    wge(sync, AC)
                    ST.inc(sync.dma_start(
                        out=drview(dst, blk * 512, (blk + 1) * 512),
                        in_=rr8(tgt_sb, 512)), 16)
                block.sync(sy_st)

                psrow = psR if proj == 0 else psR2
                def t_row(tensor, psrow=psrow):
                    wge(tensor, EV)
                    mm_group(tensor, psrow[:],
                             [(onesc[:], sq[:, k * 512:(k + 1) * 512]) for k in range(NKT)], MM)
                block.tensor(t_row)

                rowsb = [rowq, rowk][proj]
                rsc_ap = [1.0, p.c1(5)][proj]
                def s_row(scalar, psrow=psrow, rowsb=rowsb, rsc_ap=rsc_ap):
                    wge(scalar, MM)
                    wge(scalar, ST)
                    AC.inc(scalar.activation(rowsb[:], psrow[:], AF.Identity, scale=rsc_ap))
                block.scalar(s_row)

                aug_dst = d["qaug"][1:2] if proj == 0 else d["kaug"][0:1]
                def sy_row(sync, aug_dst=aug_dst, rowsb=rowsb, blk=blk):
                    wge(sync, AC)
                    ST.inc(sync.dma_start(out=aug_dst[:, blk * 512:(blk + 1) * 512], in_=rowsb[:]), 16)
                block.sync(sy_row)

            for tt in range(4):
                for oc in range(2):
                    def t_v(tensor, tt=tt, oc=oc, x_sb=x_sb):
                        wge(tensor, p.s("GLD"))
                        wge(tensor, EV)
                        mm_group(tensor, psV[:],
                                 [(x_sb[:, k * 512 + tt * 128: k * 512 + tt * 128 + 128],
                                   wv_sb[:, k * 1024 + oc * 512: k * 1024 + oc * 512 + 512])
                                  for k in range(NKT)], MM)
                    block.tensor(t_v)

                    def v_ev(vector, oc=oc):
                        wge(vector, MM)
                        wge(vector, ST)
                        EV.inc(vector.tensor_add(vt[:, oc * 512:(oc + 1) * 512], psV[:],
                                                 bvbc_sb[:, oc * 512:(oc + 1) * 512]))
                    block.vector(v_ev)

                def sy_v(sync, tt=tt, blk=blk):
                    wge(sync, EV)
                    ST.inc(sync.dma_start(
                        out=d["vb"][blk * 512 + tt * 128: blk * 512 + tt * 128 + 128, :],
                        in_=vt[:]), 16)
                block.sync(sy_v)

        def sy_fin(sync):
            wge(sync, ST)
        block.sync(sy_fin)


def emit_s1(p, d):
    nc, block = p.nc, p.block
    p.barrier()
    LD, LD2, LD3, MM, EV, AC, ST = (p.s(x) for x in ["LD", "LD2", "LD3", "MM", "EV", "AC", "ST"])
    es = ExitStack()
    with es:
        wo_sb = es.enter_context(nc.sbuf_tensor("s1_wo", [128, NKT * 1024], BF16))
        bo_sb = es.enter_context(nc.sbuf_tensor("s1_bo", [128, 8], F32))
        onesc = es.enter_context(nc.sbuf_tensor("s1_ones", [128, 1], BF16))
        onesf = es.enter_context(nc.sbuf_tensor("s1_onesf", [1, 128], F32))
        q8 = [es.enter_context(nc.sbuf_tensor(f"s1_q8_{i}", [128, NKT * 512], BF16)) for i in range(2)]
        qa2 = [es.enter_context(nc.sbuf_tensor(f"s1_qa2_{i}", [2, 512], BF16)) for i in range(2)]
        ktile = [es.enter_context(nc.sbuf_tensor(f"s1_kt{i}", [128, NKT * 128], BF16)) for i in range(2)]
        ka2 = [es.enter_context(nc.sbuf_tensor(f"s1_ka2_{i}", [2, 128], BF16)) for i in range(2)]
        fcol = es.enter_context(nc.sbuf_tensor("s1_fcol", [128, 64 * 512], BF16))
        etmp = [es.enter_context(nc.sbuf_tensor(f"s1_etmp{i}", [128, 512], F32)) for i in range(2)]
        vcol = es.enter_context(nc.sbuf_tensor("s1_vcol", [128, 64 * 128], BF16))
        x_sb = es.enter_context(nc.sbuf_tensor("s1_x", [128, NKT * 512], F32))
        attnT = es.enter_context(nc.sbuf_tensor("s1_attnT", [128, NKT * 512], BF16))
        rden = es.enter_context(nc.sbuf_tensor("s1_rden", [1, 512], F32))
        rb = es.enter_context(nc.sbuf_tensor("s1_rb", [128, 512], F32))
        tb = es.enter_context(nc.sbuf_tensor("s1_tb", [128, NKT * 512], BF16))
        rows = es.enter_context(nc.sbuf_tensor("s1_rows", [1, 5 * 512], F32))
        x1f = es.enter_context(nc.sbuf_tensor("s1_x1f", [128, NKT * 512], F32))
        x1b = es.enter_context(nc.sbuf_tensor("s1_x1b", [128, NKT * 512], BF16))
        psD = [es.enter_context(nc.psum_tensor(f"s1_psD{i}", [128, 512], F32)) for i in range(2)]
        psDen = es.enter_context(nc.psum_tensor("s1_psDen", [1, 512], F32))
        psO = [es.enter_context(nc.psum_tensor(f"s1_psO{i}", [128, 512], F32)) for i in range(2)]
        psBCa = es.enter_context(nc.psum_tensor("s1_psBCa", [128, 512], F32))
        psBCb = es.enter_context(nc.psum_tensor("s1_psBCb", [128, 512], F32))

        ln_bufs = dict(t_f=x_sb, tb=tb, sqb=attnT, rows=rows, onesc=onesc, onesf=onesf,
                       ps_sum=psDen[:], ps_sumsq=psBCa[0:1, :], ps_bcmu=psBCa[:],
                       ps_bcr=psBCb[:], outf=x1f, outb=x1b)

        def gp0(gpsimd):
            p.s("GLD").inc(gpsimd.dma_start(
                out=rr8(wo_sb, 1024),
                in_=d["wo"].rearrange("(k p) o -> p k o", p=128)), 16)
        block.gpsimd(gp0)

        def sy0(sync):
            LD.inc(sync.dma_start(out=bo_sb[:], in_=d["bo_pt"]), 16)
        block.sync(sy0)

        def v0(vector):
            wge(vector, LD)
            EV.inc(vector.memset(onesc[:], 1.0))
            EV.inc(vector.memset(onesf[:], 1.0))
        block.vector(v0)

        def load_chunk(sync, c):
            i = c % 2
            LD2.inc(sync.dma_start(out=rr8(q8[i], 512),
                                   in_=drview(d["qt8"], c * 512, (c + 1) * 512)), 16)
            LD2.inc(sync.dma_start(out=qa2[i][:], in_=d["qaug"][:, c * 512:(c + 1) * 512]), 16)

        block.sync(lambda s: load_chunk(s, 0))

        def load_ktile(sync, kt_i):
            i = kt_i % 2
            LD.inc(sync.dma_start(out=rr8(ktile[i], 128),
                                  in_=drview(d["kts8"], kt_i * 128, (kt_i + 1) * 128)), 16)
            LD.inc(sync.dma_start(out=ka2[i][:], in_=d["kaug"][:, kt_i * 128:(kt_i + 1) * 128]), 16)

        def load_vcol(gpsimd, dt):
            LD3.inc(gpsimd.dma_start(
                out=vcol.rearrange("p (k c) -> p k c", k=64),
                in_=d["vb"].rearrange("(k p) c -> p k c", p=128)[:, :, dt * 128:(dt + 1) * 128]), 16)

        for c in range(16):
            ci = c % 2
            if c + 1 < 16:
                def syn(s, c=c):
                    wge(s, MM)
                    load_chunk(s, c + 1)
                block.sync(syn)

            def syx(s, c=c):
                wge(s, EV)
                wge(s, AC)
                LD2.inc(s.dma_start(out=rr8(x_sb, 512),
                                    in_=drview(d["xT"], c * 512, (c + 1) * 512)), 16)
            block.sync(syx)

            def sy_k0(s):
                wge(s, MM)
                load_ktile(s, 0)
            block.sync(sy_k0)

            for kt_i in range(64):
                kti = kt_i % 2
                if kt_i + 1 < 64:
                    def syk(s, kt_i=kt_i):
                        wge(s, MM)
                        load_ktile(s, kt_i + 1)
                    block.sync(syk)

                ps = psD[kt_i % 2]
                def t_d(tensor, kti=kti, ps=ps, ci=ci):
                    wge(tensor, LD)
                    wge(tensor, LD2)
                    wge(tensor, AC, lag=1)
                    pairs = [(ktile[kti][:, k * 128:(k + 1) * 128], q8[ci][:, k * 512:(k + 1) * 512])
                             for k in range(NKT)]
                    pairs.append((ka2[kti][:], qa2[ci][:]))
                    mm_group(tensor, ps[:], pairs, MM)
                block.tensor(t_d)

                def s_e(scalar, ps=ps, kt_i=kt_i):
                    wge(scalar, MM)
                    wge(scalar, AC)
                    et = etmp[kt_i % 2]
                    AC.inc(scalar.activation(et[:], ps[:], AF.Exp, scale=p.c(2)))
                    wge(scalar, AC)
                    AC.inc(scalar.activation(fcol[:, kt_i * 512:(kt_i + 1) * 512], et[:], AF.Exp))
                block.scalar(s_e)

            def t_den(tensor):
                wge(tensor, AC)
                mm_group(tensor, psDen[:],
                         [(onesc[:], fcol[:, k * 512:(k + 1) * 512]) for k in range(64)], MM)
            block.tensor(t_den)

            def v_den(vector):
                wge(vector, MM)
                EV.inc(vector.reciprocal(rden[:], psDen[:]))
            block.vector(v_den)

            def t_bc(tensor):
                wge(tensor, EV)
                mm_group(tensor, psBCa[:], [(onesf[:], rden[:])], MM)
            block.tensor(t_bc)

            def s_bc(scalar):
                wge(scalar, MM)
                AC.inc(scalar.copy(rb[:], psBCa[:]))
            block.scalar(s_bc)

            def gp_v0(g):
                wge(g, MM)
                load_vcol(g, 0)
            block.gpsimd(gp_v0)
            for dt in range(8):
                ps = psO[dt % 2]
                def t_o(tensor, dt=dt, ps=ps):
                    wge(tensor, LD3)
                    wge(tensor, EV, lag=0)
                    mm_group(tensor, ps[:],
                             [(vcol[:, k * 128:(k + 1) * 128], fcol[:, k * 512:(k + 1) * 512])
                              for k in range(64)], MM)
                block.tensor(t_o)
                if dt + 1 < 8:
                    def gpv(g, dt=dt):
                        wge(g, MM)
                        load_vcol(g, dt + 1)
                    block.gpsimd(gpv)

                def v_o(vector, dt=dt, ps=ps):
                    wge(vector, MM)
                    wge(vector, AC)
                    EV.inc(vector.tensor_mul(attnT[:, dt * 512:(dt + 1) * 512], ps[:], rb[:]))
                block.vector(v_o)

            if "attn_dbg" in d:
                def sy_dbg(sync, c=c):
                    wge(sync, EV)
                    ST.inc(sync.dma_start(out=drview(d["attn_dbg"], c * 512, (c + 1) * 512),
                                          in_=rr8(attnT, 512)), 16)
                block.sync(sy_dbg)

            for ot in range(8):
                ps = psD[ot % 2]
                def t_w(tensor, ot=ot, ps=ps):
                    wge(tensor, p.s("GLD"))
                    wge(tensor, EV)
                    mm_group(tensor, ps[:],
                             [(wo_sb[:, k * 1024 + ot * 128: k * 1024 + ot * 128 + 128],
                               attnT[:, k * 512:(k + 1) * 512]) for k in range(NKT)], MM)
                block.tensor(t_w)

                def s_w(scalar, ot=ot, ps=ps):
                    wge(scalar, MM)
                    wge(scalar, AC)
                    wge(scalar, ST)
                    AC.inc(scalar.activation(x1f[:, ot * 512:(ot + 1) * 512], ps[:],
                                             AF.Identity, bias=bo_sb[:, ot:ot + 1]))
                block.scalar(s_w)

                def v_w(vector, ot=ot):
                    wge(vector, AC)
                    wge(vector, LD2)
                    EV.inc(vector.tensor_add(x_sb[:, ot * 512:(ot + 1) * 512],
                                             x1f[:, ot * 512:(ot + 1) * 512],
                                             x_sb[:, ot * 512:(ot + 1) * 512]))
                block.vector(v_w)

            emit_ln(p, d, ln_bufs, 512, make_bf=True)

            def sy_out(sync, c=c):
                wge(sync, EV)
                wge(sync, AC)
                ST.inc(sync.dma_start(out=drview(d["x1f"], c * 512, (c + 1) * 512),
                                      in_=rr8(x1f, 512)), 16)
                ST.inc(sync.dma_start(out=drview(d["x1b"], c * 512, (c + 1) * 512),
                                      in_=rr8(x1b, 512)), 16)
            block.sync(sy_out)

        def sy_fin(sync):
            wge(sync, ST)
        block.sync(sy_fin)


def wge_val(eng, sem, v):
    if v > 0:
        eng.wait_ge(sem.h, v)


def emit_attn(p, d, li, xin_f, xin_b, xout_f, xout_b):
    nc, block = p.nc, p.block
    p.barrier()
    LD, GLD, ST = p.s("LD"), p.s("GLD"), p.s("ST")
    names = ["aX", "aQK", "aQE", "aRot", "aTr", "aTrE", "aSim", "aExp", "aMsk",
             "aOut", "aRcp", "aNrmE", "aTr2", "aTr2E", "aWo", "aWoE", "aPc",
             "aLnS", "aLnR", "aLnR1", "aLnB", "aLnA"]
    sm = {n: p.s(n) for n in names}
    (aX, aQK, aQE, aRot, aTr, aTrE, aSim, aExp, aMsk, aOut, aRcp, aNrmE, aTr2,
     aTr2E, aWo, aWoE, aPc, aLnS, aLnR, aLnR1, aLnB, aLnA) = (sm[n] for n in names)
    M = [dict((n, 0) for n in names + ["ST"]) for _ in range(NW + 2)]
    loads_after = {}
    qe_after = {}
    exp_after = {}
    msk_after = {}

    def mark(i):
        for n in names:
            M[i + 1][n] = sm[n].n
        M[i + 1]["ST"] = ST.n

    es = ExitStack()
    pre = f"a{li}_"
    with es:
        wqkv_sb = es.enter_context(nc.sbuf_tensor(pre + "wqkv", [128, NKT * 3072], BF16))
        woi_sb = es.enter_context(nc.sbuf_tensor(pre + "woi", [128, NKT * 1024], BF16))
        qs_sb = es.enter_context(nc.sbuf_tensor(pre + "qs", [128, DH], F32))
        ks_sb = es.enter_context(nc.sbuf_tensor(pre + "ks", [128, DH], F32))
        qsw_sb = es.enter_context(nc.sbuf_tensor(pre + "qsw", [128, DH], F32))
        ksw_sb = es.enter_context(nc.sbuf_tensor(pre + "ksw", [128, DH], F32))
        g_sb = es.enter_context(nc.sbuf_tensor(pre + "g", [128, 8], F32))
        lb_sb = es.enter_context(nc.sbuf_tensor(pre + "lb", [128, 8], F32))
        ident = es.enter_context(nc.sbuf_tensor(pre + "id", [128, 128], BF16))
        triA = es.enter_context(nc.sbuf_tensor(pre + "triA", [128, 128], BF16))
        triB = es.enter_context(nc.sbuf_tensor(pre + "triB", [128, 128], BF16))
        tab_t = es.enter_context(nc.sbuf_tensor(pre + "tab", [128, 4 * 2 * DH], F32))
        tab = [tab_t, tab_t]
        x_sb = [es.enter_context(nc.sbuf_tensor(pre + f"x{i}", [128, NKT * 257], F32)) for i in range(2)]
        xb_sb = [es.enter_context(nc.sbuf_tensor(pre + f"xbw{i}", [128, NKT * 257], BF16)) for i in range(2)]
        q_tm = es.enter_context(nc.sbuf_tensor(pre + "qtm", [128, 1024], BF16))
        k_tm = es.enter_context(nc.sbuf_tensor(pre + "ktm", [128, 1024], BF16))
        qn_sb = es.enter_context(nc.sbuf_tensor(pre + "qn", [128, 1024], BF16))
        tmp1 = es.enter_context(nc.sbuf_tensor(pre + "tmp1", [128, 1024], BF16))
        qrot = es.enter_context(nc.sbuf_tensor(pre + "qrot", [128, 1024], BF16))
        krot = es.enter_context(nc.sbuf_tensor(pre + "krot", [128, 1024], BF16))
        ssq = es.enter_context(nc.sbuf_tensor(pre + "ssq", [128, 32], F32))
        rr_ = es.enter_context(nc.sbuf_tensor(pre + "rr", [128, 32], F32))
        csq = es.enter_context(nc.sbuf_tensor(pre + "csq", [128, 4 * DH], F32))
        qT = es.enter_context(nc.sbuf_tensor(pre + "qT", [64, H * 256], BF16))
        kT = [es.enter_context(nc.sbuf_tensor(pre + f"kT{i}", [64, H * 256], BF16)) for i in range(2)]
        vA = [es.enter_context(nc.sbuf_tensor(pre + f"vA{i}", [128, 2 * H * 65], BF16)) for i in range(3)]
        emT = es.enter_context(nc.sbuf_tensor(pre + "emT", [128, H * 768], BF16))
        attnout = es.enter_context(nc.sbuf_tensor(pre + "ao", [128, 2 * 1024], BF16))
        f_sb = es.enter_context(nc.sbuf_tensor(pre + "f", [128, 2 * H], F32))
        aT = es.enter_context(nc.sbuf_tensor(pre + "aT", [128, NKT * 256], BF16))
        xo_t = es.enter_context(nc.sbuf_tensor(pre + "xo", [128, NKT * 258], F32))
        xo = [xo_t, xo_t]
        tb_a = es.enter_context(nc.sbuf_tensor(pre + "tba", [128, NKT * 256], BF16))
        sqb_a = es.enter_context(nc.sbuf_tensor(pre + "sqba", [128, NKT * 256], BF16))
        xob = es.enter_context(nc.sbuf_tensor(pre + "xob", [128, NKT * 256], BF16))
        xn_sb = es.enter_context(nc.sbuf_tensor(pre + "xn", [128, NKT * 256], BF16))
        u_sb = es.enter_context(nc.sbuf_tensor(pre + "u", [128, 256], F32))
        rows = es.enter_context(nc.sbuf_tensor(pre + "rows", [1, 5 * 256], F32))
        onesc = es.enter_context(nc.sbuf_tensor(pre + "onesc", [128, 1], BF16))
        onesf = es.enter_context(nc.sbuf_tensor(pre + "onesf", [1, 128], F32))
        psMM = [es.enter_context(nc.psum_tensor(pre + f"psMM{i}", [128, 512], F32)) for i in range(2)]
        psT = es.enter_context(nc.psum_tensor(pre + "psT", [128, 1024], BF16))
        psS = [es.enter_context(nc.psum_tensor(pre + f"psS{i}", [128, 512], F32)) for i in range(2)]
        psG = [es.enter_context(nc.psum_tensor(pre + f"psG{i}", [128, 512], F32)) for i in range(2)]
        psT2 = es.enter_context(nc.psum_tensor(pre + "psT2", [128, 1024], BF16))
        psTs = [psT, psT2]

        def gp0(gpsimd):
            GLD.inc(gpsimd.dma_start(out=rr8(wqkv_sb, 3072),
                                     in_=d["wqkv"][li].rearrange("(k p) o -> p k o", p=128)), 16)
            GLD.inc(gpsimd.dma_start(out=rr8(woi_sb, 1024),
                                     in_=d["wo_i"][li].rearrange("(k p) o -> p k o", p=128)), 16)
        block.gpsimd(gp0)

        def sy0(sync):
            for dst, src_ in [(qs_sb, d["qs_bc"][li]), (ks_sb, d["ks_bc"][li]),
                              (qsw_sb, d["qs_sw"][li]), (ksw_sb, d["ks_sw"][li]),
                              (g_sb, d["g_pt"][li]), (lb_sb, d["lb_pt"][li]),
                              (ident, d["ident"]), (triA, d["triA"]), (triB, d["triB"])]:
                LD.inc(sync.dma_start(out=dst[:], in_=src_), 16)
        block.sync(sy0)

        def v0(vector):
            wge(vector, LD)
            aPc.inc(vector.memset(onesc[:], 1.0))
            wge(vector, aPc)
            aPc.inc(vector.memset(onesf[:], 1.0))
            for i in range(3):
                wge(vector, aPc)
                aPc.inc(vector.memset(
                    vA[i].rearrange("p (c h e) -> p c h e", c=2, h=H)[:, :, :, 64:65], 1.0))
            wge(vector, aPc)
            aPc.inc(vector.memset(rr8(x_sb[0], 257)[:, :, 0:1], 0.0))
            wge(vector, aPc)
            aPc.inc(vector.memset(rr8(xb_sb[0], 257)[:, :, 0:1], 0.0))
            wge(vector, aPc)
            aPc.inc(vector.memset(rr8(xo[0], 258)[:, :, 0:1], 0.0))
        block.vector(v0)
        init_pc = aPc.n

        def ph_loads(j):
            if j >= NW:
                return

            def sy_x(sync, j=j):
                w0 = j * W
                wge(sync, aQK)
                wge(sync, aWoE)
                wge(sync, aRot)
                wge_val(sync, aPc, init_pc)
                if j == 0:
                    aX.inc(sync.dma_start(out=rr8(x_sb[0], 257)[:, :, 1:257],
                                          in_=drview(xin_f, 0, 256)), 16)
                    aX.inc(sync.dma_start(out=rr8(xb_sb[0], 257)[:, :, 1:257],
                                          in_=drview(xin_b, 0, 256)), 16)
                else:
                    aX.inc(sync.dma_start(out=rr8(x_sb[j % 2], 257),
                                          in_=drview(xin_f, w0 - 1, w0 + 256)), 16)
                    aX.inc(sync.dma_start(out=rr8(xb_sb[j % 2], 257),
                                          in_=drview(xin_b, w0 - 1, w0 + 256)), 16)
                for ti, nm in enumerate(["cs_q", "ss_q", "cs_k", "ss_k"]):
                    aX.inc(sync.dma_start(
                        out=tab[j % 2].rearrange("p (ti tt e) -> p ti tt e", ti=4, tt=2)[:, ti],
                        in_=d[nm].rearrange("(tt p) e -> p tt e", p=128)[:, 2 * j:2 * j + 2, :]), 16)
            block.sync(sy_x)
            loads_after[j] = aX.n

        def ph_qkv(j):
            if j >= NW:
                return
            qlvl = globals().get("_DBG_QLVL", 99)
            xbw = xb_sb[j % 2]
            for tt in range(2):
                def tview(kt, tt=tt, xbw=xbw):
                    off = (1 if kt < 4 else 0) + tt * 128
                    return xbw[:, kt * 257 + off: kt * 257 + off + 128]

                for ch in range(6):
                    ps = psMM[ch % 2]

                    def t_qkv(tensor, ps=ps, ch=ch, tt=tt, j=j):
                        wge(tensor, GLD)
                        wge_val(tensor, aX, loads_after[j])
                        wge(tensor, aWoE)
                        wge(tensor, aQE, lag=1)
                        mm_group(tensor, ps[:],
                                 [(tview(k), wqkv_sb[:, k * 3072 + ch * 512: k * 3072 + ch * 512 + 512])
                                  for k in range(NKT)], aQK)
                    block.tensor(t_qkv)

                    def v_ev(vector, ps=ps, ch=ch, tt=tt, j=j):
                        wge(vector, aQK)
                        if ch < 2:
                            aQE.inc(vector.tensor_copy(q_tm[:, ch * 512:(ch + 1) * 512], ps[:]))
                        elif ch < 4:
                            aQE.inc(vector.tensor_copy(k_tm[:, (ch - 2) * 512:(ch - 1) * 512], ps[:]))
                        else:
                            if j >= 2:
                                wge_val(vector, aOut, M[j - 1]["aOut"])
                            h0 = (ch - 4) * 8
                            aQE.inc(vector.tensor_copy(
                                vA[j % 3].rearrange("p (c h e) -> p c h e", c=2, h=H)[:, tt, h0:h0 + 8, 0:64],
                                ps.rearrange("p (h e) -> p h e", e=64)))
                    block.vector(v_ev)

                if qlvl < 2:
                    continue
                def v_ss(vector, tt=tt, j=j):
                    wge(vector, aQE)
                    aRot.inc(vector.tensor_mul(tmp1[:], q_tm[:], q_tm[:]))
                    wge(vector, aRot)
                    aRot.inc(vector.tensor_reduce(ssq[:, 0:16],
                                                  tmp1.rearrange("p (h e) -> p h e", e=64),
                                                  axis=AX.X, op=ALU.add))
                    wge(vector, aRot)
                    aRot.inc(vector.tensor_mul(tmp1[:], k_tm[:], k_tm[:]))
                    wge(vector, aRot)
                    aRot.inc(vector.tensor_reduce(ssq[:, 16:32],
                                                  tmp1.rearrange("p (h e) -> p h e", e=64),
                                                  axis=AX.X, op=ALU.add))
                    wge_val(vector, aX, loads_after[j])
                    t3 = tab[j % 2].rearrange("p (ti tt e) -> p ti tt e", ti=4, tt=2)
                    c4 = csq.rearrange("p (ti e) -> p ti e", ti=4)
                    aRot.inc(vector.tensor_mul(c4[:, 0], t3[:, 0, tt], qs_sb[:]))
                    aRot.inc(vector.tensor_mul(c4[:, 1], t3[:, 1, tt], qsw_sb[:]))
                    aRot.inc(vector.tensor_mul(c4[:, 2], t3[:, 2, tt], ks_sb[:]))
                    aRot.inc(vector.tensor_mul(c4[:, 3], t3[:, 3, tt], ksw_sb[:]))
                block.vector(v_ss)

                def s_rr(scalar):
                    wge(scalar, aRot)
                    aRcp.inc(scalar.activation(rr_[:], ssq[:], AF.Ln))
                    wge(scalar, aRcp)
                    aRcp.inc(scalar.activation(rr_[:], rr_[:], AF.Exp, scale=p.c(1)))
                block.scalar(s_rr)

                def v_rrmin(vector):
                    wge(vector, aRcp)
                    aRot.inc(vector.tensor_scalar_min(rr_[:], rr_[:], 1.0e12))
                block.vector(v_rrmin)

                for which in (range(2) if qlvl >= 3 else []):
                    src = q_tm if which == 0 else k_tm
                    dst = qrot if which == 0 else krot
                    rr_sl = rr_[:, 0:16] if which == 0 else rr_[:, 16:32]
                    cs_i, ss_i = (0, 1) if which == 0 else (2, 3)

                    def v_rot(vector, src=src, dst=dst, rr_sl=rr_sl, cs_i=cs_i, ss_i=ss_i):
                        wge(vector, aRot)
                        wge(vector, aTr)
                        c4 = csq.rearrange("p (ti e) -> p ti e", ti=4)
                        h3 = lambda t: t.rearrange("p (h e) -> p h e", e=64)
                        h4 = lambda t: t.rearrange("p (h a c) -> p h a c", a=2, c=32)
                        aRot.inc(vector.tensor_mul(
                            h3(qn_sb), h3(src),
                            rr_sl.rearrange("p (h e) -> p h e", e=1).broadcast_to((128, 16, 64))))
                        wge(vector, aRot)
                        aRot.inc(vector.tensor_mul(
                            h3(tmp1), h3(qn_sb),
                            c4[:, cs_i].rearrange("p (o e) -> p o e", o=1).broadcast_to((128, 16, 64))))
                        wge(vector, aRot)
                        ss_t = c4[:, ss_i].rearrange("p (a c) -> p a c", a=2)
                        aRot.inc(vector.tensor_mul(
                            h4(dst)[:, :, 0, :], h4(qn_sb)[:, :, 1, :],
                            ss_t[:, 0:1, :].broadcast_to((128, 16, 32))))
                        wge(vector, aRot)
                        aRot.inc(vector.tensor_mul(
                            h4(dst)[:, :, 1, :], h4(qn_sb)[:, :, 0, :],
                            ss_t[:, 1:2, :].broadcast_to((128, 16, 32))))
                        wge(vector, aRot)
                        aRot.inc(vector.tensor_add(dst[:], dst[:], tmp1[:]))
                    block.vector(v_rot)

                for which in (range(2) if qlvl >= 4 else []):
                    src = qrot if which == 0 else krot
                    dstT = qT if which == 0 else kT[j % 2]
                    for bb in range(4):
                        def t_tr(tensor, src=src, bb=bb, j=j):
                            wge(tensor, aRot)
                            wge(tensor, aSim)
                            wge(tensor, aTr2E)
                            wge(tensor, aTrE, lag=1)
                            inst = None
                            for hh in range(4):
                                h = bb * 4 + hh
                                inst = tensor.transpose(psTs[bb % 2][0:64, hh * 128:(hh + 1) * 128],
                                                        src[:, h * 64:(h + 1) * 64], ident[:])
                            aTr.inc(inst)
                        block.tensor(t_tr)

                        def v_trev(vector, dstT=dstT, bb=bb, tt=tt):
                            wge(vector, aTr)
                            aTrE.inc(vector.tensor_copy(
                                dstT[0:64, tt * 2048 + bb * 512: tt * 2048 + (bb + 1) * 512],
                                psTs[bb % 2][0:64, 0:512]))
                        block.vector(v_trev)
            qe_after[j] = aQE.n

        def ph_sims(j):
            for h in range(H):
                def t_sim(tensor, h=h, j=j):
                    wge(tensor, aTrE)
                    wge(tensor, aExp, lag=(2 if j > 0 else 1))
                    if h < 2:
                        wge(tensor, aLnR1)
                        wge(tensor, aNrmE)
                        wge(tensor, aLnA)
                    qh_full = qT.rearrange("p (t h c) -> p t h c", t=2, c=128)[:, :, h, :]
                    qh0 = qT[0:64, h * 128: h * 128 + 128]
                    qh1 = qT[0:64, 2048 + h * 128: 2048 + h * 128 + 128]
                    pw_ = 1 - (j % 2)
                    cw_ = j % 2
                    if j > 0:
                        tensor.matmul(psS[h % 2][:, 0:128], kT[pw_][0:64, h * 128: h * 128 + 128],
                                      qh0, start=True, stop=True)
                        tensor.matmul(psS[h % 2][:, 128:384], kT[pw_][0:64, 2048 + h * 128: 2048 + h * 128 + 128],
                                      qh_full, start=True, stop=True)
                    tensor.matmul(psG[h % 2][:, 0:256], kT[cw_][0:64, h * 128: h * 128 + 128],
                                  qh_full, start=True, stop=True)
                    aSim.inc(tensor.matmul(psG[h % 2][:, 256:384], kT[cw_][0:64, 2048 + h * 128: 2048 + h * 128 + 128],
                                           qh1, start=True, stop=True))
                block.tensor(t_sim)

                def s_exp(scalar, h=h, j=j):
                    wge(scalar, aSim)
                    wge_val(scalar, aOut, M[j]["aOut"])
                    base = h * 768
                    if j > 0:
                        aExp.inc(scalar.activation(emT[:, base: base + 384], psS[h % 2][:, 0:384], AF.Exp))
                        wge(scalar, aExp)
                    aExp.inc(scalar.activation(emT[:, base + 384: base + 768], psG[h % 2][:, 0:384], AF.Exp))
                block.scalar(s_exp)

                def g_mask(vector, h=h, j=j):
                    wge(vector, aExp)
                    base = h * 768
                    e6 = emT.rearrange("p (s c) -> p s c", c=128)
                    b6 = base // 128
                    if j > 0:
                        vector.tensor_mul(
                            e6[:, b6:b6 + 3:2, :], e6[:, b6:b6 + 3:2, :],
                            triA.rearrange("p (o c) -> p o c", o=1).broadcast_to((128, 2, 128)))
                        vector.wait_ge(aMsk.h, aMsk.n)
                    aMsk.inc(vector.tensor_mul(
                        e6[:, b6 + 3:b6 + 6:2, :], e6[:, b6 + 3:b6 + 6:2, :],
                        triB.rearrange("p (o c) -> p o c", o=1).broadcast_to((128, 2, 128))))
                block.vector(g_mask)
            exp_after[j] = aExp.n
            msk_after[j] = aMsk.n

        def ph_outs(j):
            for h in range(H):
                for qt in range(2):
                    ps_att = psS[h % 2] if qt == 0 else psG[h % 2]

                    def t_out(tensor, h=h, qt=qt, ps_att=ps_att, j=j):
                        if h < 2:
                            wge(tensor, aLnA)
                        wge_val(tensor, aMsk, msk_after[j])
                        wge_val(tensor, aExp, exp_after[j])
                        wge_val(tensor, aQE, qe_after.get(j, aQE.n))
                        wge(tensor, aNrmE, lag=2)
                        base = h * 768
                        pairs = []
                        va_c = vA[j % 3].rearrange("p (c h e) -> p c h e", c=2, h=H)
                        va_p = vA[(j - 1) % 3].rearrange("p (c h e) -> p c h e", c=2, h=H)
                        if qt == 0:
                            if j > 0:
                                pairs.append((emT[:, base + 0: base + 128], va_p[:, 0, h, :]))
                                pairs.append((emT[:, base + 128: base + 256], va_p[:, 1, h, :]))
                            pairs.append((emT[:, base + 384: base + 512], va_c[:, 0, h, :]))
                        else:
                            if j > 0:
                                pairs.append((emT[:, base + 256: base + 384], va_p[:, 1, h, :]))
                            pairs.append((emT[:, base + 512: base + 640], va_c[:, 0, h, :]))
                            pairs.append((emT[:, base + 640: base + 768], va_c[:, 1, h, :]))
                        mm_group(tensor, ps_att[:, 384:449], pairs, aOut)
                    block.tensor(t_out)

                def v_rcp(vector, h=h):
                    wge(vector, aOut)
                    wge(vector, aNrmE)
                    aRcp.inc(vector.reciprocal(f_sb[:, 2 * h:2 * h + 1], psS[h % 2][:, 448:449]))
                    wge(vector, aRcp)
                    aRcp.inc(vector.reciprocal(f_sb[:, 2 * h + 1:2 * h + 2], psG[h % 2][:, 448:449]))
                block.vector(v_rcp)

                def s_nrm(scalar, h=h):
                    wge(scalar, aRcp)
                    wge(scalar, aTr2)
                    ao = attnout.rearrange("p (q h e) -> p q h e", q=2, e=64)
                    aNrmE.inc(scalar.activation(ao[:, 0, h, :], psS[h % 2][:, 384:448],
                                                AF.Identity, scale=f_sb[:, 2 * h:2 * h + 1]))
                    wge(scalar, aNrmE)
                    aNrmE.inc(scalar.activation(ao[:, 1, h, :], psG[h % 2][:, 384:448],
                                                AF.Identity, scale=f_sb[:, 2 * h + 1:2 * h + 2]))
                block.scalar(s_nrm)

        def ph_tr2(j):
            for qt in range(2):
                for bb in range(2):
                    g2 = qt * 2 + bb
                    def t_tr2(tensor, qt=qt, bb=bb, g2=g2):
                        wge(tensor, aNrmE)
                        wge(tensor, aTrE)
                        wge(tensor, aTr2E, lag=1)
                        inst = None
                        for cc in range(4):
                            ct = bb * 4 + cc
                            inst = tensor.transpose(psTs[g2 % 2][:, cc * 128:(cc + 1) * 128],
                                                    attnout[:, qt * 1024 + ct * 128: qt * 1024 + ct * 128 + 128],
                                                    ident[:])
                        aTr2.inc(inst)
                    block.tensor(t_tr2)

                    def v_tr2e(vector, qt=qt, bb=bb, g2=g2):
                        wge(vector, aTr2)
                        aTr2E.inc(vector.tensor_copy(
                            aT[:, qt * 1024 + bb * 512: qt * 1024 + (bb + 1) * 512],
                            psTs[g2 % 2][:, 0:512]))
                    block.vector(v_tr2e)

        def ph_wo(j):
            cw_ = j % 2

            def v_pc(vector, j=j, cw_=cw_):
                if j > 0:
                    wge(vector, aWoE)
                    wge(vector, aLnS)
                    aPc.inc(vector.tensor_copy(rr8(xo[cw_], 258)[:, :, 0:1],
                                               rr8(xo[cw_], 258)[:, :, 256:257]))
                else:
                    aPc.inc(vector.engine_nop())
            block.vector(v_pc)

            for ot in range(8):
                ps = psMM[ot % 2]

                def t_wo(tensor, ot=ot, ps=ps, j=j):
                    wge(tensor, aTr2E)
                    wge_val(tensor, aQE, qe_after.get(j + 1, aQE.n))
                    wge(tensor, aWoE, lag=1)
                    aT4 = aT.rearrange("p (q c e) -> p q c e", q=2, e=128)
                    mm_group(tensor, ps[:, 0:256],
                             [(woi_sb[:, k * 1024 + ot * 128: k * 1024 + ot * 128 + 128],
                               aT4[:, :, k, :]) for k in range(NKT)], aWo)
                block.tensor(t_wo)

                def v_wo(vector, ot=ot, ps=ps, j=j, cw_=cw_):
                    wge(vector, aWo)
                    wge(vector, aPc)
                    wge_val(vector, ST, M[j]["ST"])
                    wge(vector, aLnA)
                    aWoE.inc(vector.tensor_add(xo[cw_][:, ot * 258 + 1: ot * 258 + 257],
                                               ps[:, 0:256],
                                               x_sb[j % 2][:, ot * 257 + 1: ot * 257 + 257]))
                block.vector(v_wo)

        def ph_ln(j):
            cw_ = j % 2
            T = 256

            def v_casts(vector, j=j, cw_=cw_):
                wge(vector, aWoE)
                wge_val(vector, ST, M[j]["ST"])
                aLnS.inc(vector.tensor_copy(rr8(tb_a, 256)[:, 0:4, :],
                                            rr8(xo[cw_], 258)[:, 0:4, 1:257]))
                wge(vector, aLnS)
                aLnS.inc(vector.tensor_copy(rr8(tb_a, 256)[:, 4:8, :],
                                            rr8(xo[cw_], 258)[:, 4:8, 0:256]))
                wge(vector, aLnS)
                aLnS.inc(vector.tensor_copy(rr8(xob, 256), rr8(xo[cw_], 258)[:, :, 1:257]))
                wge(vector, aLnS)
                aLnS.inc(vector.tensor_mul(sqb_a[:], tb_a[:], tb_a[:]))
            block.vector(v_casts)

            def t_lnrows(tensor, j=j):
                wge(tensor, aLnS)
                wge_val(tensor, aExp, exp_after[j])
                wge(tensor, aNrmE)
                wge(tensor, aRcp)
                mm_group(tensor, psS[0][0:1, 0:256],
                         [(onesc[:], tb_a[:, k * 256:(k + 1) * 256]) for k in range(NKT)], aLnR)
                mm_group(tensor, psS[1][0:1, 0:256],
                         [(onesc[:], sqb_a[:, k * 256:(k + 1) * 256]) for k in range(NKT)], aLnR)
            block.tensor(t_lnrows)

            def s_lr(scalar):
                wge(scalar, aLnR)
                aLnR1.inc(scalar.activation(rows[:, 0:T], psS[0][0:1, 0:256], AF.Identity, scale=p.c1(4)))
                wge(scalar, aLnR1)
                aLnR1.inc(scalar.activation(rows[:, T:2 * T], psS[1][0:1, 0:256], AF.Identity, scale=p.c1(4)))
            block.scalar(s_lr)

            def v_lr(vector):
                wge(vector, aLnR1)
                aLnB.inc(vector.tensor_mul(rows[:, 2 * T:3 * T], rows[:, 0:T], rows[:, 0:T]))
                wge(vector, aLnB)
                aLnB.inc(vector.tensor_sub(rows[:, 2 * T:3 * T], rows[:, T:2 * T], rows[:, 2 * T:3 * T]))
            block.vector(v_lr)

            def s_lr2(scalar):
                wge(scalar, aLnB)
                aLnB.inc(scalar.activation(rows[:, 3 * T:4 * T], rows[:, 2 * T:3 * T], AF.Ln, bias=p.c1(0)))
                wge(scalar, aLnB)
                aLnB.inc(scalar.activation(rows[:, 3 * T:4 * T], rows[:, 3 * T:4 * T], AF.Exp, scale=p.c1(1)))
                wge(scalar, aLnB)
                aLnB.inc(scalar.activation(rows[:, 4 * T:5 * T], rows[:, 0:T], AF.Identity, scale=p.c1(2)))
            block.scalar(s_lr2)

            def t_lnbc(tensor, j=j):
                wge(tensor, aLnB)
                wge_val(tensor, aLnA, M[j]["aLnA"])
                mm_group(tensor, psS[0][:, 256:512], [(onesf[:], rows[:, 4 * T:5 * T])], aLnB)
                mm_group(tensor, psS[1][:, 256:512], [(onesf[:], rows[:, 3 * T:4 * T])], aLnB)
            block.tensor(t_lnbc)

            for kk in range(NKT):
                def v_lap(vector, kk=kk, cw_=cw_, j=j):
                    wge(vector, aLnB)
                    wge(vector, aLnA)
                    if kk == 0:
                        wge_val(vector, ST, M[j]["ST"])
                    off = (1 if kk < 4 else 0)
                    src = xo[cw_][:, kk * 258 + off: kk * 258 + off + 256]
                    aLnA.inc(vector.tensor_add(u_sb[:], src, psS[0][:, 256:512]))
                    wge(vector, aLnA)
                    aLnA.inc(vector.tensor_mul(u_sb[:], u_sb[:], psS[1][:, 256:512]))
                    wge(vector, aLnA)
                    aLnA.inc(vector.tensor_scalar(xn_sb[:, kk * 256:(kk + 1) * 256], u_sb[:],
                                                  g_sb[:, kk:kk + 1], lb_sb[:, kk:kk + 1],
                                                  op0=ALU.mult, op1=ALU.add))
                block.vector(v_lap)

            def sy_st(sync, j=j, cw_=cw_):
                w0 = j * W
                wge(sync, aLnA)
                wge(sync, aLnS)
                ST.inc(sync.dma_start(out=drview(xout_f, w0, w0 + 256),
                                      in_=rr8(xo[cw_], 258)[:, :, 1:257]), 16)
                ST.inc(sync.dma_start(out=drview(xout_b, w0, w0 + 256), in_=rr8(xob, 256)), 16)
                ST.inc(sync.dma_start(out=drview(d["xnb"], w0, w0 + 256), in_=rr8(xn_sb, 256)), 16)
            block.sync(sy_st)

        nwin = globals().get("_DBG_NWIN", NW)
        nph = globals().get("_DBG_NPH", 99)
        if nph >= -1:
            ph_loads(0)
        for w in range(nwin):
            if w == 0 and nph >= 0:
                ph_qkv(0)
                ph_loads(1)
            if nph >= 1: ph_sims(w)
            if nph >= 2: ph_qkv(w + 1)
            if nph >= 3: ph_outs(w)
            if nph >= 4: ph_tr2(w)
            if nph >= 5: ph_wo(w)
            if nph >= 6: ph_loads(w + 2)
            if nph >= 7: ph_ln(w)
            mark(w)

        def sy_fin(sync):
            wge(sync, ST)
        block.sync(sy_fin)


def emit_f1(p, d, li):
    nc, block = p.nc, p.block
    p.barrier()
    LD, LD2, MM, EV, AC, ST = (p.s(x) for x in ["LD", "LD2", "MM", "EV", "AC", "ST"])
    es = ExitStack()
    pre = f"f1_{li}_"
    with es:
        w1_sb = es.enter_context(nc.sbuf_tensor(pre + "w1", [128, NKT * DI2], BF16))
        b1a_sb = es.enter_context(nc.sbuf_tensor(pre + "b1a", [128, NAT], F32))
        b1g_sb = es.enter_context(nc.sbuf_tensor(pre + "b1g", [128, NAT], F32))
        xn = [es.enter_context(nc.sbuf_tensor(pre + f"xn{i}", [128, NKT * 512], BF16)) for i in range(2)]
        gg = es.enter_context(nc.sbuf_tensor(pre + "gg", [128, 512], F32))
        h_sb = [es.enter_context(nc.sbuf_tensor(pre + f"h{i}", [128, 512], BF16)) for i in range(2)]
        psa = [es.enter_context(nc.psum_tensor(pre + f"psa{i}", [128, 512], F32)) for i in range(2)]
        psg = [es.enter_context(nc.psum_tensor(pre + f"psg{i}", [128, 512], F32)) for i in range(2)]

        def gp0(gpsimd):
            p.s("GLD").inc(gpsimd.dma_start(out=rr8(w1_sb, DI2),
                                    in_=d["w1"][li].rearrange("(k p) o -> p k o", p=128)), 16)
        block.gpsimd(gp0)

        def sy0(sync):
            LD.inc(sync.dma_start(out=b1a_sb[:], in_=d["b1a_pt"][li]), 16)
            LD.inc(sync.dma_start(out=b1g_sb[:], in_=d["b1g_pt"][li]), 16)
        block.sync(sy0)

        def load_blk(sync, blk):
            LD2.inc(sync.dma_start(out=rr8(xn[blk % 2], 512),
                                   in_=drview(d["xnb"], blk * 512, (blk + 1) * 512)), 16)

        block.sync(lambda s: load_blk(s, 0))

        for blk in range(16):
            xs = xn[blk % 2]
            if blk + 1 < 16:
                def syn(s, blk=blk):
                    wge(s, MM)
                    load_blk(s, blk + 1)
                block.sync(syn)

            for at in range(NAT):
                aoff, asz = ATILES[at]
                pa = psa[at % 2]
                pg = psg[at % 2]
                def t_a(tensor, aoff=aoff, asz=asz, pa=pa, pg=pg, xs=xs):
                    wge(tensor, LD)
                    wge(tensor, p.s("GLD"))
                    wge(tensor, LD2)
                    wge(tensor, EV, lag=1)
                    wge(tensor, AC, lag=1)
                    mm_group(tensor, pa[0:asz, :],
                             [(w1_sb[:, k * DI2 + aoff: k * DI2 + aoff + asz],
                               xs[:, k * 512:(k + 1) * 512]) for k in range(NKT)], MM)
                    mm_group(tensor, pg[0:asz, :],
                             [(w1_sb[:, k * DI2 + DI + aoff: k * DI2 + DI + aoff + asz],
                               xs[:, k * 512:(k + 1) * 512]) for k in range(NKT)], MM)
                block.tensor(t_a)

                def s_g(scalar, at=at, asz=asz, pg=pg):
                    wge(scalar, MM)
                    wge(scalar, EV)
                    AC.inc(scalar.activation(gg[0:asz, :], pg[0:asz, :], AF.Gelu,
                                             bias=b1g_sb[0:asz, at:at + 1]))
                block.scalar(s_g)

                hb_ = h_sb[at % 2]
                def v_h(vector, at=at, asz=asz, pa=pa, hb_=hb_):
                    wge(vector, AC)
                    wge(vector, ST)
                    EV.inc(vector.scalar_tensor_tensor(hb_[0:asz, :], pa[0:asz, :],
                                                       b1a_sb[0:asz, at:at + 1], gg[0:asz, :],
                                                       op0=ALU.add, op1=ALU.mult))
                block.vector(v_h)

                def sy_h(sync, at=at, asz=asz, blk=blk, hb_=hb_):
                    wge(sync, EV)
                    ST.inc(sync.dma_start(
                        out=d["hb"][at * 128: at * 128 + asz, blk * 512:(blk + 1) * 512],
                        in_=hb_[0:asz, :]), 16)
                block.sync(sy_h)

        def sy_fin(sync):
            wge(sync, ST)
        block.sync(sy_fin)


def emit_f2(p, d, li, xres_f, xout_f, xout_b):
    nc, block = p.nc, p.block
    p.barrier()
    LD, LD2, MM, EV, AC, ST = (p.s(x) for x in ["LD", "LD2", "MM", "EV", "AC", "ST"])
    es = ExitStack()
    pre = f"f2_{li}_"
    TB = 256
    with es:
        w2_sb = es.enter_context(nc.sbuf_tensor(pre + "w2", [128, NAT * 1024], BF16))
        b2_sb = es.enter_context(nc.sbuf_tensor(pre + "b2", [128, 8], F32))
        hs = [es.enter_context(nc.sbuf_tensor(pre + f"hs{i}", [128, NAT * TB], BF16)) for i in range(2)]
        xr = [es.enter_context(nc.sbuf_tensor(pre + f"xr{i}", [128, NKT * TB], F32)) for i in range(2)]
        xof = es.enter_context(nc.sbuf_tensor(pre + "xof", [128, NKT * TB], F32))
        xob = es.enter_context(nc.sbuf_tensor(pre + "xob", [128, NKT * TB], BF16))
        ps2 = [es.enter_context(nc.psum_tensor(pre + f"ps{i}", [128, TB], F32)) for i in range(2)]

        def gp0(gpsimd):
            p.s("GLD").inc(gpsimd.dma_start(
                out=w2_sb.rearrange("p (k o) -> p k o", k=NAT)[:, 0:21, :],
                in_=d["w2"][li][0:2688].rearrange("(k p) o -> p k o", p=128)), 16)
            p.s("GLD").inc(gpsimd.dma_start(
                out=w2_sb[0:42, 21 * 1024: 22 * 1024],
                in_=d["w2"][li][2688:2730, :]), 16)
        block.gpsimd(gp0)

        def sy0(sync):
            LD.inc(sync.dma_start(out=b2_sb[:], in_=d["b2_pt"][li]), 16)
        block.sync(sy0)

        def load_blk(sync, gpsimd_unused, blk):
            i = blk % 2
            LD2.inc(sync.dma_start(
                out=hs[i].rearrange("p (k t) -> p k t", k=NAT),
                in_=d["hb"].rearrange("(k p) t -> p k t", p=128)[:, :, blk * TB:(blk + 1) * TB]), 16)
            LD2.inc(sync.dma_start(out=rr8(xr[i], TB),
                                   in_=drview(xres_f, blk * TB, (blk + 1) * TB)), 16)

        block.sync(lambda s: load_blk(s, None, 0))

        for blk in range(32):
            i = blk % 2
            if blk + 1 < 32:
                def syn(s, blk=blk):
                    wge(s, MM)
                    wge(s, EV)
                    load_blk(s, None, blk + 1)
                block.sync(syn)

            for ot in range(8):
                ps = ps2[ot % 2]
                def t_m(tensor, ot=ot, ps=ps, i=i):
                    wge(tensor, LD)
                    wge(tensor, p.s("GLD"))
                    wge(tensor, LD2)
                    wge(tensor, EV, lag=1)
                    pairs = []
                    for at in range(NAT):
                        asz = ATILES[at][1]
                        pairs.append((w2_sb[0:asz, at * 1024 + ot * 128: at * 1024 + ot * 128 + 128],
                                      hs[i][0:asz, at * TB:(at + 1) * TB]))
                    mm_group(tensor, ps[:], pairs, MM)
                block.tensor(t_m)

                def v_o(vector, ot=ot, ps=ps, i=i):
                    wge(vector, MM)
                    wge(vector, ST)
                    EV.inc(vector.scalar_tensor_tensor(xof[:, ot * TB:(ot + 1) * TB], ps[:],
                                                       b2_sb[:, ot:ot + 1],
                                                       xr[i][:, ot * TB:(ot + 1) * TB],
                                                       op0=ALU.add, op1=ALU.add))
                block.vector(v_o)

                def s_b(scalar, ot=ot):
                    wge(scalar, EV)
                    AC.inc(scalar.copy(xob[:, ot * TB:(ot + 1) * TB], xof[:, ot * TB:(ot + 1) * TB]))
                block.scalar(s_b)

            def sy_st(sync, blk=blk):
                wge(sync, EV)
                wge(sync, AC)
                ST.inc(sync.dma_start(out=drview(xout_f, blk * TB, (blk + 1) * TB),
                                      in_=rr8(xof, TB)), 16)
                ST.inc(sync.dma_start(out=drview(xout_b, blk * TB, (blk + 1) * TB),
                                      in_=rr8(xob, TB)), 16)
            block.sync(sy_st)

        def sy_fin(sync):
            wge(sync, ST)
        block.sync(sy_fin)


def emit_fin(p, d, xlast_f):
    nc, block = p.nc, p.block
    p.barrier()
    LD, MM, EV, AC, ST = (p.s(x) for x in ["LD", "MM", "EV", "AC", "ST"])
    es = ExitStack()
    with es:
        onesc = es.enter_context(nc.sbuf_tensor("fin_ones", [128, 1], BF16))
        onesf = es.enter_context(nc.sbuf_tensor("fin_onesf", [1, 128], F32))
        a_sb = es.enter_context(nc.sbuf_tensor("fin_a", [128, NKT * 512], F32))
        t_f = es.enter_context(nc.sbuf_tensor("fin_t", [128, NKT * 512], F32))
        tb = es.enter_context(nc.sbuf_tensor("fin_tb", [128, NKT * 512], BF16))
        sqb = es.enter_context(nc.sbuf_tensor("fin_sqb", [128, NKT * 512], BF16))
        rows = es.enter_context(nc.sbuf_tensor("fin_rows", [1, 5 * 512], F32))
        outf = es.enter_context(nc.sbuf_tensor("fin_outf", [128, NKT * 512], F32))
        psR1 = es.enter_context(nc.psum_tensor("fin_psR1", [1, 512], F32))
        psBCa = es.enter_context(nc.psum_tensor("fin_psBCa", [128, 512], F32))
        psBCb = es.enter_context(nc.psum_tensor("fin_psBCb", [128, 512], F32))

        def v0(vector):
            EV.inc(vector.memset(onesc[:], 1.0))
            EV.inc(vector.memset(onesf[:], 1.0))
        block.vector(v0)

        bufs = dict(t_f=t_f, tb=tb, sqb=sqb, rows=rows, onesc=onesc, onesf=onesf,
                    ps_sum=psR1[:], ps_sumsq=psBCa[0:1, :], ps_bcmu=psBCa[:],
                    ps_bcr=psBCb[:], outf=outf, outb=tb)

        for c in range(16):
            def sy_l(sync, c=c):
                wge(sync, EV)
                wge(sync, AC)
                wge(sync, ST)
                LD.inc(sync.dma_start(out=rr8(t_f, 512),
                                      in_=drview(d["x1f"], c * 512, (c + 1) * 512)), 16)
                LD.inc(sync.dma_start(out=rr8(a_sb, 512),
                                      in_=drview(xlast_f, c * 512, (c + 1) * 512)), 16)
            block.sync(sy_l)

            def v_add(vector):
                wge(vector, LD)
                EV.inc(vector.tensor_add(t_f[:], t_f[:], a_sb[:]))
            block.vector(v_add)

            emit_ln(p, d, bufs, 512, make_bf=False)

            def sy_st(sync, c=c):
                wge(sync, EV)
                ST.inc(sync.dma_start(out=drview(d["outT"], c * 512, (c + 1) * 512),
                                      in_=rr8(outf, 512)), 16)
            block.sync(sy_st)

        def sy_fin(sync):
            wge(sync, ST)
        block.sync(sy_fin)


def build_program(n_layers=NL, outer_only=False, dbg=(), attn_only=False):
    nc = bass.Bass("TRN2", target_bir_lowering=False)
    d = {}
    def scratch(name, shape, dtype):
        kind = "ExternalOutput" if name in dbg else "Internal"
        return nc.dram_tensor(name, shape, dtype, kind=kind).ap()
    d["xT"] = nc.dram_tensor("xT", [D, S], F32, kind="ExternalInput").ap()
    for nm in ["wq", "wk", "wv", "wo"]:
        d[nm] = nc.dram_tensor(nm, [D, D], F32, kind="ExternalInput").ap()
    for nm in ["bq_pt", "bk_pt", "bo_pt"]:
        d[nm] = nc.dram_tensor(nm, [128, 8], F32, kind="ExternalInput").ap()
    d["bv_bc"] = nc.dram_tensor("bv_bc", [128, D], F32, kind="ExternalInput").ap()
    d["wqkv"] = nc.dram_tensor("wqkv", [NL, D, 3072], F32, kind="ExternalInput").ap()
    d["wo_i"] = nc.dram_tensor("wo_i", [NL, D, D], F32, kind="ExternalInput").ap()
    d["w1"] = nc.dram_tensor("w1", [NL, D, DI2], F32, kind="ExternalInput").ap()
    d["w2"] = nc.dram_tensor("w2", [NL, DI, D], F32, kind="ExternalInput").ap()
    d["qs_bc"] = nc.dram_tensor("qs_bc", [NL, 128, DH], F32, kind="ExternalInput").ap()
    d["ks_bc"] = nc.dram_tensor("ks_bc", [NL, 128, DH], F32, kind="ExternalInput").ap()
    d["qs_sw"] = nc.dram_tensor("qs_sw", [NL, 128, DH], F32, kind="ExternalInput").ap()
    d["ks_sw"] = nc.dram_tensor("ks_sw", [NL, 128, DH], F32, kind="ExternalInput").ap()
    d["g_pt"] = nc.dram_tensor("g_pt", [NL, 128, 8], F32, kind="ExternalInput").ap()
    d["lb_pt"] = nc.dram_tensor("lb_pt", [NL, 128, 8], F32, kind="ExternalInput").ap()
    d["b1a_pt"] = nc.dram_tensor("b1a_pt", [NL, 128, NAT], F32, kind="ExternalInput").ap()
    d["b1g_pt"] = nc.dram_tensor("b1g_pt", [NL, 128, NAT], F32, kind="ExternalInput").ap()
    d["b2_pt"] = nc.dram_tensor("b2_pt", [NL, 128, 8], F32, kind="ExternalInput").ap()
    for nm in ["cs_q", "ss_q", "cs_k", "ss_k"]:
        d[nm] = nc.dram_tensor(nm, [S, DH], F32, kind="ExternalInput").ap()
    for nm in ["triA", "triB", "ident"]:
        d[nm] = nc.dram_tensor(nm, [128, 128], BF16, kind="ExternalInput").ap()
    d["qt8"] = scratch("qt8", [D, S], BF16)
    d["kts8"] = scratch("kts8", [D, S], BF16)
    d["qaug"] = scratch("qaug", [2, S], BF16)
    d["kaug"] = scratch("kaug", [2, S], BF16)
    d["vb"] = scratch("vb", [S, D], BF16)
    d["x1f"] = scratch("x1f", [D, S], F32)
    d["x1b"] = scratch("x1b", [D, S], BF16)
    d["xf"] = [scratch(f"xf{i}", [D, S], F32) for i in range(2)]
    d["xb"] = [scratch(f"xb{i}", [D, S], BF16) for i in range(2)]
    d["xnb"] = scratch("xnb", [D, S], BF16)
    d["hb"] = scratch("hb", [NAT * 128, S], BF16)
    d["outT"] = nc.dram_tensor("outT", [D, S], F32, kind="ExternalOutput").ap()

    if "attn_dbg" in dbg:
        d["attn_dbg"] = nc.dram_tensor("attn_dbg", [D, S], BF16, kind="ExternalOutput").ap()
    if "aT_dbg" in dbg:
        d["aT_dbg"] = nc.dram_tensor("aT_dbg", [D, S], BF16, kind="ExternalOutput").ap()
    d["cvals"] = nc.dram_tensor("cvals", [128, 8], F32, kind="ExternalInput").ap()
    es = ExitStack()
    with es:
        block = es.enter_context(nc.Block())
        p = Prog(nc, es, block)
        p.cv = es.enter_context(nc.sbuf_tensor("cvals_sb", [128, 8], F32))

        def _cv_load(sync):
            p.s("LD").inc(sync.dma_start(out=p.cv[:], in_=d["cvals"]), 16)
        block.sync(_cv_load)
        emit_s0(p, d)
        emit_s1(p, d)
        if outer_only:
            emit_fin(p, d, d["x1f"])
            return nc
        cur_f, cur_b = d["x1f"], d["x1b"]
        for li in range(n_layers):
            nxt_f, nxt_b = d["xf"][li % 2], d["xb"][li % 2]
            emit_attn(p, d, li, cur_f, cur_b, nxt_f, nxt_b)
            if attn_only:
                return nc
            emit_f1(p, d, li)
            emit_f2(p, d, li, nxt_f, nxt_f, nxt_b)
            cur_f, cur_b = nxt_f, nxt_b
        emit_fin(p, d, cur_f)
    return nc


_CONSTS = None


def _inputs_for_core(b, args, consts):
    (x, wq, bq, wk, bk, wv, bv, wo, bo, wqkv, qs, ks, wo_i, g, lb, w1, b1, w2, b2) = args
    m = {
        "xT": np.ascontiguousarray(x[b].T),
        "wq": wq, "wk": wk, "wv": wv, "wo": wo,
        "bq_pt": _col(bq), "bk_pt": _col(bk), "bo_pt": _col(bo),
        "bv_bc": np.ascontiguousarray(np.broadcast_to(bv, (128, D))),
        "wqkv": wqkv, "wo_i": wo_i, "w1": w1, "w2": w2,
        "qs_bc": np.ascontiguousarray(np.broadcast_to(qs[:, None, :], (NL, 128, DH))),
        "ks_bc": np.ascontiguousarray(np.broadcast_to(ks[:, None, :], (NL, 128, DH))),
        "qs_sw": np.ascontiguousarray(np.broadcast_to(np.roll(qs, -32, -1)[:, None, :], (NL, 128, DH))),
        "ks_sw": np.ascontiguousarray(np.broadcast_to(np.roll(ks, -32, -1)[:, None, :], (NL, 128, DH))),
        "g_pt": np.stack([_col(g[i]) for i in range(NL)]),
        "lb_pt": np.stack([_col(lb[i]) for i in range(NL)]),
        "b1a_pt": np.stack([_colpad(b1[i, :DI], NAT) for i in range(NL)]),
        "b1g_pt": np.stack([_colpad(b1[i, DI:], NAT) for i in range(NL)]),
        "b2_pt": np.stack([_col(b2[i]) for i in range(NL)]),
    }
    m.update(consts)
    return m


def kernel(L0_attn_x, L0_attn_Wq, L0_attn_bq, L0_attn_Wk, L0_attn_bk, L0_attn_Wv,
           L0_attn_bv, L0_attn_Wo, L0_attn_bo, L0_ffn_Wqkv, L0_ffn_q_scale,
           L0_ffn_k_scale, L0_ffn_Wo, L0_ffn_ln_g, L0_ffn_ln_b, L0_ffn_W1,
           L0_ffn_b1, L0_ffn_W2, L0_ffn_b2):
    global _CONSTS
    if _CONSTS is None:
        _CONSTS = _build_consts()
    args = tuple(np.ascontiguousarray(np.asarray(a, dtype=np.float32)) for a in (
        L0_attn_x, L0_attn_Wq, L0_attn_bq, L0_attn_Wk, L0_attn_bk, L0_attn_Wv,
        L0_attn_bv, L0_attn_Wo, L0_attn_bo, L0_ffn_Wqkv, L0_ffn_q_scale,
        L0_ffn_k_scale, L0_ffn_Wo, L0_ffn_ln_g, L0_ffn_ln_b, L0_ffn_W1,
        L0_ffn_b1, L0_ffn_W2, L0_ffn_b2))
    nc = build_program()
    in_maps = [_inputs_for_core(b, args, _CONSTS) for b in range(8)]
    res = run_bass_kernel_spmd(nc, in_maps, core_ids=list(range(8)))
    out = np.stack([np.ascontiguousarray(np.asarray(r["outT"]).T) for r in res.results])
    return out.astype(np.float32)


# version 04
# speedup vs baseline: 1.6364x; 1.0428x over previous
import sys
sys.path.insert(0, '/opt/trn_rl_repo')
import numpy as np
import ml_dtypes
from contextlib import ExitStack

import concourse.bass as bass
import concourse.mybir as mybir
from concourse.bass_utils import run_bass_kernel_spmd

F32 = mybir.dt.float32
BF16 = mybir.dt.bfloat16
AF = mybir.ActivationFunctionType
ALU = mybir.AluOpType
AX = mybir.AxisListType

S = 8192
D = 1024
NKT = 8
H = 16
DH = 64
W = 256
NW = S // W
DI2 = 5460
DI = 2730
NL = 6
ATILES = [(i * 128, 128) for i in range(21)] + [(2688, 42)]
NAT = len(ATILES)

bfnp = ml_dtypes.bfloat16


def _build_consts():
    t = np.arange(S, dtype=np.float64)
    half = np.arange(0, DH, 2, dtype=np.float64)
    inv_freq = 1.0 / 10000.0 ** (half / DH)
    freqs = t[:, None] * inv_freq[None, :]
    freqs = np.concatenate([freqs, freqs], -1)
    base_scale = (half + 0.4 * DH) / (1.4 * DH)
    power = (t - S // 2) / (W // 2)
    scale = base_scale[None, :] ** power[:, None]
    scale = np.concatenate([scale, scale], -1)
    cos = np.cos(freqs); sin = np.sin(freqs)
    sgn = np.concatenate([-np.ones(32), np.ones(32)])[None, :]
    consts = {
        "cs_q": (8.0 * cos * scale).astype(np.float32),
        "ss_q": (8.0 * sin * scale * sgn).astype(np.float32),
        "cs_k": (cos / scale).astype(np.float32),
        "ss_k": (sin / scale * sgn).astype(np.float32),
    }
    kl = np.arange(128)[:, None]; ql = np.arange(128)[None, :]
    consts["triA"] = (ql <= kl).astype(bfnp)
    consts["triB"] = (ql >= kl).astype(bfnp)
    consts["ident"] = np.eye(128, dtype=bfnp)
    cv = np.zeros((128, 8), np.float32)
    cv[:, 0] = 1e-5; cv[:, 1] = -0.5; cv[:, 2] = -1.0; cv[:, 3] = -2.0
    cv[:, 4] = 1.0 / D; cv[:, 5] = 0.25; cv[:, 6] = 1.0
    consts["cvals"] = cv
    return consts


def _col(v):
    return np.ascontiguousarray(v.reshape(-1, 128).T)


def _colpad(v, n):
    out = np.zeros(n * 128, v.dtype)
    out[: v.shape[0]] = v
    return _col(out)


class Sem:
    def __init__(self, es, nc, name):
        self.h = es.enter_context(nc.semaphore(name))
        self.n = 0

    def inc(self, inst, k=1):
        inst.then_inc(self.h, k)
        self.n += k
        return inst


def wge(eng, sem, lag=0):
    v = sem.n - lag
    if v > 0:
        eng.wait_ge(sem.h, v)


class Prog:
    def __init__(self, nc, es, block):
        self.nc = nc
        self.es = es
        self.block = block
        self.sem = {}
        for name in ["LD", "LD2", "LD3", "MM", "EV", "AC", "ST", "GLD"]:
            self.sem[name] = Sem(es, nc, name)
        self.cv = None

    def s(self, name):
        if name not in self.sem:
            self.sem[name] = Sem(self.es, self.nc, name)
        return self.sem[name]

    def c(self, i):
        return self.cv[:, i:i + 1]

    def c1(self, i):
        return self.cv[0:1, i:i + 1]

    def barrier(self):
        for eng_open in [self.block.sync, self.block.tensor, self.block.vector,
                         self.block.scalar, self.block.gpsimd]:
            def _b(eng):
                for s in self.sem.values():
                    wge(eng, s)
            eng_open(_b)


def mm_group(tensor, out_ap, pairs, sem=None, k=1):
    inst = None
    n = len(pairs)
    for i, (l, r) in enumerate(pairs):
        inst = tensor.matmul(out_ap, l, r, start=(i == 0), stop=(i == n - 1))
    if sem is not None:
        sem.inc(inst, k)
    return inst


def rr8(ap, inner):
    return ap.rearrange("p (k t) -> p k t", t=inner)


def drview(dram, T0, T1):
    return dram.rearrange("(k p) t -> p k t", p=128)[:, :, T0:T1]


def emit_ln(p, d, bufs, T, g_ap=None, b_ap=None, make_f32=True, make_bf=False):
    block = p.block
    MM, EV, AC = p.s("MM"), p.s("EV"), p.s("AC")
    t_f, tb, sqb, rows = bufs["t_f"], bufs["tb"], bufs["sqb"], bufs["rows"]
    onesc, onesf = bufs["onesc"], bufs["onesf"]

    def s_cast(scalar):
        wge(scalar, EV)
        AC.inc(scalar.copy(tb[:, 0:NKT * T], t_f[:, 0:NKT * T]))
    block.scalar(s_cast)

    def v_sq(vector):
        wge(vector, AC)
        EV.inc(vector.tensor_mul(sqb[:, 0:NKT * T], tb[:, 0:NKT * T], tb[:, 0:NKT * T]))
    block.vector(v_sq)

    def t_rows(tensor):
        wge(tensor, EV)
        mm_group(tensor, bufs["ps_sum"],
                 [(onesc[:], tb[:, k * T:(k + 1) * T]) for k in range(NKT)], MM)
        mm_group(tensor, bufs["ps_sumsq"],
                 [(onesc[:], sqb[:, k * T:(k + 1) * T]) for k in range(NKT)], MM)
    block.tensor(t_rows)

    def s_r1(scalar):
        wge(scalar, MM)
        AC.inc(scalar.activation(rows[:, 0:T], bufs["ps_sum"], AF.Identity, scale=p.c1(4)))
        wge(scalar, AC)
        AC.inc(scalar.activation(rows[:, T:2 * T], bufs["ps_sumsq"], AF.Identity, scale=p.c1(4)))
    block.scalar(s_r1)

    def v_r(vector):
        wge(vector, AC)
        EV.inc(vector.tensor_mul(rows[:, 2 * T:3 * T], rows[:, 0:T], rows[:, 0:T]))
        wge(vector, EV)
        EV.inc(vector.tensor_sub(rows[:, 2 * T:3 * T], rows[:, T:2 * T], rows[:, 2 * T:3 * T]))
    block.vector(v_r)

    def s_r2(scalar):
        wge(scalar, EV)
        AC.inc(scalar.activation(rows[:, 3 * T:4 * T], rows[:, 2 * T:3 * T], AF.Ln, bias=p.c1(0)))
        wge(scalar, AC)
        AC.inc(scalar.activation(rows[:, 3 * T:4 * T], rows[:, 3 * T:4 * T], AF.Exp, scale=p.c1(1)))
        wge(scalar, AC)
        AC.inc(scalar.activation(rows[:, 4 * T:5 * T], rows[:, 0:T], AF.Identity, scale=p.c1(2)))
    block.scalar(s_r2)

    def t_bc(tensor):
        wge(tensor, AC)
        mm_group(tensor, bufs["ps_bcmu"], [(onesf[:], rows[:, 4 * T:5 * T])], MM)
        mm_group(tensor, bufs["ps_bcr"], [(onesf[:], rows[:, 3 * T:4 * T])], MM)
    block.tensor(t_bc)

    for k in range(NKT):
        def v_ap(vector, k=k):
            wge(vector, MM)
            EV.inc(vector.tensor_add(t_f[:, k * T:(k + 1) * T], t_f[:, k * T:(k + 1) * T],
                                     bufs["ps_bcmu"]))
            wge(vector, EV)
            if g_ap is None:
                EV.inc(vector.tensor_mul(bufs["outf"][:, k * T:(k + 1) * T],
                                         t_f[:, k * T:(k + 1) * T], bufs["ps_bcr"]))
            else:
                EV.inc(vector.tensor_mul(t_f[:, k * T:(k + 1) * T],
                                         t_f[:, k * T:(k + 1) * T], bufs["ps_bcr"]))
                wge(vector, EV)
                EV.inc(vector.tensor_scalar(bufs["outb"][:, k * T:(k + 1) * T],
                                            t_f[:, k * T:(k + 1) * T],
                                            g_ap[:, k:k + 1], b_ap[:, k:k + 1],
                                            op0=ALU.mult, op1=ALU.add))
        block.vector(v_ap)

        if g_ap is None and make_bf:
            def s_cb(scalar, k=k):
                wge(scalar, EV)
                AC.inc(scalar.copy(bufs["outb"][:, k * T:(k + 1) * T],
                                   bufs["outf"][:, k * T:(k + 1) * T]))
            block.scalar(s_cb)


def emit_s0(p, d):
    nc, block = p.nc, p.block
    p.barrier()
    LD, MM, EV, AC, ST = (p.s(x) for x in ["LD", "MM", "EV", "AC", "ST"])
    es = ExitStack()
    with es:
        wq_sb = es.enter_context(nc.sbuf_tensor("s0_wq", [128, NKT * 1024], BF16))
        wk_sb = es.enter_context(nc.sbuf_tensor("s0_wk", [128, NKT * 1024], BF16))
        wv_sb = es.enter_context(nc.sbuf_tensor("s0_wv", [128, NKT * 1024], BF16))
        bq_sb = es.enter_context(nc.sbuf_tensor("s0_bq", [128, 8], F32))
        bkm2_sb = es.enter_context(nc.sbuf_tensor("s0_bkm2", [128, 8], F32))
        bvbc_sb = es.enter_context(nc.sbuf_tensor("s0_bvbc", [128, 1024], F32))
        onesc = es.enter_context(nc.sbuf_tensor("s0_ones", [128, 1], BF16))
        xb = [es.enter_context(nc.sbuf_tensor(f"s0_xb{i}", [128, NKT * 512], BF16)) for i in range(2)]
        qt = es.enter_context(nc.sbuf_tensor("s0_qt", [128, NKT * 512], BF16))
        kt = es.enter_context(nc.sbuf_tensor("s0_kt", [128, NKT * 512], BF16))
        sq = es.enter_context(nc.sbuf_tensor("s0_sq", [128, NKT * 512], BF16))
        vt = es.enter_context(nc.sbuf_tensor("s0_vt", [128, 1024], BF16))
        rowq = es.enter_context(nc.sbuf_tensor("s0_rowq", [1, 512], BF16))
        rowk = es.enter_context(nc.sbuf_tensor("s0_rowk", [1, 512], BF16))
        rone = es.enter_context(nc.sbuf_tensor("s0_rone", [1, 512], BF16))
        psA = es.enter_context(nc.psum_tensor("s0_psA", [128, 512], F32))
        psB = es.enter_context(nc.psum_tensor("s0_psB", [128, 512], F32))
        psV = es.enter_context(nc.psum_tensor("s0_psV", [128, 512], F32))
        psR = es.enter_context(nc.psum_tensor("s0_psR", [1, 512], F32))
        psR2 = es.enter_context(nc.psum_tensor("s0_psR2", [1, 512], F32))

        def gp_loads(gpsimd):
            GLD = p.s("GLD")
            for w_sb, w_dram in [(wq_sb, d["wq"]), (wk_sb, d["wk"]), (wv_sb, d["wv"])]:
                GLD.inc(gpsimd.dma_start(
                    out=rr8(w_sb, 1024),
                    in_=w_dram.rearrange("(k p) o -> p k o", p=128)), 16)
        block.gpsimd(gp_loads)

        def sy_loads(sync):
            LD.inc(sync.dma_start(out=bq_sb[:], in_=d["bq_pt"]), 16)
            LD.inc(sync.dma_start(out=bkm2_sb[:], in_=d["bk_pt"]), 16)
            LD.inc(sync.dma_start(out=bvbc_sb[:], in_=d["bv_bc"]), 16)
        block.sync(sy_loads)

        def v_init(vector):
            wge(vector, LD)
            EV.inc(vector.memset(onesc[:], 1.0))
            EV.inc(vector.memset(rone[:], 1.0))
            wge(vector, EV)
            EV.inc(vector.tensor_scalar_mul(bkm2_sb[:], bkm2_sb[:], -2.0))
        block.vector(v_init)

        def sy_aug(sync):
            wge(sync, EV)
            for blk in range(16):
                ST.inc(sync.dma_start(out=d["qaug"][0:1, blk * 512:(blk + 1) * 512], in_=rone[:]), 16)
                ST.inc(sync.dma_start(out=d["kaug"][1:2, blk * 512:(blk + 1) * 512], in_=rone[:]), 16)
        block.sync(sy_aug)

        def load_blk(gpsimd, blk):
            p.s("GLD").inc(gpsimd.dma_start(
                out=rr8(xb[blk % 2], 512),
                in_=drview(d["xT"], blk * 512, (blk + 1) * 512)), 16)

        block.gpsimd(lambda g: load_blk(g, 0))

        for blk in range(16):
            x_sb = xb[blk % 2]
            if blk + 1 < 16:
                def gp(g, blk=blk):
                    wge(g, MM)
                    load_blk(g, blk + 1)
                block.gpsimd(gp)

            for proj in range(2):
                w_sb = [wq_sb, wk_sb][proj]
                dst = [d["qt8"], d["kts8"]][proj]
                b_ap = [bq_sb, bkm2_sb][proj]
                scale_ap = [1.0, p.c(3)][proj]
                tgt_sb = [qt, kt][proj]
                for ot in range(8):
                    ps = psA if ot % 2 == 0 else psB
                    def t_mm(tensor, ot=ot, ps=ps, w_sb=w_sb, x_sb=x_sb):
                        wge(tensor, LD)
                        wge(tensor, p.s("GLD"))
                        wge(tensor, AC)
                        mm_group(tensor, ps[:],
                                 [(w_sb[:, k * 1024 + ot * 128: k * 1024 + ot * 128 + 128],
                                   x_sb[:, k * 512:(k + 1) * 512]) for k in range(NKT)], MM)
                    block.tensor(t_mm)

                    def s_ev(scalar, ot=ot, ps=ps, b_ap=b_ap, scale_ap=scale_ap, tgt_sb=tgt_sb):
                        wge(scalar, MM)
                        wge(scalar, ST)
                        AC.inc(scalar.activation(tgt_sb[:, ot * 512:(ot + 1) * 512], ps[:],
                                                 AF.Identity, bias=b_ap[:, ot:ot + 1], scale=scale_ap))
                    block.scalar(s_ev)

                    def v_sq(vector, ot=ot, tgt_sb=tgt_sb):
                        wge(vector, AC)
                        EV.inc(vector.tensor_mul(sq[:, ot * 512:(ot + 1) * 512],
                                                 tgt_sb[:, ot * 512:(ot + 1) * 512],
                                                 tgt_sb[:, ot * 512:(ot + 1) * 512]))
                    block.vector(v_sq)

                def sy_st(sync, dst=dst, tgt_sb=tgt_sb, blk=blk):
                    wge(sync, AC)
                    ST.inc(sync.dma_start(
                        out=drview(dst, blk * 512, (blk + 1) * 512),
                        in_=rr8(tgt_sb, 512)), 16)
                block.sync(sy_st)

                psrow = psR if proj == 0 else psR2
                def t_row(tensor, psrow=psrow):
                    wge(tensor, EV)
                    mm_group(tensor, psrow[:],
                             [(onesc[:], sq[:, k * 512:(k + 1) * 512]) for k in range(NKT)], MM)
                block.tensor(t_row)

                rowsb = [rowq, rowk][proj]
                rsc_ap = [1.0, p.c1(5)][proj]
                def s_row(scalar, psrow=psrow, rowsb=rowsb, rsc_ap=rsc_ap):
                    wge(scalar, MM)
                    wge(scalar, ST)
                    AC.inc(scalar.activation(rowsb[:], psrow[:], AF.Identity, scale=rsc_ap))
                block.scalar(s_row)

                aug_dst = d["qaug"][1:2] if proj == 0 else d["kaug"][0:1]
                def sy_row(sync, aug_dst=aug_dst, rowsb=rowsb, blk=blk):
                    wge(sync, AC)
                    ST.inc(sync.dma_start(out=aug_dst[:, blk * 512:(blk + 1) * 512], in_=rowsb[:]), 16)
                block.sync(sy_row)

            for tt in range(4):
                for oc in range(2):
                    def t_v(tensor, tt=tt, oc=oc, x_sb=x_sb):
                        wge(tensor, p.s("GLD"))
                        wge(tensor, EV)
                        mm_group(tensor, psV[:],
                                 [(x_sb[:, k * 512 + tt * 128: k * 512 + tt * 128 + 128],
                                   wv_sb[:, k * 1024 + oc * 512: k * 1024 + oc * 512 + 512])
                                  for k in range(NKT)], MM)
                    block.tensor(t_v)

                    def v_ev(vector, oc=oc):
                        wge(vector, MM)
                        wge(vector, ST)
                        EV.inc(vector.tensor_add(vt[:, oc * 512:(oc + 1) * 512], psV[:],
                                                 bvbc_sb[:, oc * 512:(oc + 1) * 512]))
                    block.vector(v_ev)

                def sy_v(sync, tt=tt, blk=blk):
                    wge(sync, EV)
                    ST.inc(sync.dma_start(
                        out=d["vb"][blk * 512 + tt * 128: blk * 512 + tt * 128 + 128, :],
                        in_=vt[:]), 16)
                block.sync(sy_v)

        def sy_fin(sync):
            wge(sync, ST)
        block.sync(sy_fin)


def emit_s1(p, d):
    nc, block = p.nc, p.block
    p.barrier()
    LD, LD2, LD3, MM, EV, AC, ST = (p.s(x) for x in ["LD", "LD2", "LD3", "MM", "EV", "AC", "ST"])
    es = ExitStack()
    with es:
        wo_sb = es.enter_context(nc.sbuf_tensor("s1_wo", [128, NKT * 1024], BF16))
        bo_sb = es.enter_context(nc.sbuf_tensor("s1_bo", [128, 8], F32))
        onesc = es.enter_context(nc.sbuf_tensor("s1_ones", [128, 1], BF16))
        onesf = es.enter_context(nc.sbuf_tensor("s1_onesf", [1, 128], F32))
        q8 = [es.enter_context(nc.sbuf_tensor(f"s1_q8_{i}", [128, NKT * 512], BF16)) for i in range(2)]
        qa2 = [es.enter_context(nc.sbuf_tensor(f"s1_qa2_{i}", [2, 512], BF16)) for i in range(2)]
        ktile = [es.enter_context(nc.sbuf_tensor(f"s1_kt{i}", [128, NKT * 128], BF16)) for i in range(2)]
        ka2 = [es.enter_context(nc.sbuf_tensor(f"s1_ka2_{i}", [2, 128], BF16)) for i in range(2)]
        fcol = es.enter_context(nc.sbuf_tensor("s1_fcol", [128, 64 * 512], BF16))

        vcol = es.enter_context(nc.sbuf_tensor("s1_vcol", [128, 64 * 128], BF16))
        x_sb = es.enter_context(nc.sbuf_tensor("s1_x", [128, NKT * 512], F32))
        attnT = es.enter_context(nc.sbuf_tensor("s1_attnT", [128, NKT * 512], BF16))
        rden = es.enter_context(nc.sbuf_tensor("s1_rden", [1, 512], F32))
        rb = es.enter_context(nc.sbuf_tensor("s1_rb", [128, 512], F32))
        tb = es.enter_context(nc.sbuf_tensor("s1_tb", [128, NKT * 512], BF16))
        rows = es.enter_context(nc.sbuf_tensor("s1_rows", [1, 5 * 512], F32))
        x1f = es.enter_context(nc.sbuf_tensor("s1_x1f", [128, NKT * 512], F32))
        x1b = es.enter_context(nc.sbuf_tensor("s1_x1b", [128, NKT * 512], BF16))
        psD = [es.enter_context(nc.psum_tensor(f"s1_psD{i}", [128, 512], F32)) for i in range(2)]
        psDen = es.enter_context(nc.psum_tensor("s1_psDen", [1, 512], F32))
        psO = [es.enter_context(nc.psum_tensor(f"s1_psO{i}", [128, 512], F32)) for i in range(2)]
        psBCa = es.enter_context(nc.psum_tensor("s1_psBCa", [128, 512], F32))
        psBCb = es.enter_context(nc.psum_tensor("s1_psBCb", [128, 512], F32))
        psET = es.enter_context(nc.psum_tensor("s1_psET", [128, 512], F32))
        sD = p.s("sD"); sE1 = p.s("sE1"); sE2 = p.s("sE2"); sOV = p.s("sOV")

        ln_bufs = dict(t_f=x_sb, tb=tb, sqb=attnT, rows=rows, onesc=onesc, onesf=onesf,
                       ps_sum=psDen[:], ps_sumsq=psBCa[0:1, :], ps_bcmu=psBCa[:],
                       ps_bcr=psBCb[:], outf=x1f, outb=x1b)

        def gp0(gpsimd):
            p.s("GLD").inc(gpsimd.dma_start(
                out=rr8(wo_sb, 1024),
                in_=d["wo"].rearrange("(k p) o -> p k o", p=128)), 16)
        block.gpsimd(gp0)

        def sy0(sync):
            LD.inc(sync.dma_start(out=bo_sb[:], in_=d["bo_pt"]), 16)
        block.sync(sy0)

        def v0(vector):
            wge(vector, LD)
            EV.inc(vector.memset(onesc[:], 1.0))
            EV.inc(vector.memset(onesf[:], 1.0))
        block.vector(v0)

        def load_chunk(sync, c):
            i = c % 2
            LD2.inc(sync.dma_start(out=rr8(q8[i], 512),
                                   in_=drview(d["qt8"], c * 512, (c + 1) * 512)), 16)
            LD2.inc(sync.dma_start(out=qa2[i][:], in_=d["qaug"][:, c * 512:(c + 1) * 512]), 16)

        block.sync(lambda s: load_chunk(s, 0))

        def load_ktile(sync, kt_i):
            i = kt_i % 2
            LD.inc(sync.dma_start(out=rr8(ktile[i], 128),
                                  in_=drview(d["kts8"], kt_i * 128, (kt_i + 1) * 128)), 16)
            LD.inc(sync.dma_start(out=ka2[i][:], in_=d["kaug"][:, kt_i * 128:(kt_i + 1) * 128]), 16)

        def load_vcol(gpsimd, dt):
            LD3.inc(gpsimd.dma_start(
                out=vcol.rearrange("p (k c) -> p k c", k=64),
                in_=d["vb"].rearrange("(k p) c -> p k c", p=128)[:, :, dt * 128:(dt + 1) * 128]), 16)

        for c in range(16):
            ci = c % 2
            if c + 1 < 16:
                def syn(s, c=c):
                    wge(s, MM)
                    load_chunk(s, c + 1)
                block.sync(syn)

            def syx(s, c=c):
                wge(s, EV)
                wge(s, AC)
                LD2.inc(s.dma_start(out=rr8(x_sb, 512),
                                    in_=drview(d["xT"], c * 512, (c + 1) * 512)), 16)
            block.sync(syx)

            def sy_k0(s):
                wge(s, MM)
                load_ktile(s, 0)
            block.sync(sy_k0)

            for kt_i in range(64):
                kti = kt_i % 2
                if kt_i + 1 < 64:
                    def syk(s, kt_i=kt_i):
                        wge(s, MM)
                        load_ktile(s, kt_i + 1)
                    block.sync(syk)

                ps = psD[kt_i % 2]
                def t_d(tensor, kti=kti, ps=ps, ci=ci):
                    wge(tensor, LD)
                    wge(tensor, LD2)
                    wge(tensor, sE1, lag=1)
                    pairs = [(ktile[kti][:, k * 128:(k + 1) * 128], q8[ci][:, k * 512:(k + 1) * 512])
                             for k in range(NKT)]
                    pairs.append((ka2[kti][:], qa2[ci][:]))
                    mm_group(tensor, ps[:], pairs, MM)
                block.tensor(t_d)

                def s_e(scalar, ps=ps, kt_i=kt_i):
                    wge(scalar, MM)
                    wge(scalar, sE2)
                    sE1.inc(scalar.activation(psET[:], ps[:], AF.Exp, scale=p.c(2)))
                    wge(scalar, sE1)
                    sE2.inc(scalar.activation(fcol[:, kt_i * 512:(kt_i + 1) * 512], psET[:], AF.Exp))
                block.scalar(s_e)

            def t_den(tensor):
                wge(tensor, sE2)
                mm_group(tensor, psDen[:],
                         [(onesc[:], fcol[:, k * 512:(k + 1) * 512]) for k in range(64)], MM)
            block.tensor(t_den)

            def v_den(vector):
                wge(vector, MM)
                EV.inc(vector.reciprocal(rden[:], psDen[:]))
            block.vector(v_den)

            def t_bc(tensor):
                wge(tensor, EV)
                mm_group(tensor, psBCa[:], [(onesf[:], rden[:])], MM)
            block.tensor(t_bc)

            def s_bc(scalar):
                wge(scalar, MM)
                AC.inc(scalar.copy(rb[:], psBCa[:]))
            block.scalar(s_bc)

            def gp_v0(g):
                wge(g, MM)
                load_vcol(g, 0)
            block.gpsimd(gp_v0)
            for dt in range(8):
                ps = psO[dt % 2]
                def t_o(tensor, dt=dt, ps=ps):
                    wge(tensor, LD3)
                    wge(tensor, sOV, lag=1)
                    mm_group(tensor, ps[:],
                             [(vcol[:, k * 128:(k + 1) * 128], fcol[:, k * 512:(k + 1) * 512])
                              for k in range(64)], MM)
                block.tensor(t_o)
                if dt + 1 < 8:
                    def gpv(g, dt=dt):
                        wge(g, MM)
                        load_vcol(g, dt + 1)
                    block.gpsimd(gpv)

                def v_o(vector, dt=dt, ps=ps):
                    wge(vector, MM)
                    wge(vector, AC)
                    sOV.inc(vector.tensor_mul(attnT[:, dt * 512:(dt + 1) * 512], ps[:], rb[:]))
                block.vector(v_o)

            if "attn_dbg" in d:
                def sy_dbg(sync, c=c):
                    wge(sync, EV)
                    ST.inc(sync.dma_start(out=drview(d["attn_dbg"], c * 512, (c + 1) * 512),
                                          in_=rr8(attnT, 512)), 16)
                block.sync(sy_dbg)

            for ot in range(8):
                ps = psD[ot % 2]
                def t_w(tensor, ot=ot, ps=ps):
                    wge(tensor, p.s("GLD"))
                    wge(tensor, EV)
                    wge(tensor, sOV)
                    wge(tensor, sE1)
                    mm_group(tensor, ps[:],
                             [(wo_sb[:, k * 1024 + ot * 128: k * 1024 + ot * 128 + 128],
                               attnT[:, k * 512:(k + 1) * 512]) for k in range(NKT)], MM)
                block.tensor(t_w)

                def s_w(scalar, ot=ot, ps=ps):
                    wge(scalar, MM)
                    wge(scalar, AC)
                    wge(scalar, ST)
                    AC.inc(scalar.activation(x1f[:, ot * 512:(ot + 1) * 512], ps[:],
                                             AF.Identity, bias=bo_sb[:, ot:ot + 1]))
                block.scalar(s_w)

                def v_w(vector, ot=ot):
                    wge(vector, AC)
                    wge(vector, LD2)
                    EV.inc(vector.tensor_add(x_sb[:, ot * 512:(ot + 1) * 512],
                                             x1f[:, ot * 512:(ot + 1) * 512],
                                             x_sb[:, ot * 512:(ot + 1) * 512]))
                block.vector(v_w)

            emit_ln(p, d, ln_bufs, 512, make_bf=True)

            def sy_out(sync, c=c):
                wge(sync, EV)
                wge(sync, AC)
                ST.inc(sync.dma_start(out=drview(d["x1f"], c * 512, (c + 1) * 512),
                                      in_=rr8(x1f, 512)), 16)
                ST.inc(sync.dma_start(out=drview(d["x1b"], c * 512, (c + 1) * 512),
                                      in_=rr8(x1b, 512)), 16)
            block.sync(sy_out)

        def sy_fin(sync):
            wge(sync, ST)
        block.sync(sy_fin)


def wge_val(eng, sem, v):
    if v > 0:
        eng.wait_ge(sem.h, v)


def emit_attn(p, d, li, xin_f, xin_b, xout_f, xout_b):
    nc, block = p.nc, p.block
    p.barrier()
    LD, GLD, ST = p.s("LD"), p.s("GLD"), p.s("ST")
    names = ["aX", "aQK", "aQE", "aRot", "aTr", "aTrE", "aSim", "aExp", "aMsk",
             "aOut", "aRcp", "aNrmE", "aTr2", "aTr2E", "aWo", "aWoE", "aPc",
             "aLnS", "aLnR", "aLnR1", "aLnB", "aLnA"]
    sm = {n: p.s(n) for n in names}
    (aX, aQK, aQE, aRot, aTr, aTrE, aSim, aExp, aMsk, aOut, aRcp, aNrmE, aTr2,
     aTr2E, aWo, aWoE, aPc, aLnS, aLnR, aLnR1, aLnB, aLnA) = (sm[n] for n in names)
    M = [dict((n, 0) for n in names + ["ST"]) for _ in range(NW + 2)]
    loads_after = {}
    qe_after = {}
    exp_after = {}
    msk_after = {}

    def mark(i):
        for n in names:
            M[i + 1][n] = sm[n].n
        M[i + 1]["ST"] = ST.n

    es = ExitStack()
    pre = f"a{li}_"
    with es:
        wqkv_sb = es.enter_context(nc.sbuf_tensor(pre + "wqkv", [128, NKT * 3072], BF16))
        woi_sb = es.enter_context(nc.sbuf_tensor(pre + "woi", [128, NKT * 1024], BF16))
        qs_sb = es.enter_context(nc.sbuf_tensor(pre + "qs", [128, DH], F32))
        ks_sb = es.enter_context(nc.sbuf_tensor(pre + "ks", [128, DH], F32))
        qsw_sb = es.enter_context(nc.sbuf_tensor(pre + "qsw", [128, DH], F32))
        ksw_sb = es.enter_context(nc.sbuf_tensor(pre + "ksw", [128, DH], F32))
        g_sb = es.enter_context(nc.sbuf_tensor(pre + "g", [128, 8], F32))
        lb_sb = es.enter_context(nc.sbuf_tensor(pre + "lb", [128, 8], F32))
        ident = es.enter_context(nc.sbuf_tensor(pre + "id", [128, 128], BF16))
        triA = es.enter_context(nc.sbuf_tensor(pre + "triA", [128, 128], BF16))
        triB = es.enter_context(nc.sbuf_tensor(pre + "triB", [128, 128], BF16))
        tab_t = es.enter_context(nc.sbuf_tensor(pre + "tab", [128, 4 * 2 * DH], F32))
        tab = [tab_t, tab_t]
        x_sb = [es.enter_context(nc.sbuf_tensor(pre + f"x{i}", [128, NKT * 257], F32)) for i in range(2)]
        xb_sb = [es.enter_context(nc.sbuf_tensor(pre + f"xbw{i}", [128, NKT * 257], BF16)) for i in range(2)]
        q_tm = es.enter_context(nc.sbuf_tensor(pre + "qtm", [128, 1024], BF16))
        k_tm = es.enter_context(nc.sbuf_tensor(pre + "ktm", [128, 1024], BF16))
        qn_sb = es.enter_context(nc.sbuf_tensor(pre + "qn", [128, 1024], BF16))
        tmp1 = es.enter_context(nc.sbuf_tensor(pre + "tmp1", [128, 1024], BF16))
        qrot = es.enter_context(nc.sbuf_tensor(pre + "qrot", [128, 1024], BF16))
        krot = es.enter_context(nc.sbuf_tensor(pre + "krot", [128, 1024], BF16))
        ssq = es.enter_context(nc.sbuf_tensor(pre + "ssq", [128, 32], F32))
        rr_ = es.enter_context(nc.sbuf_tensor(pre + "rr", [128, 32], F32))
        csq = es.enter_context(nc.sbuf_tensor(pre + "csq", [128, 4 * DH], F32))
        qT = es.enter_context(nc.sbuf_tensor(pre + "qT", [64, H * 256], BF16))
        kT = [es.enter_context(nc.sbuf_tensor(pre + f"kT{i}", [64, H * 256], BF16)) for i in range(2)]
        vA = [es.enter_context(nc.sbuf_tensor(pre + f"vA{i}", [128, 2 * H * 65], BF16)) for i in range(3)]
        emT = es.enter_context(nc.sbuf_tensor(pre + "emT", [128, H * 768], BF16))
        attnout = es.enter_context(nc.sbuf_tensor(pre + "ao", [128, 2 * 1024], BF16))
        f_sb = es.enter_context(nc.sbuf_tensor(pre + "f", [128, 2 * H], F32))
        aT = es.enter_context(nc.sbuf_tensor(pre + "aT", [128, NKT * 256], BF16))
        xo_t = es.enter_context(nc.sbuf_tensor(pre + "xo", [128, NKT * 258], F32))
        xo = [xo_t, xo_t]
        tb_a = es.enter_context(nc.sbuf_tensor(pre + "tba", [128, NKT * 256], BF16))
        sqb_a = es.enter_context(nc.sbuf_tensor(pre + "sqba", [128, NKT * 256], BF16))
        xob = es.enter_context(nc.sbuf_tensor(pre + "xob", [128, NKT * 256], BF16))
        xn_sb = es.enter_context(nc.sbuf_tensor(pre + "xn", [128, NKT * 256], BF16))
        u_sb = es.enter_context(nc.sbuf_tensor(pre + "u", [128, 256], F32))
        rows = es.enter_context(nc.sbuf_tensor(pre + "rows", [1, 5 * 256], F32))
        onesc = es.enter_context(nc.sbuf_tensor(pre + "onesc", [128, 1], BF16))
        onesf = es.enter_context(nc.sbuf_tensor(pre + "onesf", [1, 128], F32))
        psMM = [es.enter_context(nc.psum_tensor(pre + f"psMM{i}", [128, 512], F32)) for i in range(2)]
        psT = es.enter_context(nc.psum_tensor(pre + "psT", [128, 1024], BF16))
        psS = [es.enter_context(nc.psum_tensor(pre + f"psS{i}", [128, 512], F32)) for i in range(2)]
        psG = [es.enter_context(nc.psum_tensor(pre + f"psG{i}", [128, 512], F32)) for i in range(2)]
        psT2 = es.enter_context(nc.psum_tensor(pre + "psT2", [128, 1024], BF16))
        psTs = [psT, psT2]

        def gp0(gpsimd):
            GLD.inc(gpsimd.dma_start(out=rr8(wqkv_sb, 3072),
                                     in_=d["wqkv"][li].rearrange("(k p) o -> p k o", p=128)), 16)
            GLD.inc(gpsimd.dma_start(out=rr8(woi_sb, 1024),
                                     in_=d["wo_i"][li].rearrange("(k p) o -> p k o", p=128)), 16)
        block.gpsimd(gp0)

        def sy0(sync):
            for dst, src_ in [(qs_sb, d["qs_bc"][li]), (ks_sb, d["ks_bc"][li]),
                              (qsw_sb, d["qs_sw"][li]), (ksw_sb, d["ks_sw"][li]),
                              (g_sb, d["g_pt"][li]), (lb_sb, d["lb_pt"][li]),
                              (ident, d["ident"]), (triA, d["triA"]), (triB, d["triB"])]:
                LD.inc(sync.dma_start(out=dst[:], in_=src_), 16)
        block.sync(sy0)

        def v0(vector):
            wge(vector, LD)
            aPc.inc(vector.memset(onesc[:], 1.0))
            wge(vector, aPc)
            aPc.inc(vector.memset(onesf[:], 1.0))
            for i in range(3):
                wge(vector, aPc)
                aPc.inc(vector.memset(
                    vA[i].rearrange("p (c h e) -> p c h e", c=2, h=H)[:, :, :, 64:65], 1.0))
            wge(vector, aPc)
            aPc.inc(vector.memset(rr8(x_sb[0], 257)[:, :, 0:1], 0.0))
            wge(vector, aPc)
            aPc.inc(vector.memset(rr8(xb_sb[0], 257)[:, :, 0:1], 0.0))
            wge(vector, aPc)
            aPc.inc(vector.memset(rr8(xo[0], 258)[:, :, 0:1], 0.0))
        block.vector(v0)
        init_pc = aPc.n

        def ph_loads(j):
            if j >= NW:
                return

            def sy_x(sync, j=j):
                w0 = j * W
                wge(sync, aQK)
                wge(sync, aWoE)
                wge(sync, aRot)
                wge_val(sync, aPc, init_pc)
                if j == 0:
                    aX.inc(sync.dma_start(out=rr8(x_sb[0], 257)[:, :, 1:257],
                                          in_=drview(xin_f, 0, 256)), 16)
                    aX.inc(sync.dma_start(out=rr8(xb_sb[0], 257)[:, :, 1:257],
                                          in_=drview(xin_b, 0, 256)), 16)
                else:
                    aX.inc(sync.dma_start(out=rr8(x_sb[j % 2], 257),
                                          in_=drview(xin_f, w0 - 1, w0 + 256)), 16)
                    aX.inc(sync.dma_start(out=rr8(xb_sb[j % 2], 257),
                                          in_=drview(xin_b, w0 - 1, w0 + 256)), 16)
                for ti, nm in enumerate(["cs_q", "ss_q", "cs_k", "ss_k"]):
                    aX.inc(sync.dma_start(
                        out=tab[j % 2].rearrange("p (ti tt e) -> p ti tt e", ti=4, tt=2)[:, ti],
                        in_=d[nm].rearrange("(tt p) e -> p tt e", p=128)[:, 2 * j:2 * j + 2, :]), 16)
            block.sync(sy_x)
            loads_after[j] = aX.n

        def ph_qkv(j):
            if j >= NW:
                return
            qlvl = globals().get("_DBG_QLVL", 99)
            xbw = xb_sb[j % 2]
            for tt in range(2):
                def tview(kt, tt=tt, xbw=xbw):
                    off = (1 if kt < 4 else 0) + tt * 128
                    return xbw[:, kt * 257 + off: kt * 257 + off + 128]

                for ch in range(6):
                    ps = psMM[ch % 2]

                    def t_qkv(tensor, ps=ps, ch=ch, tt=tt, j=j):
                        wge(tensor, GLD)
                        wge_val(tensor, aX, loads_after[j])
                        wge(tensor, aWoE)
                        wge(tensor, aQE, lag=1)
                        mm_group(tensor, ps[:],
                                 [(tview(k), wqkv_sb[:, k * 3072 + ch * 512: k * 3072 + ch * 512 + 512])
                                  for k in range(NKT)], aQK)
                    block.tensor(t_qkv)

                    def v_ev(vector, ps=ps, ch=ch, tt=tt, j=j):
                        wge(vector, aQK)
                        if ch < 2:
                            aQE.inc(vector.tensor_copy(q_tm[:, ch * 512:(ch + 1) * 512], ps[:]))
                        elif ch < 4:
                            aQE.inc(vector.tensor_copy(k_tm[:, (ch - 2) * 512:(ch - 1) * 512], ps[:]))
                        else:
                            if j >= 2:
                                wge_val(vector, aOut, M[j - 1]["aOut"])
                            h0 = (ch - 4) * 8
                            aQE.inc(vector.tensor_copy(
                                vA[j % 3].rearrange("p (c h e) -> p c h e", c=2, h=H)[:, tt, h0:h0 + 8, 0:64],
                                ps.rearrange("p (h e) -> p h e", e=64)))
                    block.vector(v_ev)

                if qlvl < 2:
                    continue
                def v_ss(vector, tt=tt, j=j):
                    wge(vector, aQE)
                    aRot.inc(vector.tensor_mul(tmp1[:], q_tm[:], q_tm[:]))
                    wge(vector, aRot)
                    aRot.inc(vector.tensor_reduce(ssq[:, 0:16],
                                                  tmp1.rearrange("p (h e) -> p h e", e=64),
                                                  axis=AX.X, op=ALU.add))
                    wge(vector, aRot)
                    aRot.inc(vector.tensor_mul(tmp1[:], k_tm[:], k_tm[:]))
                    wge(vector, aRot)
                    aRot.inc(vector.tensor_reduce(ssq[:, 16:32],
                                                  tmp1.rearrange("p (h e) -> p h e", e=64),
                                                  axis=AX.X, op=ALU.add))
                    wge_val(vector, aX, loads_after[j])
                    t3 = tab[j % 2].rearrange("p (ti tt e) -> p ti tt e", ti=4, tt=2)
                    c4 = csq.rearrange("p (ti e) -> p ti e", ti=4)
                    aRot.inc(vector.tensor_mul(c4[:, 0], t3[:, 0, tt], qs_sb[:]))
                    aRot.inc(vector.tensor_mul(c4[:, 1], t3[:, 1, tt], qsw_sb[:]))
                    aRot.inc(vector.tensor_mul(c4[:, 2], t3[:, 2, tt], ks_sb[:]))
                    aRot.inc(vector.tensor_mul(c4[:, 3], t3[:, 3, tt], ksw_sb[:]))
                block.vector(v_ss)

                def s_rr(scalar):
                    wge(scalar, aRot)
                    aRcp.inc(scalar.activation(rr_[:], ssq[:], AF.Ln))
                    wge(scalar, aRcp)
                    aRcp.inc(scalar.activation(rr_[:], rr_[:], AF.Exp, scale=p.c(1)))
                block.scalar(s_rr)

                def v_rrmin(vector):
                    wge(vector, aRcp)
                    aRot.inc(vector.tensor_scalar_min(rr_[:], rr_[:], 1.0e12))
                block.vector(v_rrmin)

                for which in (range(2) if qlvl >= 3 else []):
                    src = q_tm if which == 0 else k_tm
                    dst = qrot if which == 0 else krot
                    rr_sl = rr_[:, 0:16] if which == 0 else rr_[:, 16:32]
                    cs_i, ss_i = (0, 1) if which == 0 else (2, 3)

                    def v_rot(vector, src=src, dst=dst, rr_sl=rr_sl, cs_i=cs_i, ss_i=ss_i):
                        wge(vector, aRot)
                        wge(vector, aTr)
                        c4 = csq.rearrange("p (ti e) -> p ti e", ti=4)
                        h3 = lambda t: t.rearrange("p (h e) -> p h e", e=64)
                        h4 = lambda t: t.rearrange("p (h a c) -> p h a c", a=2, c=32)
                        aRot.inc(vector.tensor_mul(
                            h3(qn_sb), h3(src),
                            rr_sl.rearrange("p (h e) -> p h e", e=1).broadcast_to((128, 16, 64))))
                        wge(vector, aRot)
                        aRot.inc(vector.tensor_mul(
                            h3(tmp1), h3(qn_sb),
                            c4[:, cs_i].rearrange("p (o e) -> p o e", o=1).broadcast_to((128, 16, 64))))
                        wge(vector, aRot)
                        ss_t = c4[:, ss_i].rearrange("p (a c) -> p a c", a=2)
                        aRot.inc(vector.tensor_mul(
                            h4(dst)[:, :, 0, :], h4(qn_sb)[:, :, 1, :],
                            ss_t[:, 0:1, :].broadcast_to((128, 16, 32))))
                        wge(vector, aRot)
                        aRot.inc(vector.tensor_mul(
                            h4(dst)[:, :, 1, :], h4(qn_sb)[:, :, 0, :],
                            ss_t[:, 1:2, :].broadcast_to((128, 16, 32))))
                        wge(vector, aRot)
                        aRot.inc(vector.tensor_add(dst[:], dst[:], tmp1[:]))
                    block.vector(v_rot)

                for which in (range(2) if qlvl >= 4 else []):
                    src = qrot if which == 0 else krot
                    dstT = qT if which == 0 else kT[j % 2]
                    for bb in range(4):
                        def t_tr(tensor, src=src, bb=bb, j=j):
                            wge(tensor, aRot)
                            wge(tensor, aSim)
                            wge(tensor, aTr2E)
                            wge(tensor, aTrE, lag=1)
                            inst = None
                            for hh in range(4):
                                h = bb * 4 + hh
                                inst = tensor.transpose(psTs[bb % 2][0:64, hh * 128:(hh + 1) * 128],
                                                        src[:, h * 64:(h + 1) * 64], ident[:])
                            aTr.inc(inst)
                        block.tensor(t_tr)

                        def v_trev(vector, dstT=dstT, bb=bb, tt=tt):
                            wge(vector, aTr)
                            aTrE.inc(vector.tensor_copy(
                                dstT[0:64, tt * 2048 + bb * 512: tt * 2048 + (bb + 1) * 512],
                                psTs[bb % 2][0:64, 0:512]))
                        block.vector(v_trev)
            qe_after[j] = aQE.n

        def ph_sims(j):
            for h in range(H):
                def t_sim(tensor, h=h, j=j):
                    wge(tensor, aTrE)
                    wge(tensor, aExp, lag=(2 if j > 0 else 1))
                    if h < 2:
                        wge(tensor, aLnR1)
                        wge(tensor, aNrmE)
                        wge(tensor, aLnA)
                    qh_full = qT.rearrange("p (t h c) -> p t h c", t=2, c=128)[:, :, h, :]
                    qh0 = qT[0:64, h * 128: h * 128 + 128]
                    qh1 = qT[0:64, 2048 + h * 128: 2048 + h * 128 + 128]
                    pw_ = 1 - (j % 2)
                    cw_ = j % 2
                    if j > 0:
                        tensor.matmul(psS[h % 2][:, 0:128], kT[pw_][0:64, h * 128: h * 128 + 128],
                                      qh0, start=True, stop=True)
                        tensor.matmul(psS[h % 2][:, 128:384], kT[pw_][0:64, 2048 + h * 128: 2048 + h * 128 + 128],
                                      qh_full, start=True, stop=True)
                    tensor.matmul(psG[h % 2][:, 0:256], kT[cw_][0:64, h * 128: h * 128 + 128],
                                  qh_full, start=True, stop=True)
                    aSim.inc(tensor.matmul(psG[h % 2][:, 256:384], kT[cw_][0:64, 2048 + h * 128: 2048 + h * 128 + 128],
                                           qh1, start=True, stop=True))
                block.tensor(t_sim)

                def s_exp(scalar, h=h, j=j):
                    wge(scalar, aSim)
                    wge_val(scalar, aOut, M[j]["aOut"])
                    base = h * 768
                    if j > 0:
                        aExp.inc(scalar.activation(emT[:, base: base + 384], psS[h % 2][:, 0:384], AF.Exp))
                        wge(scalar, aExp)
                    aExp.inc(scalar.activation(emT[:, base + 384: base + 768], psG[h % 2][:, 0:384], AF.Exp))
                block.scalar(s_exp)

                def g_mask(vector, h=h, j=j):
                    wge(vector, aExp)
                    base = h * 768
                    e6 = emT.rearrange("p (s c) -> p s c", c=128)
                    b6 = base // 128
                    if j > 0:
                        vector.tensor_mul(
                            e6[:, b6:b6 + 3:2, :], e6[:, b6:b6 + 3:2, :],
                            triA.rearrange("p (o c) -> p o c", o=1).broadcast_to((128, 2, 128)))
                        vector.wait_ge(aMsk.h, aMsk.n)
                    aMsk.inc(vector.tensor_mul(
                        e6[:, b6 + 3:b6 + 6:2, :], e6[:, b6 + 3:b6 + 6:2, :],
                        triB.rearrange("p (o c) -> p o c", o=1).broadcast_to((128, 2, 128))))
                block.vector(g_mask)
            exp_after[j] = aExp.n
            msk_after[j] = aMsk.n

        def ph_outs(j):
            for h in range(H):
                for qt in range(2):
                    ps_att = psS[h % 2] if qt == 0 else psG[h % 2]

                    def t_out(tensor, h=h, qt=qt, ps_att=ps_att, j=j):
                        if h < 2:
                            wge(tensor, aLnA)
                        wge_val(tensor, aMsk, msk_after[j])
                        wge_val(tensor, aExp, exp_after[j])
                        wge_val(tensor, aQE, qe_after.get(j, aQE.n))
                        wge(tensor, aNrmE, lag=2)
                        base = h * 768
                        pairs = []
                        va_c = vA[j % 3].rearrange("p (c h e) -> p c h e", c=2, h=H)
                        va_p = vA[(j - 1) % 3].rearrange("p (c h e) -> p c h e", c=2, h=H)
                        if qt == 0:
                            if j > 0:
                                pairs.append((emT[:, base + 0: base + 128], va_p[:, 0, h, :]))
                                pairs.append((emT[:, base + 128: base + 256], va_p[:, 1, h, :]))
                            pairs.append((emT[:, base + 384: base + 512], va_c[:, 0, h, :]))
                        else:
                            if j > 0:
                                pairs.append((emT[:, base + 256: base + 384], va_p[:, 1, h, :]))
                            pairs.append((emT[:, base + 512: base + 640], va_c[:, 0, h, :]))
                            pairs.append((emT[:, base + 640: base + 768], va_c[:, 1, h, :]))
                        mm_group(tensor, ps_att[:, 384:449], pairs, aOut)
                    block.tensor(t_out)

                def v_rcp(vector, h=h):
                    wge(vector, aOut)
                    wge(vector, aNrmE)
                    aRcp.inc(vector.reciprocal(f_sb[:, 2 * h:2 * h + 1], psS[h % 2][:, 448:449]))
                    wge(vector, aRcp)
                    aRcp.inc(vector.reciprocal(f_sb[:, 2 * h + 1:2 * h + 2], psG[h % 2][:, 448:449]))
                block.vector(v_rcp)

                def s_nrm(vector, h=h):
                    wge(vector, aRcp)
                    wge(vector, aTr2)
                    ao = attnout.rearrange("p (q h e) -> p q h e", q=2, e=64)
                    aNrmE.inc(vector.tensor_scalar_mul(ao[:, 0, h, :], psS[h % 2][:, 384:448],
                                                       f_sb[:, 2 * h:2 * h + 1]))
                    wge(vector, aNrmE)
                    aNrmE.inc(vector.tensor_scalar_mul(ao[:, 1, h, :], psG[h % 2][:, 384:448],
                                                       f_sb[:, 2 * h + 1:2 * h + 2]))
                block.vector(s_nrm)

        def ph_tr2(j):
            for qt in range(2):
                for bb in range(2):
                    g2 = qt * 2 + bb
                    def t_tr2(tensor, qt=qt, bb=bb, g2=g2):
                        wge(tensor, aNrmE)
                        wge(tensor, aTrE)
                        wge(tensor, aTr2E, lag=1)
                        inst = None
                        for cc in range(4):
                            ct = bb * 4 + cc
                            inst = tensor.transpose(psTs[g2 % 2][:, cc * 128:(cc + 1) * 128],
                                                    attnout[:, qt * 1024 + ct * 128: qt * 1024 + ct * 128 + 128],
                                                    ident[:])
                        aTr2.inc(inst)
                    block.tensor(t_tr2)

                    def v_tr2e(vector, qt=qt, bb=bb, g2=g2):
                        wge(vector, aTr2)
                        aTr2E.inc(vector.tensor_copy(
                            aT[:, qt * 1024 + bb * 512: qt * 1024 + (bb + 1) * 512],
                            psTs[g2 % 2][:, 0:512]))
                    block.vector(v_tr2e)

        def ph_wo(j):
            cw_ = j % 2

            def v_pc(vector, j=j, cw_=cw_):
                if j > 0:
                    wge(vector, aWoE)
                    wge(vector, aLnS)
                    aPc.inc(vector.tensor_copy(rr8(xo[cw_], 258)[:, :, 0:1],
                                               rr8(xo[cw_], 258)[:, :, 256:257]))
                else:
                    aPc.inc(vector.engine_nop())
            block.vector(v_pc)

            for ot in range(8):
                ps = psMM[ot % 2]

                def t_wo(tensor, ot=ot, ps=ps, j=j):
                    wge(tensor, aTr2E)
                    wge_val(tensor, aQE, qe_after.get(j + 1, aQE.n))
                    wge(tensor, aWoE, lag=1)
                    aT4 = aT.rearrange("p (q c e) -> p q c e", q=2, e=128)
                    mm_group(tensor, ps[:, 0:256],
                             [(woi_sb[:, k * 1024 + ot * 128: k * 1024 + ot * 128 + 128],
                               aT4[:, :, k, :]) for k in range(NKT)], aWo)
                block.tensor(t_wo)

                def v_wo(vector, ot=ot, ps=ps, j=j, cw_=cw_):
                    wge(vector, aWo)
                    wge(vector, aPc)
                    wge_val(vector, ST, M[j]["ST"])
                    wge(vector, aLnA)
                    aWoE.inc(vector.tensor_add(xo[cw_][:, ot * 258 + 1: ot * 258 + 257],
                                               ps[:, 0:256],
                                               x_sb[j % 2][:, ot * 257 + 1: ot * 257 + 257]))
                block.vector(v_wo)

        def ph_ln(j):
            cw_ = j % 2
            T = 256

            def v_casts(vector, j=j, cw_=cw_):
                wge(vector, aWoE)
                wge_val(vector, ST, M[j]["ST"])
                aLnS.inc(vector.tensor_copy(rr8(tb_a, 256)[:, 0:4, :],
                                            rr8(xo[cw_], 258)[:, 0:4, 1:257]))
                wge(vector, aLnS)
                aLnS.inc(vector.tensor_copy(rr8(tb_a, 256)[:, 4:8, :],
                                            rr8(xo[cw_], 258)[:, 4:8, 0:256]))
                wge(vector, aLnS)
                aLnS.inc(vector.tensor_copy(rr8(xob, 256), rr8(xo[cw_], 258)[:, :, 1:257]))
                wge(vector, aLnS)
                aLnS.inc(vector.tensor_mul(sqb_a[:], tb_a[:], tb_a[:]))
            block.vector(v_casts)

            def t_lnrows(tensor, j=j):
                wge(tensor, aLnS)
                wge_val(tensor, aExp, exp_after[j])
                wge(tensor, aNrmE)
                wge(tensor, aRcp)
                mm_group(tensor, psS[0][0:1, 0:256],
                         [(onesc[:], tb_a[:, k * 256:(k + 1) * 256]) for k in range(NKT)], aLnR)
                mm_group(tensor, psS[1][0:1, 0:256],
                         [(onesc[:], sqb_a[:, k * 256:(k + 1) * 256]) for k in range(NKT)], aLnR)
            block.tensor(t_lnrows)

            def s_lr(scalar):
                wge(scalar, aLnR)
                aLnR1.inc(scalar.activation(rows[:, 0:T], psS[0][0:1, 0:256], AF.Identity, scale=p.c1(4)))
                wge(scalar, aLnR1)
                aLnR1.inc(scalar.activation(rows[:, T:2 * T], psS[1][0:1, 0:256], AF.Identity, scale=p.c1(4)))
            block.scalar(s_lr)

            def v_lr(vector):
                wge(vector, aLnR1)
                aLnB.inc(vector.tensor_mul(rows[:, 2 * T:3 * T], rows[:, 0:T], rows[:, 0:T]))
                wge(vector, aLnB)
                aLnB.inc(vector.tensor_sub(rows[:, 2 * T:3 * T], rows[:, T:2 * T], rows[:, 2 * T:3 * T]))
            block.vector(v_lr)

            def s_lr2(scalar):
                wge(scalar, aLnB)
                aLnB.inc(scalar.activation(rows[:, 3 * T:4 * T], rows[:, 2 * T:3 * T], AF.Ln, bias=p.c1(0)))
                wge(scalar, aLnB)
                aLnB.inc(scalar.activation(rows[:, 3 * T:4 * T], rows[:, 3 * T:4 * T], AF.Exp, scale=p.c1(1)))
                wge(scalar, aLnB)
                aLnB.inc(scalar.activation(rows[:, 4 * T:5 * T], rows[:, 0:T], AF.Identity, scale=p.c1(2)))
            block.scalar(s_lr2)

            def t_lnbc(tensor, j=j):
                wge(tensor, aLnB)
                wge_val(tensor, aLnA, M[j]["aLnA"])
                mm_group(tensor, psS[0][:, 256:512], [(onesf[:], rows[:, 4 * T:5 * T])], aLnB)
                mm_group(tensor, psS[1][:, 256:512], [(onesf[:], rows[:, 3 * T:4 * T])], aLnB)
            block.tensor(t_lnbc)

            for kk in range(NKT):
                def v_lap(vector, kk=kk, cw_=cw_, j=j):
                    wge(vector, aLnB)
                    wge(vector, aLnA)
                    if kk == 0:
                        wge_val(vector, ST, M[j]["ST"])
                    off = (1 if kk < 4 else 0)
                    src = xo[cw_][:, kk * 258 + off: kk * 258 + off + 256]
                    aLnA.inc(vector.tensor_add(u_sb[:], src, psS[0][:, 256:512]))
                    wge(vector, aLnA)
                    aLnA.inc(vector.tensor_mul(u_sb[:], u_sb[:], psS[1][:, 256:512]))
                    wge(vector, aLnA)
                    aLnA.inc(vector.tensor_scalar(xn_sb[:, kk * 256:(kk + 1) * 256], u_sb[:],
                                                  g_sb[:, kk:kk + 1], lb_sb[:, kk:kk + 1],
                                                  op0=ALU.mult, op1=ALU.add))
                block.vector(v_lap)

            def sy_st(sync, j=j, cw_=cw_):
                w0 = j * W
                wge(sync, aLnA)
                wge(sync, aLnS)
                ST.inc(sync.dma_start(out=drview(xout_f, w0, w0 + 256),
                                      in_=rr8(xo[cw_], 258)[:, :, 1:257]), 16)
                ST.inc(sync.dma_start(out=drview(xout_b, w0, w0 + 256), in_=rr8(xob, 256)), 16)
                ST.inc(sync.dma_start(out=drview(d["xnb"], w0, w0 + 256), in_=rr8(xn_sb, 256)), 16)
            block.sync(sy_st)

        nwin = globals().get("_DBG_NWIN", NW)
        nph = globals().get("_DBG_NPH", 99)
        if nph >= -1:
            ph_loads(0)
        for w in range(nwin):
            if w == 0 and nph >= 0:
                ph_qkv(0)
                ph_loads(1)
            if nph >= 1: ph_sims(w)
            if nph >= 2: ph_qkv(w + 1)
            if nph >= 3: ph_outs(w)
            if nph >= 4: ph_tr2(w)
            if nph >= 5: ph_wo(w)
            if nph >= 6: ph_loads(w + 2)
            if nph >= 7: ph_ln(w)
            mark(w)

        def sy_fin(sync):
            wge(sync, ST)
        block.sync(sy_fin)


def emit_f1(p, d, li):
    nc, block = p.nc, p.block
    p.barrier()
    LD, LD2, MM, EV, AC, ST = (p.s(x) for x in ["LD", "LD2", "MM", "EV", "AC", "ST"])
    es = ExitStack()
    pre = f"f1_{li}_"
    with es:
        w1_sb = es.enter_context(nc.sbuf_tensor(pre + "w1", [128, NKT * DI2], BF16))
        b1a_sb = es.enter_context(nc.sbuf_tensor(pre + "b1a", [128, NAT], F32))
        b1g_sb = es.enter_context(nc.sbuf_tensor(pre + "b1g", [128, NAT], F32))
        xn = [es.enter_context(nc.sbuf_tensor(pre + f"xn{i}", [128, NKT * 512], BF16)) for i in range(2)]
        gg = es.enter_context(nc.sbuf_tensor(pre + "gg", [128, 512], F32))
        h_sb = [es.enter_context(nc.sbuf_tensor(pre + f"h{i}", [128, 512], BF16)) for i in range(2)]
        psa = [es.enter_context(nc.psum_tensor(pre + f"psa{i}", [128, 512], F32)) for i in range(2)]
        psg = [es.enter_context(nc.psum_tensor(pre + f"psg{i}", [128, 512], F32)) for i in range(2)]

        def gp0(gpsimd):
            p.s("GLD").inc(gpsimd.dma_start(out=rr8(w1_sb, DI2),
                                    in_=d["w1"][li].rearrange("(k p) o -> p k o", p=128)), 16)
        block.gpsimd(gp0)

        def sy0(sync):
            LD.inc(sync.dma_start(out=b1a_sb[:], in_=d["b1a_pt"][li]), 16)
            LD.inc(sync.dma_start(out=b1g_sb[:], in_=d["b1g_pt"][li]), 16)
        block.sync(sy0)

        def load_blk(sync, blk):
            LD2.inc(sync.dma_start(out=rr8(xn[blk % 2], 512),
                                   in_=drview(d["xnb"], blk * 512, (blk + 1) * 512)), 16)

        block.sync(lambda s: load_blk(s, 0))

        for blk in range(16):
            xs = xn[blk % 2]
            if blk + 1 < 16:
                def syn(s, blk=blk):
                    wge(s, MM)
                    load_blk(s, blk + 1)
                block.sync(syn)

            for at in range(NAT):
                aoff, asz = ATILES[at]
                pa = psa[at % 2]
                pg = psg[at % 2]
                def t_a(tensor, aoff=aoff, asz=asz, pa=pa, pg=pg, xs=xs):
                    wge(tensor, LD)
                    wge(tensor, p.s("GLD"))
                    wge(tensor, LD2)
                    wge(tensor, EV, lag=1)
                    wge(tensor, AC, lag=1)
                    mm_group(tensor, pa[0:asz, :],
                             [(w1_sb[:, k * DI2 + aoff: k * DI2 + aoff + asz],
                               xs[:, k * 512:(k + 1) * 512]) for k in range(NKT)], MM)
                    mm_group(tensor, pg[0:asz, :],
                             [(w1_sb[:, k * DI2 + DI + aoff: k * DI2 + DI + aoff + asz],
                               xs[:, k * 512:(k + 1) * 512]) for k in range(NKT)], MM)
                block.tensor(t_a)

                def s_g(scalar, at=at, asz=asz, pg=pg):
                    wge(scalar, MM)
                    wge(scalar, EV)
                    AC.inc(scalar.activation(gg[0:asz, :], pg[0:asz, :], AF.Gelu,
                                             bias=b1g_sb[0:asz, at:at + 1]))
                block.scalar(s_g)

                hb_ = h_sb[at % 2]
                def v_h(vector, at=at, asz=asz, pa=pa, hb_=hb_):
                    wge(vector, AC)
                    wge(vector, ST)
                    EV.inc(vector.scalar_tensor_tensor(hb_[0:asz, :], pa[0:asz, :],
                                                       b1a_sb[0:asz, at:at + 1], gg[0:asz, :],
                                                       op0=ALU.add, op1=ALU.mult))
                block.vector(v_h)

                def sy_h(sync, at=at, asz=asz, blk=blk, hb_=hb_):
                    wge(sync, EV)
                    ST.inc(sync.dma_start(
                        out=d["hb"][at * 128: at * 128 + asz, blk * 512:(blk + 1) * 512],
                        in_=hb_[0:asz, :]), 16)
                block.sync(sy_h)

        def sy_fin(sync):
            wge(sync, ST)
        block.sync(sy_fin)


def emit_f2(p, d, li, xres_f, xout_f, xout_b):
    nc, block = p.nc, p.block
    p.barrier()
    LD, LD2, MM, EV, AC, ST = (p.s(x) for x in ["LD", "LD2", "MM", "EV", "AC", "ST"])
    es = ExitStack()
    pre = f"f2_{li}_"
    TB = 256
    with es:
        w2_sb = es.enter_context(nc.sbuf_tensor(pre + "w2", [128, NAT * 1024], BF16))
        b2_sb = es.enter_context(nc.sbuf_tensor(pre + "b2", [128, 8], F32))
        hs = [es.enter_context(nc.sbuf_tensor(pre + f"hs{i}", [128, NAT * TB], BF16)) for i in range(2)]
        xr = [es.enter_context(nc.sbuf_tensor(pre + f"xr{i}", [128, NKT * TB], F32)) for i in range(2)]
        xof = es.enter_context(nc.sbuf_tensor(pre + "xof", [128, NKT * TB], F32))
        xob = es.enter_context(nc.sbuf_tensor(pre + "xob", [128, NKT * TB], BF16))
        ps2 = [es.enter_context(nc.psum_tensor(pre + f"ps{i}", [128, TB], F32)) for i in range(2)]

        def gp0(gpsimd):
            p.s("GLD").inc(gpsimd.dma_start(
                out=w2_sb.rearrange("p (k o) -> p k o", k=NAT)[:, 0:21, :],
                in_=d["w2"][li][0:2688].rearrange("(k p) o -> p k o", p=128)), 16)
            p.s("GLD").inc(gpsimd.dma_start(
                out=w2_sb[0:42, 21 * 1024: 22 * 1024],
                in_=d["w2"][li][2688:2730, :]), 16)
        block.gpsimd(gp0)

        def sy0(sync):
            LD.inc(sync.dma_start(out=b2_sb[:], in_=d["b2_pt"][li]), 16)
        block.sync(sy0)

        def load_blk(sync, gpsimd_unused, blk):
            i = blk % 2
            LD2.inc(sync.dma_start(
                out=hs[i].rearrange("p (k t) -> p k t", k=NAT),
                in_=d["hb"].rearrange("(k p) t -> p k t", p=128)[:, :, blk * TB:(blk + 1) * TB]), 16)
            LD2.inc(sync.dma_start(out=rr8(xr[i], TB),
                                   in_=drview(xres_f, blk * TB, (blk + 1) * TB)), 16)

        block.sync(lambda s: load_blk(s, None, 0))

        for blk in range(32):
            i = blk % 2
            if blk + 1 < 32:
                def syn(s, blk=blk):
                    wge(s, MM)
                    wge(s, EV)
                    load_blk(s, None, blk + 1)
                block.sync(syn)

            for ot in range(8):
                ps = ps2[ot % 2]
                def t_m(tensor, ot=ot, ps=ps, i=i):
                    wge(tensor, LD)
                    wge(tensor, p.s("GLD"))
                    wge(tensor, LD2)
                    wge(tensor, EV, lag=1)
                    pairs = []
                    for at in range(NAT):
                        asz = ATILES[at][1]
                        pairs.append((w2_sb[0:asz, at * 1024 + ot * 128: at * 1024 + ot * 128 + 128],
                                      hs[i][0:asz, at * TB:(at + 1) * TB]))
                    mm_group(tensor, ps[:], pairs, MM)
                block.tensor(t_m)

                def v_o(vector, ot=ot, ps=ps, i=i):
                    wge(vector, MM)
                    wge(vector, ST)
                    EV.inc(vector.scalar_tensor_tensor(xof[:, ot * TB:(ot + 1) * TB], ps[:],
                                                       b2_sb[:, ot:ot + 1],
                                                       xr[i][:, ot * TB:(ot + 1) * TB],
                                                       op0=ALU.add, op1=ALU.add))
                block.vector(v_o)

                def s_b(scalar, ot=ot):
                    wge(scalar, EV)
                    AC.inc(scalar.copy(xob[:, ot * TB:(ot + 1) * TB], xof[:, ot * TB:(ot + 1) * TB]))
                block.scalar(s_b)

            def sy_st(sync, blk=blk):
                wge(sync, EV)
                wge(sync, AC)
                ST.inc(sync.dma_start(out=drview(xout_f, blk * TB, (blk + 1) * TB),
                                      in_=rr8(xof, TB)), 16)
                ST.inc(sync.dma_start(out=drview(xout_b, blk * TB, (blk + 1) * TB),
                                      in_=rr8(xob, TB)), 16)
            block.sync(sy_st)

        def sy_fin(sync):
            wge(sync, ST)
        block.sync(sy_fin)


def emit_fin(p, d, xlast_f):
    nc, block = p.nc, p.block
    p.barrier()
    LD, MM, EV, AC, ST = (p.s(x) for x in ["LD", "MM", "EV", "AC", "ST"])
    es = ExitStack()
    with es:
        onesc = es.enter_context(nc.sbuf_tensor("fin_ones", [128, 1], BF16))
        onesf = es.enter_context(nc.sbuf_tensor("fin_onesf", [1, 128], F32))
        a_sb = es.enter_context(nc.sbuf_tensor("fin_a", [128, NKT * 512], F32))
        t_f = es.enter_context(nc.sbuf_tensor("fin_t", [128, NKT * 512], F32))
        tb = es.enter_context(nc.sbuf_tensor("fin_tb", [128, NKT * 512], BF16))
        sqb = es.enter_context(nc.sbuf_tensor("fin_sqb", [128, NKT * 512], BF16))
        rows = es.enter_context(nc.sbuf_tensor("fin_rows", [1, 5 * 512], F32))
        outf = es.enter_context(nc.sbuf_tensor("fin_outf", [128, NKT * 512], F32))
        psR1 = es.enter_context(nc.psum_tensor("fin_psR1", [1, 512], F32))
        psBCa = es.enter_context(nc.psum_tensor("fin_psBCa", [128, 512], F32))
        psBCb = es.enter_context(nc.psum_tensor("fin_psBCb", [128, 512], F32))

        def v0(vector):
            EV.inc(vector.memset(onesc[:], 1.0))
            EV.inc(vector.memset(onesf[:], 1.0))
        block.vector(v0)

        bufs = dict(t_f=t_f, tb=tb, sqb=sqb, rows=rows, onesc=onesc, onesf=onesf,
                    ps_sum=psR1[:], ps_sumsq=psBCa[0:1, :], ps_bcmu=psBCa[:],
                    ps_bcr=psBCb[:], outf=outf, outb=tb)

        for c in range(16):
            def sy_l(sync, c=c):
                wge(sync, EV)
                wge(sync, AC)
                wge(sync, ST)
                LD.inc(sync.dma_start(out=rr8(t_f, 512),
                                      in_=drview(d["x1f"], c * 512, (c + 1) * 512)), 16)
                LD.inc(sync.dma_start(out=rr8(a_sb, 512),
                                      in_=drview(xlast_f, c * 512, (c + 1) * 512)), 16)
            block.sync(sy_l)

            def v_add(vector):
                wge(vector, LD)
                EV.inc(vector.tensor_add(t_f[:], t_f[:], a_sb[:]))
            block.vector(v_add)

            emit_ln(p, d, bufs, 512, make_bf=False)

            def sy_st(sync, c=c):
                wge(sync, EV)
                ST.inc(sync.dma_start(out=drview(d["outT"], c * 512, (c + 1) * 512),
                                      in_=rr8(outf, 512)), 16)
            block.sync(sy_st)

        def sy_fin(sync):
            wge(sync, ST)
        block.sync(sy_fin)


def build_program(n_layers=NL, outer_only=False, dbg=(), attn_only=False):
    nc = bass.Bass("TRN2", target_bir_lowering=False)
    d = {}
    def scratch(name, shape, dtype):
        kind = "ExternalOutput" if name in dbg else "Internal"
        return nc.dram_tensor(name, shape, dtype, kind=kind).ap()
    d["xT"] = nc.dram_tensor("xT", [D, S], F32, kind="ExternalInput").ap()
    for nm in ["wq", "wk", "wv", "wo"]:
        d[nm] = nc.dram_tensor(nm, [D, D], F32, kind="ExternalInput").ap()
    for nm in ["bq_pt", "bk_pt", "bo_pt"]:
        d[nm] = nc.dram_tensor(nm, [128, 8], F32, kind="ExternalInput").ap()
    d["bv_bc"] = nc.dram_tensor("bv_bc", [128, D], F32, kind="ExternalInput").ap()
    d["wqkv"] = nc.dram_tensor("wqkv", [NL, D, 3072], F32, kind="ExternalInput").ap()
    d["wo_i"] = nc.dram_tensor("wo_i", [NL, D, D], F32, kind="ExternalInput").ap()
    d["w1"] = nc.dram_tensor("w1", [NL, D, DI2], F32, kind="ExternalInput").ap()
    d["w2"] = nc.dram_tensor("w2", [NL, DI, D], F32, kind="ExternalInput").ap()
    d["qs_bc"] = nc.dram_tensor("qs_bc", [NL, 128, DH], F32, kind="ExternalInput").ap()
    d["ks_bc"] = nc.dram_tensor("ks_bc", [NL, 128, DH], F32, kind="ExternalInput").ap()
    d["qs_sw"] = nc.dram_tensor("qs_sw", [NL, 128, DH], F32, kind="ExternalInput").ap()
    d["ks_sw"] = nc.dram_tensor("ks_sw", [NL, 128, DH], F32, kind="ExternalInput").ap()
    d["g_pt"] = nc.dram_tensor("g_pt", [NL, 128, 8], F32, kind="ExternalInput").ap()
    d["lb_pt"] = nc.dram_tensor("lb_pt", [NL, 128, 8], F32, kind="ExternalInput").ap()
    d["b1a_pt"] = nc.dram_tensor("b1a_pt", [NL, 128, NAT], F32, kind="ExternalInput").ap()
    d["b1g_pt"] = nc.dram_tensor("b1g_pt", [NL, 128, NAT], F32, kind="ExternalInput").ap()
    d["b2_pt"] = nc.dram_tensor("b2_pt", [NL, 128, 8], F32, kind="ExternalInput").ap()
    for nm in ["cs_q", "ss_q", "cs_k", "ss_k"]:
        d[nm] = nc.dram_tensor(nm, [S, DH], F32, kind="ExternalInput").ap()
    for nm in ["triA", "triB", "ident"]:
        d[nm] = nc.dram_tensor(nm, [128, 128], BF16, kind="ExternalInput").ap()
    d["qt8"] = scratch("qt8", [D, S], BF16)
    d["kts8"] = scratch("kts8", [D, S], BF16)
    d["qaug"] = scratch("qaug", [2, S], BF16)
    d["kaug"] = scratch("kaug", [2, S], BF16)
    d["vb"] = scratch("vb", [S, D], BF16)
    d["x1f"] = scratch("x1f", [D, S], F32)
    d["x1b"] = scratch("x1b", [D, S], BF16)
    d["xf"] = [scratch(f"xf{i}", [D, S], F32) for i in range(2)]
    d["xb"] = [scratch(f"xb{i}", [D, S], BF16) for i in range(2)]
    d["xnb"] = scratch("xnb", [D, S], BF16)
    d["hb"] = scratch("hb", [NAT * 128, S], BF16)
    d["outT"] = nc.dram_tensor("outT", [D, S], F32, kind="ExternalOutput").ap()

    if "attn_dbg" in dbg:
        d["attn_dbg"] = nc.dram_tensor("attn_dbg", [D, S], BF16, kind="ExternalOutput").ap()
    if "aT_dbg" in dbg:
        d["aT_dbg"] = nc.dram_tensor("aT_dbg", [D, S], BF16, kind="ExternalOutput").ap()
    d["cvals"] = nc.dram_tensor("cvals", [128, 8], F32, kind="ExternalInput").ap()
    es = ExitStack()
    with es:
        block = es.enter_context(nc.Block())
        p = Prog(nc, es, block)
        p.cv = es.enter_context(nc.sbuf_tensor("cvals_sb", [128, 8], F32))

        def _cv_load(sync):
            p.s("LD").inc(sync.dma_start(out=p.cv[:], in_=d["cvals"]), 16)
        block.sync(_cv_load)
        emit_s0(p, d)
        emit_s1(p, d)
        if outer_only:
            emit_fin(p, d, d["x1f"])
            return nc
        cur_f, cur_b = d["x1f"], d["x1b"]
        for li in range(n_layers):
            nxt_f, nxt_b = d["xf"][li % 2], d["xb"][li % 2]
            emit_attn(p, d, li, cur_f, cur_b, nxt_f, nxt_b)
            if attn_only:
                return nc
            emit_f1(p, d, li)
            emit_f2(p, d, li, nxt_f, nxt_f, nxt_b)
            cur_f, cur_b = nxt_f, nxt_b
        emit_fin(p, d, cur_f)
    return nc


_CONSTS = None


def _inputs_for_core(b, args, consts):
    (x, wq, bq, wk, bk, wv, bv, wo, bo, wqkv, qs, ks, wo_i, g, lb, w1, b1, w2, b2) = args
    m = {
        "xT": np.ascontiguousarray(x[b].T),
        "wq": wq, "wk": wk, "wv": wv, "wo": wo,
        "bq_pt": _col(bq), "bk_pt": _col(bk), "bo_pt": _col(bo),
        "bv_bc": np.ascontiguousarray(np.broadcast_to(bv, (128, D))),
        "wqkv": wqkv, "wo_i": wo_i, "w1": w1, "w2": w2,
        "qs_bc": np.ascontiguousarray(np.broadcast_to(qs[:, None, :], (NL, 128, DH))),
        "ks_bc": np.ascontiguousarray(np.broadcast_to(ks[:, None, :], (NL, 128, DH))),
        "qs_sw": np.ascontiguousarray(np.broadcast_to(np.roll(qs, -32, -1)[:, None, :], (NL, 128, DH))),
        "ks_sw": np.ascontiguousarray(np.broadcast_to(np.roll(ks, -32, -1)[:, None, :], (NL, 128, DH))),
        "g_pt": np.stack([_col(g[i]) for i in range(NL)]),
        "lb_pt": np.stack([_col(lb[i]) for i in range(NL)]),
        "b1a_pt": np.stack([_colpad(b1[i, :DI], NAT) for i in range(NL)]),
        "b1g_pt": np.stack([_colpad(b1[i, DI:], NAT) for i in range(NL)]),
        "b2_pt": np.stack([_col(b2[i]) for i in range(NL)]),
    }
    m.update(consts)
    return m


def kernel(L0_attn_x, L0_attn_Wq, L0_attn_bq, L0_attn_Wk, L0_attn_bk, L0_attn_Wv,
           L0_attn_bv, L0_attn_Wo, L0_attn_bo, L0_ffn_Wqkv, L0_ffn_q_scale,
           L0_ffn_k_scale, L0_ffn_Wo, L0_ffn_ln_g, L0_ffn_ln_b, L0_ffn_W1,
           L0_ffn_b1, L0_ffn_W2, L0_ffn_b2):
    global _CONSTS
    if _CONSTS is None:
        _CONSTS = _build_consts()
    args = tuple(np.ascontiguousarray(np.asarray(a, dtype=np.float32)) for a in (
        L0_attn_x, L0_attn_Wq, L0_attn_bq, L0_attn_Wk, L0_attn_bk, L0_attn_Wv,
        L0_attn_bv, L0_attn_Wo, L0_attn_bo, L0_ffn_Wqkv, L0_ffn_q_scale,
        L0_ffn_k_scale, L0_ffn_Wo, L0_ffn_ln_g, L0_ffn_ln_b, L0_ffn_W1,
        L0_ffn_b1, L0_ffn_W2, L0_ffn_b2))
    nc = build_program()
    in_maps = [_inputs_for_core(b, args, _CONSTS) for b in range(8)]
    res = run_bass_kernel_spmd(nc, in_maps, core_ids=list(range(8)))
    out = np.stack([np.ascontiguousarray(np.asarray(r["outT"]).T) for r in res.results])
    return out.astype(np.float32)


# version 05
# speedup vs baseline: 1.9472x; 1.1899x over previous
import sys
sys.path.insert(0, '/opt/trn_rl_repo')
import numpy as np
import ml_dtypes
from contextlib import ExitStack

import concourse.bass as bass
import concourse.mybir as mybir
from concourse.bass_utils import run_bass_kernel_spmd

F32 = mybir.dt.float32
BF16 = mybir.dt.bfloat16
AF = mybir.ActivationFunctionType
ALU = mybir.AluOpType
AX = mybir.AxisListType

S = 8192
D = 1024
NKT = 8
H = 16
DH = 64
W = 256
NW = S // W
DI2 = 5460
DI = 2730
NL = 6
ATILES = [(i * 128, 128) for i in range(21)] + [(2688, 42)]
NAT = len(ATILES)

bfnp = ml_dtypes.bfloat16


def _build_consts():
    t = np.arange(S, dtype=np.float64)
    half = np.arange(0, DH, 2, dtype=np.float64)
    inv_freq = 1.0 / 10000.0 ** (half / DH)
    freqs = t[:, None] * inv_freq[None, :]
    freqs = np.concatenate([freqs, freqs], -1)
    base_scale = (half + 0.4 * DH) / (1.4 * DH)
    power = (t - S // 2) / (W // 2)
    scale = base_scale[None, :] ** power[:, None]
    scale = np.concatenate([scale, scale], -1)
    cos = np.cos(freqs); sin = np.sin(freqs)
    sgn = np.concatenate([-np.ones(32), np.ones(32)])[None, :]
    consts = {
        "cs_q": (8.0 * cos * scale).astype(np.float32),
        "ss_q": (8.0 * sin * scale * sgn).astype(np.float32),
        "cs_k": (cos / scale).astype(np.float32),
        "ss_k": (sin / scale * sgn).astype(np.float32),
    }
    kl = np.arange(128)[:, None]; ql = np.arange(128)[None, :]
    consts["triA"] = (ql <= kl).astype(bfnp)
    consts["triB"] = (ql >= kl).astype(bfnp)
    consts["ident"] = np.eye(128, dtype=bfnp)
    cv = np.zeros((128, 8), np.float32)
    cv[:, 0] = 1e-5; cv[:, 1] = -0.5; cv[:, 2] = -1.0; cv[:, 3] = -2.0
    cv[:, 4] = 1.0 / D; cv[:, 5] = 0.25; cv[:, 6] = 1.0
    consts["cvals"] = cv
    return consts


def _col(v):
    return np.ascontiguousarray(v.reshape(-1, 128).T)


def _colpad(v, n):
    out = np.zeros(n * 128, v.dtype)
    out[: v.shape[0]] = v
    return _col(out)


class Sem:
    def __init__(self, es, nc, name):
        self.h = es.enter_context(nc.semaphore(name))
        self.n = 0

    def inc(self, inst, k=1):
        inst.then_inc(self.h, k)
        self.n += k
        return inst


def wge(eng, sem, lag=0):
    v = sem.n - lag
    if v > 0:
        eng.wait_ge(sem.h, v)


class Prog:
    def __init__(self, nc, es, block):
        self.nc = nc
        self.es = es
        self.block = block
        self.sem = {}
        for name in ["LD", "LD2", "LD3", "MM", "EV", "AC", "ST", "GLD"]:
            self.sem[name] = Sem(es, nc, name)
        self.cv = None

    def s(self, name):
        if name not in self.sem:
            self.sem[name] = Sem(self.es, self.nc, name)
        return self.sem[name]

    def c(self, i):
        return self.cv[:, i:i + 1]

    def c1(self, i):
        return self.cv[0:1, i:i + 1]

    def barrier(self):
        for eng_open in [self.block.sync, self.block.tensor, self.block.vector,
                         self.block.scalar, self.block.gpsimd]:
            def _b(eng):
                for s in self.sem.values():
                    wge(eng, s)
            eng_open(_b)


def mm_group(tensor, out_ap, pairs, sem=None, k=1):
    inst = None
    n = len(pairs)
    for i, (l, r) in enumerate(pairs):
        inst = tensor.matmul(out_ap, l, r, start=(i == 0), stop=(i == n - 1))
    if sem is not None:
        sem.inc(inst, k)
    return inst


def rr8(ap, inner):
    return ap.rearrange("p (k t) -> p k t", t=inner)


def drview(dram, T0, T1):
    return dram.rearrange("(k p) t -> p k t", p=128)[:, :, T0:T1]


def emit_ln(p, d, bufs, T, g_ap=None, b_ap=None, make_f32=True, make_bf=False):
    block = p.block
    MM, EV, AC = p.s("MM"), p.s("EV"), p.s("AC")
    t_f, tb, sqb, rows = bufs["t_f"], bufs["tb"], bufs["sqb"], bufs["rows"]
    onesc, onesf = bufs["onesc"], bufs["onesf"]

    def s_cast(scalar):
        wge(scalar, EV)
        AC.inc(scalar.copy(tb[:, 0:NKT * T], t_f[:, 0:NKT * T]))
    block.scalar(s_cast)

    def v_sq(vector):
        wge(vector, AC)
        EV.inc(vector.tensor_mul(sqb[:, 0:NKT * T], tb[:, 0:NKT * T], tb[:, 0:NKT * T]))
    block.vector(v_sq)

    def t_rows(tensor):
        wge(tensor, EV)
        mm_group(tensor, bufs["ps_sum"],
                 [(onesc[:], tb[:, k * T:(k + 1) * T]) for k in range(NKT)], MM)
        mm_group(tensor, bufs["ps_sumsq"],
                 [(onesc[:], sqb[:, k * T:(k + 1) * T]) for k in range(NKT)], MM)
    block.tensor(t_rows)

    def s_r1(scalar):
        wge(scalar, MM)
        AC.inc(scalar.activation(rows[:, 0:T], bufs["ps_sum"], AF.Identity, scale=p.c1(4)))
        wge(scalar, AC)
        AC.inc(scalar.activation(rows[:, T:2 * T], bufs["ps_sumsq"], AF.Identity, scale=p.c1(4)))
    block.scalar(s_r1)

    def v_r(vector):
        wge(vector, AC)
        EV.inc(vector.tensor_mul(rows[:, 2 * T:3 * T], rows[:, 0:T], rows[:, 0:T]))
        wge(vector, EV)
        EV.inc(vector.tensor_sub(rows[:, 2 * T:3 * T], rows[:, T:2 * T], rows[:, 2 * T:3 * T]))
    block.vector(v_r)

    def s_r2(scalar):
        wge(scalar, EV)
        AC.inc(scalar.activation(rows[:, 3 * T:4 * T], rows[:, 2 * T:3 * T], AF.Ln, bias=p.c1(0)))
        wge(scalar, AC)
        AC.inc(scalar.activation(rows[:, 3 * T:4 * T], rows[:, 3 * T:4 * T], AF.Exp, scale=p.c1(1)))
        wge(scalar, AC)
        AC.inc(scalar.activation(rows[:, 4 * T:5 * T], rows[:, 0:T], AF.Identity, scale=p.c1(2)))
    block.scalar(s_r2)

    def t_bc(tensor):
        wge(tensor, AC)
        mm_group(tensor, bufs["ps_bcmu"], [(onesf[:], rows[:, 4 * T:5 * T])], MM)
        mm_group(tensor, bufs["ps_bcr"], [(onesf[:], rows[:, 3 * T:4 * T])], MM)
    block.tensor(t_bc)

    for k in range(NKT):
        def v_ap(vector, k=k):
            wge(vector, MM)
            EV.inc(vector.tensor_add(t_f[:, k * T:(k + 1) * T], t_f[:, k * T:(k + 1) * T],
                                     bufs["ps_bcmu"]))
            wge(vector, EV)
            if g_ap is None:
                EV.inc(vector.tensor_mul(bufs["outf"][:, k * T:(k + 1) * T],
                                         t_f[:, k * T:(k + 1) * T], bufs["ps_bcr"]))
            else:
                EV.inc(vector.tensor_mul(t_f[:, k * T:(k + 1) * T],
                                         t_f[:, k * T:(k + 1) * T], bufs["ps_bcr"]))
                wge(vector, EV)
                EV.inc(vector.tensor_scalar(bufs["outb"][:, k * T:(k + 1) * T],
                                            t_f[:, k * T:(k + 1) * T],
                                            g_ap[:, k:k + 1], b_ap[:, k:k + 1],
                                            op0=ALU.mult, op1=ALU.add))
        block.vector(v_ap)

        if g_ap is None and make_bf:
            def s_cb(scalar, k=k):
                wge(scalar, EV)
                AC.inc(scalar.copy(bufs["outb"][:, k * T:(k + 1) * T],
                                   bufs["outf"][:, k * T:(k + 1) * T]))
            block.scalar(s_cb)


def emit_s0(p, d):
    nc, block = p.nc, p.block
    p.barrier()
    LD, MM, EV, AC, ST = (p.s(x) for x in ["LD", "MM", "EV", "AC", "ST"])
    es = ExitStack()
    with es:
        wq_sb = es.enter_context(nc.sbuf_tensor("s0_wq", [128, NKT * 1024], BF16))
        wk_sb = es.enter_context(nc.sbuf_tensor("s0_wk", [128, NKT * 1024], BF16))
        wv_sb = es.enter_context(nc.sbuf_tensor("s0_wv", [128, NKT * 1024], BF16))
        bq_sb = es.enter_context(nc.sbuf_tensor("s0_bq", [128, 8], F32))
        bkm2_sb = es.enter_context(nc.sbuf_tensor("s0_bkm2", [128, 8], F32))
        bvbc_sb = es.enter_context(nc.sbuf_tensor("s0_bvbc", [128, 1024], F32))
        onesc = es.enter_context(nc.sbuf_tensor("s0_ones", [128, 1], BF16))
        xb = [es.enter_context(nc.sbuf_tensor(f"s0_xb{i}", [128, NKT * 512], BF16)) for i in range(2)]
        qt = es.enter_context(nc.sbuf_tensor("s0_qt", [128, NKT * 512], BF16))
        kt = es.enter_context(nc.sbuf_tensor("s0_kt", [128, NKT * 512], BF16))
        sq = es.enter_context(nc.sbuf_tensor("s0_sq", [128, NKT * 512], BF16))
        vt = es.enter_context(nc.sbuf_tensor("s0_vt", [128, 1024], BF16))
        rowq = es.enter_context(nc.sbuf_tensor("s0_rowq", [1, 512], BF16))
        rowk = es.enter_context(nc.sbuf_tensor("s0_rowk", [1, 512], BF16))
        rone = es.enter_context(nc.sbuf_tensor("s0_rone", [1, 512], BF16))
        psA = es.enter_context(nc.psum_tensor("s0_psA", [128, 512], F32))
        psB = es.enter_context(nc.psum_tensor("s0_psB", [128, 512], F32))
        psV = es.enter_context(nc.psum_tensor("s0_psV", [128, 512], F32))
        psR = es.enter_context(nc.psum_tensor("s0_psR", [1, 512], F32))
        psR2 = es.enter_context(nc.psum_tensor("s0_psR2", [1, 512], F32))

        def gp_loads(gpsimd):
            GLD = p.s("GLD")
            for w_sb, w_dram in [(wq_sb, d["wq"]), (wk_sb, d["wk"]), (wv_sb, d["wv"])]:
                GLD.inc(gpsimd.dma_start(
                    out=rr8(w_sb, 1024),
                    in_=w_dram.rearrange("(k p) o -> p k o", p=128)), 16)
        block.gpsimd(gp_loads)

        def sy_loads(sync):
            LD.inc(sync.dma_start(out=bq_sb[:], in_=d["bq_pt"]), 16)
            LD.inc(sync.dma_start(out=bkm2_sb[:], in_=d["bk_pt"]), 16)
            LD.inc(sync.dma_start(out=bvbc_sb[:], in_=d["bv_bc"]), 16)
        block.sync(sy_loads)

        def v_init(vector):
            wge(vector, LD)
            EV.inc(vector.memset(onesc[:], 1.0))
            EV.inc(vector.memset(rone[:], 1.0))
            wge(vector, EV)
            EV.inc(vector.tensor_scalar_mul(bkm2_sb[:], bkm2_sb[:], -2.0))
        block.vector(v_init)

        def sy_aug(sync):
            wge(sync, EV)
            for blk in range(16):
                ST.inc(sync.dma_start(out=d["qaug"][0:1, blk * 512:(blk + 1) * 512], in_=rone[:]), 16)
                ST.inc(sync.dma_start(out=d["kaug"][1:2, blk * 512:(blk + 1) * 512], in_=rone[:]), 16)
        block.sync(sy_aug)

        x_cnt = {}

        def load_blk(gpsimd, blk):
            p.s("GLD").inc(gpsimd.dma_start(
                out=rr8(xb[blk % 2], 512),
                in_=drview(d["xT"], blk * 512, (blk + 1) * 512)), 16)
            x_cnt[blk] = p.s("GLD").n

        block.gpsimd(lambda g: load_blk(g, 0))

        for blk in range(16):
            x_sb = xb[blk % 2]
            if blk + 1 < 16:
                def gp(g, blk=blk):
                    wge(g, MM)
                    load_blk(g, blk + 1)
                block.gpsimd(gp)

            for proj in range(2):
                w_sb = [wq_sb, wk_sb][proj]
                dst = [d["qt8"], d["kts8"]][proj]
                b_ap = [bq_sb, bkm2_sb][proj]
                scale_ap = [1.0, p.c(3)][proj]
                tgt_sb = [qt, kt][proj]
                for ot in range(8):
                    ps = psA if ot % 2 == 0 else psB
                    def t_mm(tensor, ot=ot, ps=ps, w_sb=w_sb, x_sb=x_sb, blk=blk):
                        wge(tensor, LD)
                        wge_val(tensor, p.s("GLD"), x_cnt[blk])
                        wge(tensor, AC)
                        mm_group(tensor, ps[:],
                                 [(w_sb[:, k * 1024 + ot * 128: k * 1024 + ot * 128 + 128],
                                   x_sb[:, k * 512:(k + 1) * 512]) for k in range(NKT)], MM)
                    block.tensor(t_mm)

                    def s_ev(scalar, ot=ot, ps=ps, b_ap=b_ap, scale_ap=scale_ap, tgt_sb=tgt_sb):
                        wge(scalar, MM)
                        wge(scalar, ST)
                        AC.inc(scalar.activation(tgt_sb[:, ot * 512:(ot + 1) * 512], ps[:],
                                                 AF.Identity, bias=b_ap[:, ot:ot + 1], scale=scale_ap))
                    block.scalar(s_ev)

                    def v_sq(vector, ot=ot, tgt_sb=tgt_sb):
                        wge(vector, AC)
                        EV.inc(vector.tensor_mul(sq[:, ot * 512:(ot + 1) * 512],
                                                 tgt_sb[:, ot * 512:(ot + 1) * 512],
                                                 tgt_sb[:, ot * 512:(ot + 1) * 512]))
                    block.vector(v_sq)

                def sy_st(sync, dst=dst, tgt_sb=tgt_sb, blk=blk):
                    wge(sync, AC)
                    ST.inc(sync.dma_start(
                        out=drview(dst, blk * 512, (blk + 1) * 512),
                        in_=rr8(tgt_sb, 512)), 16)
                block.sync(sy_st)

                psrow = psR if proj == 0 else psR2
                def t_row(tensor, psrow=psrow):
                    wge(tensor, EV)
                    mm_group(tensor, psrow[:],
                             [(onesc[:], sq[:, k * 512:(k + 1) * 512]) for k in range(NKT)], MM)
                block.tensor(t_row)

                rowsb = [rowq, rowk][proj]
                rsc_ap = [1.0, p.c1(5)][proj]
                def s_row(scalar, psrow=psrow, rowsb=rowsb, rsc_ap=rsc_ap):
                    wge(scalar, MM)
                    wge(scalar, ST)
                    AC.inc(scalar.activation(rowsb[:], psrow[:], AF.Identity, scale=rsc_ap))
                block.scalar(s_row)

                aug_dst = d["qaug"][1:2] if proj == 0 else d["kaug"][0:1]
                def sy_row(sync, aug_dst=aug_dst, rowsb=rowsb, blk=blk):
                    wge(sync, AC)
                    ST.inc(sync.dma_start(out=aug_dst[:, blk * 512:(blk + 1) * 512], in_=rowsb[:]), 16)
                block.sync(sy_row)

            for tt in range(4):
                for oc in range(2):
                    def t_v(tensor, tt=tt, oc=oc, x_sb=x_sb, blk=blk):
                        wge_val(tensor, p.s("GLD"), x_cnt[blk])
                        wge(tensor, EV)
                        mm_group(tensor, psV[:],
                                 [(x_sb[:, k * 512 + tt * 128: k * 512 + tt * 128 + 128],
                                   wv_sb[:, k * 1024 + oc * 512: k * 1024 + oc * 512 + 512])
                                  for k in range(NKT)], MM)
                    block.tensor(t_v)

                    def v_ev(vector, oc=oc):
                        wge(vector, MM)
                        wge(vector, ST)
                        EV.inc(vector.tensor_add(vt[:, oc * 512:(oc + 1) * 512], psV[:],
                                                 bvbc_sb[:, oc * 512:(oc + 1) * 512]))
                    block.vector(v_ev)

                def sy_v(sync, tt=tt, blk=blk):
                    wge(sync, EV)
                    ST.inc(sync.dma_start(
                        out=d["vb"][blk * 512 + tt * 128: blk * 512 + tt * 128 + 128, :],
                        in_=vt[:]), 16)
                block.sync(sy_v)

        def sy_fin(sync):
            wge(sync, ST)
        block.sync(sy_fin)


def emit_s1(p, d):
    nc, block = p.nc, p.block
    p.barrier()
    LD, LD2, LD3, MM, EV, AC, ST = (p.s(x) for x in ["LD", "LD2", "LD3", "MM", "EV", "AC", "ST"])
    es = ExitStack()
    with es:
        wo_sb = es.enter_context(nc.sbuf_tensor("s1_wo", [128, NKT * 1024], BF16))
        bo_sb = es.enter_context(nc.sbuf_tensor("s1_bo", [128, 8], F32))
        onesc = es.enter_context(nc.sbuf_tensor("s1_ones", [128, 1], BF16))
        onesf = es.enter_context(nc.sbuf_tensor("s1_onesf", [1, 128], F32))
        q8 = [es.enter_context(nc.sbuf_tensor(f"s1_q8_{i}", [128, NKT * 512], BF16)) for i in range(2)]
        qa2 = [es.enter_context(nc.sbuf_tensor(f"s1_qa2_{i}", [2, 512], BF16)) for i in range(2)]
        ktile = [es.enter_context(nc.sbuf_tensor(f"s1_kt{i}", [128, NKT * 128], BF16)) for i in range(2)]
        ka2 = [es.enter_context(nc.sbuf_tensor(f"s1_ka2_{i}", [2, 128], BF16)) for i in range(2)]
        fcol = es.enter_context(nc.sbuf_tensor("s1_fcol", [128, 64 * 512], BF16))

        vcol = [es.enter_context(nc.sbuf_tensor(f"s1_vcol{i}", [128, 64 * 128], BF16)) for i in range(2)]
        x_sb = es.enter_context(nc.sbuf_tensor("s1_x", [128, NKT * 512], F32))
        attnT = es.enter_context(nc.sbuf_tensor("s1_attnT", [128, NKT * 512], BF16))
        rden = es.enter_context(nc.sbuf_tensor("s1_rden", [1, 512], F32))
        rb = es.enter_context(nc.sbuf_tensor("s1_rb", [128, 512], F32))
        tb = es.enter_context(nc.sbuf_tensor("s1_tb", [128, NKT * 512], BF16))
        rows = es.enter_context(nc.sbuf_tensor("s1_rows", [1, 5 * 512], F32))
        x1f = es.enter_context(nc.sbuf_tensor("s1_x1f", [128, NKT * 512], F32))
        psD = [es.enter_context(nc.psum_tensor(f"s1_psD{i}", [128, 512], F32)) for i in range(2)]
        psDen = es.enter_context(nc.psum_tensor("s1_psDen", [1, 512], F32))
        psO = [es.enter_context(nc.psum_tensor(f"s1_psO{i}", [128, 512], F32)) for i in range(2)]
        psBCa = es.enter_context(nc.psum_tensor("s1_psBCa", [128, 512], F32))
        psBCb = es.enter_context(nc.psum_tensor("s1_psBCb", [128, 512], F32))
        psET = es.enter_context(nc.psum_tensor("s1_psET", [128, 512], F32))
        sD = p.s("sD"); sE1 = p.s("sE1"); sE2 = p.s("sE2"); sOV = p.s("sOV")

        ln_bufs = dict(t_f=x_sb, tb=tb, sqb=attnT, rows=rows, onesc=onesc, onesf=onesf,
                       ps_sum=psDen[:], ps_sumsq=psBCa[0:1, :], ps_bcmu=psBCa[:],
                       ps_bcr=psBCb[:], outf=x1f, outb=None)

        def gp0(gpsimd):
            p.s("GLD").inc(gpsimd.dma_start(
                out=rr8(wo_sb, 1024),
                in_=d["wo"].rearrange("(k p) o -> p k o", p=128)), 16)
        block.gpsimd(gp0)

        def sy0(sync):
            LD.inc(sync.dma_start(out=bo_sb[:], in_=d["bo_pt"]), 16)
        block.sync(sy0)

        def v0(vector):
            wge(vector, LD)
            EV.inc(vector.memset(onesc[:], 1.0))
            EV.inc(vector.memset(onesf[:], 1.0))
        block.vector(v0)

        q8_cnt = {}

        def load_chunk(sync, c):
            i = c % 2
            LD2.inc(sync.dma_start(out=rr8(q8[i], 512),
                                   in_=drview(d["qt8"], c * 512, (c + 1) * 512)), 16)
            LD2.inc(sync.dma_start(out=qa2[i][:], in_=d["qaug"][:, c * 512:(c + 1) * 512]), 16)
            q8_cnt[c] = LD2.n

        block.sync(lambda s: load_chunk(s, 0))

        kt_cnt = {}

        def load_ktile(sync, kt_i, g):
            i = kt_i % 2
            LD.inc(sync.dma_start(out=rr8(ktile[i], 128),
                                  in_=drview(d["kts8"], kt_i * 128, (kt_i + 1) * 128)), 16)
            LD.inc(sync.dma_start(out=ka2[i][:], in_=d["kaug"][:, kt_i * 128:(kt_i + 1) * 128]), 16)
            kt_cnt[g] = LD.n

        vc_cnt = {}

        def load_vcol(gpsimd, dt, g):
            LD3.inc(gpsimd.dma_start(
                out=vcol[dt % 2].rearrange("p (k c) -> p k c", k=64),
                in_=d["vb"].rearrange("(k p) c -> p k c", p=128)[:, :, dt * 128:(dt + 1) * 128]), 16)
            vc_cnt[g] = LD3.n

        e1_hist = {}
        mm_o_hist = {}

        for c in range(16):
            ci = c % 2
            if c + 1 < 16:
                def syn(s, c=c):
                    wge(s, MM)
                    load_chunk(s, c + 1)
                block.sync(syn)

            def syx(s, c=c):
                wge(s, EV)
                wge(s, AC)
                LD2.inc(s.dma_start(out=rr8(x_sb, 512),
                                    in_=drview(d["xT"], c * 512, (c + 1) * 512)), 16)
            block.sync(syx)

            def sy_k0(s, c=c):
                wge(s, MM)
                load_ktile(s, 0, c * 64)
            block.sync(sy_k0)

            def gp_v01(g, c=c):
                for dt in range(2):
                    if c > 0:
                        wge_val(g, MM, mm_o_hist[(c - 1, 6 + dt)])
                    load_vcol(g, dt, (c, dt))
            block.gpsimd(gp_v01)

            for ktp in range(32):
                for kt_i in (2 * ktp, 2 * ktp + 1):
                    g = c * 64 + kt_i
                    kti = kt_i % 2
                    if kt_i + 1 < 64:
                        def syk(s, kt_i=kt_i, c=c):
                            wge(s, MM)
                            load_ktile(s, kt_i + 1, c * 64 + kt_i + 1)
                        block.sync(syk)

                    ps = psD[kt_i % 2]
                    def t_d(tensor, kti=kti, ps=ps, ci=ci, g=g, c=c):
                        wge_val(tensor, LD, kt_cnt[g])
                        wge_val(tensor, LD2, q8_cnt[c])
                        if g >= 2:
                            wge_val(tensor, sE1, e1_hist[g - 2])
                        pairs = [(ktile[kti][:, k * 128:(k + 1) * 128], q8[ci][:, k * 512:(k + 1) * 512])
                                 for k in range(NKT)]
                        pairs.append((ka2[kti][:], qa2[ci][:]))
                        mm_group(tensor, ps[:], pairs, MM)
                    block.tensor(t_d)

                def s_e(scalar, ktp=ktp, c=c):
                    wge(scalar, MM)
                    for kt_i in (2 * ktp, 2 * ktp + 1):
                        ps = psD[kt_i % 2]
                        sE1.inc(scalar.activation(psET[:], ps[:], AF.Exp, scale=p.c(2)))
                        e1_hist[c * 64 + kt_i] = sE1.n
                        sE2.inc(scalar.activation(fcol[:, kt_i * 512:(kt_i + 1) * 512], psET[:], AF.Exp))
                block.scalar(s_e)

            def t_den(tensor):
                wge(tensor, sE2)
                mm_group(tensor, psDen[:],
                         [(onesc[:], fcol[:, k * 512:(k + 1) * 512]) for k in range(64)], MM)
            block.tensor(t_den)

            def v_den(vector):
                wge(vector, MM)
                EV.inc(vector.reciprocal(rden[:], psDen[:]))
            block.vector(v_den)

            def t_bc(tensor):
                wge(tensor, EV)
                mm_group(tensor, psBCa[:], [(onesf[:], rden[:])], MM)
            block.tensor(t_bc)

            def s_bc(scalar):
                wge(scalar, MM)
                AC.inc(scalar.copy(rb[:], psBCa[:]))
            block.scalar(s_bc)

            for dt in range(8):
                ps = psO[dt % 2]
                def t_o(tensor, dt=dt, ps=ps, c=c):
                    wge_val(tensor, LD3, vc_cnt[(c, dt)])
                    wge(tensor, sOV, lag=1)
                    mm_group(tensor, ps[:],
                             [(vcol[dt % 2][:, k * 128:(k + 1) * 128], fcol[:, k * 512:(k + 1) * 512])
                              for k in range(64)], MM)
                block.tensor(t_o)
                mm_o_hist[(c, dt)] = MM.n
                if dt + 2 < 8:
                    def gpv(g, dt=dt, c=c):
                        wge_val(g, MM, mm_o_hist[(c, dt)])
                        load_vcol(g, dt + 2, (c, dt + 2))
                    block.gpsimd(gpv)

                def v_o(vector, dt=dt, ps=ps):
                    wge(vector, MM)
                    wge(vector, AC)
                    sOV.inc(vector.tensor_mul(attnT[:, dt * 512:(dt + 1) * 512], ps[:], rb[:]))
                block.vector(v_o)

            if "attn_dbg" in d:
                def sy_dbg(sync, c=c):
                    wge(sync, EV)
                    ST.inc(sync.dma_start(out=drview(d["attn_dbg"], c * 512, (c + 1) * 512),
                                          in_=rr8(attnT, 512)), 16)
                block.sync(sy_dbg)

            for ot in range(8):
                ps = psD[ot % 2]
                def t_w(tensor, ot=ot, ps=ps):
                    wge(tensor, p.s("GLD"))
                    wge(tensor, EV)
                    wge(tensor, sOV)
                    wge(tensor, sE1)
                    mm_group(tensor, ps[:],
                             [(wo_sb[:, k * 1024 + ot * 128: k * 1024 + ot * 128 + 128],
                               attnT[:, k * 512:(k + 1) * 512]) for k in range(NKT)], MM)
                block.tensor(t_w)

                def s_w(scalar, ot=ot, ps=ps):
                    wge(scalar, MM)
                    wge(scalar, AC)
                    wge(scalar, ST)
                    AC.inc(scalar.activation(x1f[:, ot * 512:(ot + 1) * 512], ps[:],
                                             AF.Identity, bias=bo_sb[:, ot:ot + 1]))
                block.scalar(s_w)

                def v_w(vector, ot=ot):
                    wge(vector, AC)
                    wge(vector, LD2)
                    EV.inc(vector.tensor_add(x_sb[:, ot * 512:(ot + 1) * 512],
                                             x1f[:, ot * 512:(ot + 1) * 512],
                                             x_sb[:, ot * 512:(ot + 1) * 512]))
                block.vector(v_w)

            emit_ln(p, d, ln_bufs, 512, make_bf=False)

            def sy_out(sync, c=c):
                wge(sync, EV)
                wge(sync, AC)
                ST.inc(sync.dma_start(out=drview(d["x1f"], c * 512, (c + 1) * 512),
                                      in_=rr8(x1f, 512)), 16)
            block.sync(sy_out)

            def gp_out(g, c=c):
                wge(g, EV)
                wge(g, AC)
                ST.inc(g.dma_start(out=drview(d["x1b"], c * 512, (c + 1) * 512),
                                   in_=rr8(x1f, 512)), 16)
            block.gpsimd(gp_out)

        def sy_fin(sync):
            wge(sync, ST)
        block.sync(sy_fin)


def wge_val(eng, sem, v):
    if v > 0:
        eng.wait_ge(sem.h, v)


def emit_attn(p, d, li, xin_f, xin_b, xout_f, xout_b):
    nc, block = p.nc, p.block
    p.barrier()
    LD, GLD, ST = p.s("LD"), p.s("GLD"), p.s("ST")
    names = ["aX", "aQK", "aQE", "aRot", "aTr", "aTrE", "aSim", "aExp", "aMsk",
             "aOut", "aRcp", "aNrmE", "aTr2", "aTr2E", "aWo", "aWoE", "aPc",
             "aLnS", "aLnR", "aLnR1", "aLnB", "aLnA"]
    sm = {n: p.s(n) for n in names}
    (aX, aQK, aQE, aRot, aTr, aTrE, aSim, aExp, aMsk, aOut, aRcp, aNrmE, aTr2,
     aTr2E, aWo, aWoE, aPc, aLnS, aLnR, aLnR1, aLnB, aLnA) = (sm[n] for n in names)
    M = [dict((n, 0) for n in names + ["ST"]) for _ in range(NW + 2)]
    loads_after = {}
    qe_after = {}
    exp_after = {}
    msk_after = {}

    def mark(i):
        for n in names:
            M[i + 1][n] = sm[n].n
        M[i + 1]["ST"] = ST.n

    es = ExitStack()
    pre = f"a{li}_"
    with es:
        wqkv_sb = es.enter_context(nc.sbuf_tensor(pre + "wqkv", [128, NKT * 3072], BF16))
        woi_sb = es.enter_context(nc.sbuf_tensor(pre + "woi", [128, NKT * 1024], BF16))
        qs_sb = es.enter_context(nc.sbuf_tensor(pre + "qs", [128, DH], F32))
        ks_sb = es.enter_context(nc.sbuf_tensor(pre + "ks", [128, DH], F32))
        qsw_sb = es.enter_context(nc.sbuf_tensor(pre + "qsw", [128, DH], F32))
        ksw_sb = es.enter_context(nc.sbuf_tensor(pre + "ksw", [128, DH], F32))
        g_sb = es.enter_context(nc.sbuf_tensor(pre + "g", [128, 8], F32))
        lb_sb = es.enter_context(nc.sbuf_tensor(pre + "lb", [128, 8], F32))
        ident = es.enter_context(nc.sbuf_tensor(pre + "id", [128, 128], BF16))
        triA = es.enter_context(nc.sbuf_tensor(pre + "triA", [128, 128], BF16))
        triB = es.enter_context(nc.sbuf_tensor(pre + "triB", [128, 128], BF16))
        tab_t = es.enter_context(nc.sbuf_tensor(pre + "tab", [128, 4 * 2 * DH], F32))
        tab = [tab_t, tab_t]
        x_sb = [es.enter_context(nc.sbuf_tensor(pre + f"x{i}", [128, NKT * 257], F32)) for i in range(2)]
        xb_sb = [es.enter_context(nc.sbuf_tensor(pre + f"xbw{i}", [128, NKT * 257], BF16)) for i in range(2)]
        q_tm = es.enter_context(nc.sbuf_tensor(pre + "qtm", [128, 1024], BF16))
        k_tm = es.enter_context(nc.sbuf_tensor(pre + "ktm", [128, 1024], BF16))
        qn_sb = es.enter_context(nc.sbuf_tensor(pre + "qn", [128, 1024], BF16))
        tmp1 = es.enter_context(nc.sbuf_tensor(pre + "tmp1", [128, 1024], BF16))
        qrot = es.enter_context(nc.sbuf_tensor(pre + "qrot", [128, 1024], BF16))
        krot = es.enter_context(nc.sbuf_tensor(pre + "krot", [128, 1024], BF16))
        ssq = es.enter_context(nc.sbuf_tensor(pre + "ssq", [128, 32], F32))
        rr_ = es.enter_context(nc.sbuf_tensor(pre + "rr", [128, 32], F32))
        csq = es.enter_context(nc.sbuf_tensor(pre + "csq", [128, 4 * DH], F32))
        qT = es.enter_context(nc.sbuf_tensor(pre + "qT", [64, H * 256], BF16))
        kT = [es.enter_context(nc.sbuf_tensor(pre + f"kT{i}", [64, H * 256], BF16)) for i in range(2)]
        vA = [es.enter_context(nc.sbuf_tensor(pre + f"vA{i}", [128, 2 * H * 65], BF16)) for i in range(3)]
        emT = es.enter_context(nc.sbuf_tensor(pre + "emT", [128, H * 768], BF16))
        attnout = es.enter_context(nc.sbuf_tensor(pre + "ao", [128, 2 * 1024], BF16))
        f_sb = es.enter_context(nc.sbuf_tensor(pre + "f", [128, 2 * H], F32))
        aT = es.enter_context(nc.sbuf_tensor(pre + "aT", [128, NKT * 256], BF16))
        xo_t = es.enter_context(nc.sbuf_tensor(pre + "xo", [128, NKT * 258], F32))
        xo = [xo_t, xo_t]
        tb_a = es.enter_context(nc.sbuf_tensor(pre + "tba", [128, NKT * 256], BF16))
        sqb_a = es.enter_context(nc.sbuf_tensor(pre + "sqba", [128, NKT * 256], BF16))
        xn_sb = es.enter_context(nc.sbuf_tensor(pre + "xn", [128, NKT * 256], BF16))
        u_sb = es.enter_context(nc.sbuf_tensor(pre + "u", [128, 256], F32))
        rows = es.enter_context(nc.sbuf_tensor(pre + "rows", [1, 5 * 256], F32))
        onesc = es.enter_context(nc.sbuf_tensor(pre + "onesc", [128, 1], BF16))
        onesf = es.enter_context(nc.sbuf_tensor(pre + "onesf", [1, 128], F32))
        psMM = [es.enter_context(nc.psum_tensor(pre + f"psMM{i}", [128, 512], F32)) for i in range(2)]
        psT = es.enter_context(nc.psum_tensor(pre + "psT", [128, 1024], BF16))
        psS = [es.enter_context(nc.psum_tensor(pre + f"psS{i}", [128, 512], F32)) for i in range(2)]
        psG = [es.enter_context(nc.psum_tensor(pre + f"psG{i}", [128, 512], F32)) for i in range(2)]
        psT2 = es.enter_context(nc.psum_tensor(pre + "psT2", [128, 1024], BF16))
        psTs = [psT, psT2]

        def gp0(gpsimd):
            GLD.inc(gpsimd.dma_start(out=rr8(wqkv_sb, 3072),
                                     in_=d["wqkv"][li].rearrange("(k p) o -> p k o", p=128)), 16)
            GLD.inc(gpsimd.dma_start(out=rr8(woi_sb, 1024),
                                     in_=d["wo_i"][li].rearrange("(k p) o -> p k o", p=128)), 16)
        block.gpsimd(gp0)

        def sy0(sync):
            for dst, src_ in [(qs_sb, d["qs_bc"][li]), (ks_sb, d["ks_bc"][li]),
                              (qsw_sb, d["qs_sw"][li]), (ksw_sb, d["ks_sw"][li]),
                              (g_sb, d["g_pt"][li]), (lb_sb, d["lb_pt"][li]),
                              (ident, d["ident"]), (triA, d["triA"]), (triB, d["triB"])]:
                LD.inc(sync.dma_start(out=dst[:], in_=src_), 16)
        block.sync(sy0)

        def v0(vector):
            wge(vector, LD)
            aPc.inc(vector.memset(onesc[:], 1.0))
            wge(vector, aPc)
            aPc.inc(vector.memset(onesf[:], 1.0))
            for i in range(3):
                wge(vector, aPc)
                aPc.inc(vector.memset(
                    vA[i].rearrange("p (c h e) -> p c h e", c=2, h=H)[:, :, :, 64:65], 1.0))
            wge(vector, aPc)
            aPc.inc(vector.memset(rr8(x_sb[0], 257)[:, :, 0:1], 0.0))
            wge(vector, aPc)
            aPc.inc(vector.memset(rr8(xb_sb[0], 257)[:, :, 0:1], 0.0))
            wge(vector, aPc)
            aPc.inc(vector.memset(rr8(xo[0], 258)[:, :, 0:1], 0.0))
        block.vector(v0)
        init_pc = aPc.n

        def ph_loads(j):
            if j >= NW:
                return

            def sy_x(sync, j=j):
                w0 = j * W
                wge(sync, aQK)
                wge(sync, aWoE)
                wge(sync, aRot)
                wge_val(sync, aPc, init_pc)
                if j == 0:
                    aX.inc(sync.dma_start(out=rr8(x_sb[0], 257)[:, :, 1:257],
                                          in_=drview(xin_f, 0, 256)), 16)
                    aX.inc(sync.dma_start(out=rr8(xb_sb[0], 257)[:, :, 1:257],
                                          in_=drview(xin_b, 0, 256)), 16)
                else:
                    aX.inc(sync.dma_start(out=rr8(x_sb[j % 2], 257),
                                          in_=drview(xin_f, w0 - 1, w0 + 256)), 16)
                    aX.inc(sync.dma_start(out=rr8(xb_sb[j % 2], 257),
                                          in_=drview(xin_b, w0 - 1, w0 + 256)), 16)
                for ti, nm in enumerate(["cs_q", "ss_q", "cs_k", "ss_k"]):
                    aX.inc(sync.dma_start(
                        out=tab[j % 2].rearrange("p (ti tt e) -> p ti tt e", ti=4, tt=2)[:, ti],
                        in_=d[nm].rearrange("(tt p) e -> p tt e", p=128)[:, 2 * j:2 * j + 2, :]), 16)
            block.sync(sy_x)
            loads_after[j] = aX.n

        def ph_qkv(j):
            if j >= NW:
                return
            qlvl = globals().get("_DBG_QLVL", 99)
            xbw = xb_sb[j % 2]
            for tt in range(2):
                def tview(kt, tt=tt, xbw=xbw):
                    off = (1 if kt < 4 else 0) + tt * 128
                    return xbw[:, kt * 257 + off: kt * 257 + off + 128]

                for ch in range(6):
                    ps = psMM[ch % 2]

                    def t_qkv(tensor, ps=ps, ch=ch, tt=tt, j=j):
                        wge(tensor, GLD)
                        wge_val(tensor, aX, loads_after[j])
                        wge(tensor, aWoE)
                        wge(tensor, aQE, lag=1)
                        mm_group(tensor, ps[:],
                                 [(tview(k), wqkv_sb[:, k * 3072 + ch * 512: k * 3072 + ch * 512 + 512])
                                  for k in range(NKT)], aQK)
                    block.tensor(t_qkv)

                    def v_ev(vector, ps=ps, ch=ch, tt=tt, j=j):
                        wge(vector, aQK)
                        if ch < 2:
                            aQE.inc(vector.tensor_copy(q_tm[:, ch * 512:(ch + 1) * 512], ps[:]))
                        elif ch < 4:
                            aQE.inc(vector.tensor_copy(k_tm[:, (ch - 2) * 512:(ch - 1) * 512], ps[:]))
                        else:
                            if j >= 2:
                                wge_val(vector, aOut, M[j - 1]["aOut"])
                            h0 = (ch - 4) * 8
                            aQE.inc(vector.tensor_copy(
                                vA[j % 3].rearrange("p (c h e) -> p c h e", c=2, h=H)[:, tt, h0:h0 + 8, 0:64],
                                ps.rearrange("p (h e) -> p h e", e=64)))
                    block.vector(v_ev)

                if qlvl < 2:
                    continue
                def v_ss(vector, tt=tt, j=j):
                    wge(vector, aQE)
                    aRot.inc(vector.tensor_mul(tmp1[:], q_tm[:], q_tm[:]))
                    wge(vector, aRot)
                    aRot.inc(vector.tensor_reduce(ssq[:, 0:16],
                                                  tmp1.rearrange("p (h e) -> p h e", e=64),
                                                  axis=AX.X, op=ALU.add))
                    wge(vector, aRot)
                    aRot.inc(vector.tensor_mul(tmp1[:], k_tm[:], k_tm[:]))
                    wge(vector, aRot)
                    aRot.inc(vector.tensor_reduce(ssq[:, 16:32],
                                                  tmp1.rearrange("p (h e) -> p h e", e=64),
                                                  axis=AX.X, op=ALU.add))
                    wge_val(vector, aX, loads_after[j])
                    t3 = tab[j % 2].rearrange("p (ti tt e) -> p ti tt e", ti=4, tt=2)
                    c4 = csq.rearrange("p (ti e) -> p ti e", ti=4)
                    aRot.inc(vector.tensor_mul(c4[:, 0], t3[:, 0, tt], qs_sb[:]))
                    aRot.inc(vector.tensor_mul(c4[:, 1], t3[:, 1, tt], qsw_sb[:]))
                    aRot.inc(vector.tensor_mul(c4[:, 2], t3[:, 2, tt], ks_sb[:]))
                    aRot.inc(vector.tensor_mul(c4[:, 3], t3[:, 3, tt], ksw_sb[:]))
                block.vector(v_ss)

                def s_rr(scalar):
                    wge(scalar, aRot)
                    aRcp.inc(scalar.activation(rr_[:], ssq[:], AF.Ln))
                    wge(scalar, aRcp)
                    aRcp.inc(scalar.activation(rr_[:], rr_[:], AF.Exp, scale=p.c(1)))
                block.scalar(s_rr)

                def v_rrmin(vector):
                    wge(vector, aRcp)
                    aRot.inc(vector.tensor_scalar_min(rr_[:], rr_[:], 1.0e12))
                block.vector(v_rrmin)

                for which in (range(2) if qlvl >= 3 else []):
                    src = q_tm if which == 0 else k_tm
                    dst = qrot if which == 0 else krot
                    rr_sl = rr_[:, 0:16] if which == 0 else rr_[:, 16:32]
                    cs_i, ss_i = (0, 1) if which == 0 else (2, 3)

                    def v_rot(vector, src=src, dst=dst, rr_sl=rr_sl, cs_i=cs_i, ss_i=ss_i):
                        wge(vector, aRot)
                        wge(vector, aTr)
                        c4 = csq.rearrange("p (ti e) -> p ti e", ti=4)
                        h3 = lambda t: t.rearrange("p (h e) -> p h e", e=64)
                        h4 = lambda t: t.rearrange("p (h a c) -> p h a c", a=2, c=32)
                        aRot.inc(vector.tensor_mul(
                            h3(qn_sb), h3(src),
                            rr_sl.rearrange("p (h e) -> p h e", e=1).broadcast_to((128, 16, 64))))
                        wge(vector, aRot)
                        aRot.inc(vector.tensor_mul(
                            h3(tmp1), h3(qn_sb),
                            c4[:, cs_i].rearrange("p (o e) -> p o e", o=1).broadcast_to((128, 16, 64))))
                        wge(vector, aRot)
                        ss_t = c4[:, ss_i].rearrange("p (a c) -> p a c", a=2)
                        aRot.inc(vector.tensor_mul(
                            h4(dst)[:, :, 0, :], h4(qn_sb)[:, :, 1, :],
                            ss_t[:, 0:1, :].broadcast_to((128, 16, 32))))
                        wge(vector, aRot)
                        aRot.inc(vector.tensor_mul(
                            h4(dst)[:, :, 1, :], h4(qn_sb)[:, :, 0, :],
                            ss_t[:, 1:2, :].broadcast_to((128, 16, 32))))
                        wge(vector, aRot)
                        aRot.inc(vector.tensor_add(dst[:], dst[:], tmp1[:]))
                    block.vector(v_rot)

                for which in (range(2) if qlvl >= 4 else []):
                    src = qrot if which == 0 else krot
                    dstT = qT if which == 0 else kT[j % 2]
                    for bb in range(4):
                        def t_tr(tensor, src=src, bb=bb, j=j):
                            wge(tensor, aRot)
                            wge(tensor, aSim)
                            wge(tensor, aTr2E)
                            wge(tensor, aTrE, lag=1)
                            inst = None
                            for hh in range(4):
                                h = bb * 4 + hh
                                inst = tensor.transpose(psTs[bb % 2][0:64, hh * 128:(hh + 1) * 128],
                                                        src[:, h * 64:(h + 1) * 64], ident[:])
                            aTr.inc(inst)
                        block.tensor(t_tr)

                        def v_trev(vector, dstT=dstT, bb=bb, tt=tt):
                            wge(vector, aTr)
                            aTrE.inc(vector.tensor_copy(
                                dstT[0:64, tt * 2048 + bb * 512: tt * 2048 + (bb + 1) * 512],
                                psTs[bb % 2][0:64, 0:512]))
                        block.vector(v_trev)
            qe_after[j] = aQE.n

        exp_cnt = {}

        def ph_sims(j):
            for h in range(H):
                def t_sim(tensor, h=h, j=j):
                    wge(tensor, aTrE)
                    if h >= 2:
                        wge_val(tensor, aExp, exp_cnt[(j, h - 2)])
                    elif j > 0:
                        wge_val(tensor, aExp, exp_cnt[(j - 1, 14 + h)])
                    if h < 2:
                        wge(tensor, aLnR1)
                        wge(tensor, aNrmE)
                        wge(tensor, aLnA)
                    qh_full = qT.rearrange("p (t h c) -> p t h c", t=2, c=128)[:, :, h, :]
                    qh0 = qT[0:64, h * 128: h * 128 + 128]
                    qh1 = qT[0:64, 2048 + h * 128: 2048 + h * 128 + 128]
                    pw_ = 1 - (j % 2)
                    cw_ = j % 2
                    if j > 0:
                        tensor.matmul(psS[h % 2][:, 0:128], kT[pw_][0:64, h * 128: h * 128 + 128],
                                      qh0, start=True, stop=True)
                        tensor.matmul(psS[h % 2][:, 128:384], kT[pw_][0:64, 2048 + h * 128: 2048 + h * 128 + 128],
                                      qh_full, start=True, stop=True)
                    tensor.matmul(psG[h % 2][:, 0:256], kT[cw_][0:64, h * 128: h * 128 + 128],
                                  qh_full, start=True, stop=True)
                    aSim.inc(tensor.matmul(psG[h % 2][:, 256:384], kT[cw_][0:64, 2048 + h * 128: 2048 + h * 128 + 128],
                                           qh1, start=True, stop=True))
                block.tensor(t_sim)

                if h % 2 == 0:
                    continue

                def s_exp(scalar, h=h, j=j):
                    wge(scalar, aSim)
                    if h == 1:
                        wge_val(scalar, aOut, M[j]["aOut"])
                    for hh in (h - 1, h):
                        base = hh * 768
                        if j > 0:
                            aExp.inc(scalar.activation(emT[:, base: base + 384],
                                                       psS[hh % 2][:, 0:384], AF.Exp))
                        aExp.inc(scalar.activation(emT[:, base + 384: base + 768],
                                                   psG[hh % 2][:, 0:384], AF.Exp))
                        exp_cnt[(j, hh)] = aExp.n
                block.scalar(s_exp)

                def g_mask(vector, h=h, j=j):
                    wge(vector, aExp)
                    e6 = emT.rearrange("p (s c) -> p s c", c=128)
                    for hh in (h - 1, h):
                        b6 = hh * 6
                        if j > 0:
                            vector.tensor_mul(
                                e6[:, b6:b6 + 3:2, :], e6[:, b6:b6 + 3:2, :],
                                triA.rearrange("p (o c) -> p o c", o=1).broadcast_to((128, 2, 128)))
                            vector.wait_ge(aMsk.h, aMsk.n)
                        aMsk.inc(vector.tensor_mul(
                            e6[:, b6 + 3:b6 + 6:2, :], e6[:, b6 + 3:b6 + 6:2, :],
                            triB.rearrange("p (o c) -> p o c", o=1).broadcast_to((128, 2, 128))))
                block.vector(g_mask)
            exp_after[j] = aExp.n
            msk_after[j] = aMsk.n

        def ph_outs(j):
            for h in range(H):
                for qt in range(2):
                    ps_att = psS[h % 2] if qt == 0 else psG[h % 2]

                    def t_out(tensor, h=h, qt=qt, ps_att=ps_att, j=j):
                        if h < 2:
                            wge(tensor, aLnA)
                        wge_val(tensor, aMsk, msk_after[j])
                        wge_val(tensor, aExp, exp_after[j])
                        wge_val(tensor, aQE, qe_after.get(j, aQE.n))
                        wge(tensor, aNrmE, lag=2)
                        base = h * 768
                        pairs = []
                        va_c = vA[j % 3].rearrange("p (c h e) -> p c h e", c=2, h=H)
                        va_p = vA[(j - 1) % 3].rearrange("p (c h e) -> p c h e", c=2, h=H)
                        if qt == 0:
                            if j > 0:
                                pairs.append((emT[:, base + 0: base + 128], va_p[:, 0, h, :]))
                                pairs.append((emT[:, base + 128: base + 256], va_p[:, 1, h, :]))
                            pairs.append((emT[:, base + 384: base + 512], va_c[:, 0, h, :]))
                        else:
                            if j > 0:
                                pairs.append((emT[:, base + 256: base + 384], va_p[:, 1, h, :]))
                            pairs.append((emT[:, base + 512: base + 640], va_c[:, 0, h, :]))
                            pairs.append((emT[:, base + 640: base + 768], va_c[:, 1, h, :]))
                        mm_group(tensor, ps_att[:, 384:449], pairs, aOut)
                    block.tensor(t_out)

                def v_rcp(vector, h=h):
                    wge(vector, aOut)
                    wge(vector, aNrmE)
                    aRcp.inc(vector.reciprocal(f_sb[:, 2 * h:2 * h + 1], psS[h % 2][:, 448:449]))
                    wge(vector, aRcp)
                    aRcp.inc(vector.reciprocal(f_sb[:, 2 * h + 1:2 * h + 2], psG[h % 2][:, 448:449]))
                block.vector(v_rcp)

                def s_nrm(vector, h=h):
                    wge(vector, aRcp)
                    wge(vector, aTr2)
                    ao = attnout.rearrange("p (q h e) -> p q h e", q=2, e=64)
                    aNrmE.inc(vector.tensor_scalar_mul(ao[:, 0, h, :], psS[h % 2][:, 384:448],
                                                       f_sb[:, 2 * h:2 * h + 1]))
                    wge(vector, aNrmE)
                    aNrmE.inc(vector.tensor_scalar_mul(ao[:, 1, h, :], psG[h % 2][:, 384:448],
                                                       f_sb[:, 2 * h + 1:2 * h + 2]))
                block.vector(s_nrm)

        def ph_tr2(j):
            for qt in range(2):
                for bb in range(2):
                    g2 = qt * 2 + bb
                    def t_tr2(tensor, qt=qt, bb=bb, g2=g2):
                        wge(tensor, aNrmE)
                        wge(tensor, aTrE)
                        wge(tensor, aTr2E, lag=1)
                        inst = None
                        for cc in range(4):
                            ct = bb * 4 + cc
                            inst = tensor.transpose(psTs[g2 % 2][:, cc * 128:(cc + 1) * 128],
                                                    attnout[:, qt * 1024 + ct * 128: qt * 1024 + ct * 128 + 128],
                                                    ident[:])
                        aTr2.inc(inst)
                    block.tensor(t_tr2)

                    def v_tr2e(vector, qt=qt, bb=bb, g2=g2):
                        wge(vector, aTr2)
                        aTr2E.inc(vector.tensor_copy(
                            aT[:, qt * 1024 + bb * 512: qt * 1024 + (bb + 1) * 512],
                            psTs[g2 % 2][:, 0:512]))
                    block.vector(v_tr2e)

        def ph_wo(j):
            cw_ = j % 2

            def v_pc(vector, j=j, cw_=cw_):
                if j > 0:
                    wge(vector, aWoE)
                    wge(vector, aLnS)
                    aPc.inc(vector.tensor_copy(rr8(xo[cw_], 258)[:, :, 0:1],
                                               rr8(xo[cw_], 258)[:, :, 256:257]))
                else:
                    aPc.inc(vector.engine_nop())
            block.vector(v_pc)

            for ot in range(8):
                ps = psMM[ot % 2]

                def t_wo(tensor, ot=ot, ps=ps, j=j):
                    wge(tensor, aTr2E)
                    wge_val(tensor, aQE, qe_after.get(j + 1, aQE.n))
                    wge(tensor, aWoE, lag=1)
                    aT4 = aT.rearrange("p (q c e) -> p q c e", q=2, e=128)
                    mm_group(tensor, ps[:, 0:256],
                             [(woi_sb[:, k * 1024 + ot * 128: k * 1024 + ot * 128 + 128],
                               aT4[:, :, k, :]) for k in range(NKT)], aWo)
                block.tensor(t_wo)

                def v_wo(vector, ot=ot, ps=ps, j=j, cw_=cw_):
                    wge(vector, aWo)
                    wge(vector, aPc)
                    wge_val(vector, ST, M[j]["ST"])
                    wge(vector, aLnA)
                    aWoE.inc(vector.tensor_add(xo[cw_][:, ot * 258 + 1: ot * 258 + 257],
                                               ps[:, 0:256],
                                               x_sb[j % 2][:, ot * 257 + 1: ot * 257 + 257]))
                block.vector(v_wo)

        def ph_ln(j):
            cw_ = j % 2
            T = 256

            def v_casts(vector, j=j, cw_=cw_):
                wge(vector, aWoE)
                wge_val(vector, ST, M[j]["ST"])
                aLnS.inc(vector.tensor_copy(rr8(tb_a, 256)[:, 0:4, :],
                                            rr8(xo[cw_], 258)[:, 0:4, 1:257]))
                wge(vector, aLnS)
                aLnS.inc(vector.tensor_copy(rr8(tb_a, 256)[:, 4:8, :],
                                            rr8(xo[cw_], 258)[:, 4:8, 0:256]))
                wge(vector, aLnS)
                aLnS.inc(vector.tensor_mul(sqb_a[:], tb_a[:], tb_a[:]))
            block.vector(v_casts)

            def t_lnrows(tensor, j=j):
                wge(tensor, aLnS)
                wge_val(tensor, aExp, exp_after[j])
                wge(tensor, aNrmE)
                wge(tensor, aRcp)
                mm_group(tensor, psS[0][0:1, 0:256],
                         [(onesc[:], tb_a[:, k * 256:(k + 1) * 256]) for k in range(NKT)], aLnR)
                mm_group(tensor, psS[1][0:1, 0:256],
                         [(onesc[:], sqb_a[:, k * 256:(k + 1) * 256]) for k in range(NKT)], aLnR)
            block.tensor(t_lnrows)

            def s_lr(scalar):
                wge(scalar, aLnR)
                aLnR1.inc(scalar.activation(rows[:, 0:T], psS[0][0:1, 0:256], AF.Identity, scale=p.c1(4)))
                wge(scalar, aLnR1)
                aLnR1.inc(scalar.activation(rows[:, T:2 * T], psS[1][0:1, 0:256], AF.Identity, scale=p.c1(4)))
            block.scalar(s_lr)

            def v_lr(vector):
                wge(vector, aLnR1)
                aLnB.inc(vector.tensor_mul(rows[:, 2 * T:3 * T], rows[:, 0:T], rows[:, 0:T]))
                wge(vector, aLnB)
                aLnB.inc(vector.tensor_sub(rows[:, 2 * T:3 * T], rows[:, T:2 * T], rows[:, 2 * T:3 * T]))
            block.vector(v_lr)

            def s_lr2(scalar):
                wge(scalar, aLnB)
                aLnB.inc(scalar.activation(rows[:, 3 * T:4 * T], rows[:, 2 * T:3 * T], AF.Ln, bias=p.c1(0)))
                wge(scalar, aLnB)
                aLnB.inc(scalar.activation(rows[:, 3 * T:4 * T], rows[:, 3 * T:4 * T], AF.Exp, scale=p.c1(1)))
                wge(scalar, aLnB)
                aLnB.inc(scalar.activation(rows[:, 4 * T:5 * T], rows[:, 0:T], AF.Identity, scale=p.c1(2)))
            block.scalar(s_lr2)

            def t_lnbc(tensor, j=j):
                wge(tensor, aLnB)
                wge_val(tensor, aLnA, M[j]["aLnA"])
                mm_group(tensor, psS[0][:, 256:512], [(onesf[:], rows[:, 4 * T:5 * T])], aLnB)
                mm_group(tensor, psS[1][:, 256:512], [(onesf[:], rows[:, 3 * T:4 * T])], aLnB)
            block.tensor(t_lnbc)

            for kk in range(NKT):
                def v_lap(vector, kk=kk, cw_=cw_, j=j):
                    wge(vector, aLnB)
                    wge(vector, aLnA)
                    if kk == 0:
                        wge_val(vector, ST, M[j]["ST"])
                    off = (1 if kk < 4 else 0)
                    src = xo[cw_][:, kk * 258 + off: kk * 258 + off + 256]
                    aLnA.inc(vector.tensor_add(u_sb[:], src, psS[0][:, 256:512]))
                    wge(vector, aLnA)
                    aLnA.inc(vector.tensor_mul(u_sb[:], u_sb[:], psS[1][:, 256:512]))
                    wge(vector, aLnA)
                    aLnA.inc(vector.tensor_scalar(xn_sb[:, kk * 256:(kk + 1) * 256], u_sb[:],
                                                  g_sb[:, kk:kk + 1], lb_sb[:, kk:kk + 1],
                                                  op0=ALU.mult, op1=ALU.add))
                block.vector(v_lap)

            def sy_st(sync, j=j, cw_=cw_):
                w0 = j * W
                wge(sync, aLnA)
                wge(sync, aLnS)
                ST.inc(sync.dma_start(out=drview(xout_f, w0, w0 + 256),
                                      in_=rr8(xo[cw_], 258)[:, :, 1:257]), 16)
                ST.inc(sync.dma_start(out=drview(d["xnb"], w0, w0 + 256), in_=rr8(xn_sb, 256)), 16)
            block.sync(sy_st)

            def gp_st(g, j=j, cw_=cw_):
                w0 = j * W
                wge(g, aLnA)
                wge(g, aLnS)
                ST.inc(g.dma_start(out=drview(xout_b, w0, w0 + 256),
                                   in_=rr8(xo[cw_], 258)[:, :, 1:257]), 16)
            block.gpsimd(gp_st)

        nwin = globals().get("_DBG_NWIN", NW)
        nph = globals().get("_DBG_NPH", 99)
        if nph >= -1:
            ph_loads(0)
        for w in range(nwin):
            if w == 0 and nph >= 0:
                ph_qkv(0)
                ph_loads(1)
            if nph >= 1: ph_sims(w)
            if nph >= 2: ph_qkv(w + 1)
            if nph >= 3: ph_outs(w)
            if nph >= 4: ph_tr2(w)
            if nph >= 5: ph_wo(w)
            if nph >= 6: ph_loads(w + 2)
            if nph >= 7: ph_ln(w)
            mark(w)

        def sy_fin(sync):
            wge(sync, ST)
        block.sync(sy_fin)


def emit_f1(p, d, li):
    nc, block = p.nc, p.block
    p.barrier()
    LD, LD2, MM, EV, AC, ST = (p.s(x) for x in ["LD", "LD2", "MM", "EV", "AC", "ST"])
    es = ExitStack()
    pre = f"f1_{li}_"
    with es:
        w1_sb = es.enter_context(nc.sbuf_tensor(pre + "w1", [128, NKT * DI2], BF16))
        b1a_sb = es.enter_context(nc.sbuf_tensor(pre + "b1a", [128, NAT], F32))
        b1g_sb = es.enter_context(nc.sbuf_tensor(pre + "b1g", [128, NAT], F32))
        xn = [es.enter_context(nc.sbuf_tensor(pre + f"xn{i}", [128, NKT * 512], BF16)) for i in range(2)]
        gg = es.enter_context(nc.sbuf_tensor(pre + "gg", [128, 512], F32))
        h_sb = [es.enter_context(nc.sbuf_tensor(pre + f"h{i}", [128, 512], BF16)) for i in range(2)]
        psa = [es.enter_context(nc.psum_tensor(pre + f"psa{i}", [128, 512], F32)) for i in range(2)]
        psg = [es.enter_context(nc.psum_tensor(pre + f"psg{i}", [128, 512], F32)) for i in range(2)]

        def gp0(gpsimd):
            p.s("GLD").inc(gpsimd.dma_start(out=rr8(w1_sb, DI2),
                                    in_=d["w1"][li].rearrange("(k p) o -> p k o", p=128)), 16)
        block.gpsimd(gp0)

        def sy0(sync):
            LD.inc(sync.dma_start(out=b1a_sb[:], in_=d["b1a_pt"][li]), 16)
            LD.inc(sync.dma_start(out=b1g_sb[:], in_=d["b1g_pt"][li]), 16)
        block.sync(sy0)

        xn_cnt = {}

        def load_blk(sync, blk):
            LD2.inc(sync.dma_start(out=rr8(xn[blk % 2], 512),
                                   in_=drview(d["xnb"], blk * 512, (blk + 1) * 512)), 16)
            xn_cnt[blk] = LD2.n

        block.sync(lambda s: load_blk(s, 0))

        for blk in range(16):
            xs = xn[blk % 2]
            if blk + 1 < 16:
                def syn(s, blk=blk):
                    wge(s, MM)
                    load_blk(s, blk + 1)
                block.sync(syn)

            for at in range(NAT):
                aoff, asz = ATILES[at]
                pa = psa[at % 2]
                pg = psg[at % 2]
                def t_a(tensor, aoff=aoff, asz=asz, pa=pa, pg=pg, xs=xs, blk=blk):
                    wge(tensor, LD)
                    wge(tensor, p.s("GLD"))
                    wge_val(tensor, LD2, xn_cnt[blk])
                    wge(tensor, EV, lag=1)
                    wge(tensor, AC, lag=1)
                    mm_group(tensor, pa[0:asz, :],
                             [(w1_sb[:, k * DI2 + aoff: k * DI2 + aoff + asz],
                               xs[:, k * 512:(k + 1) * 512]) for k in range(NKT)], MM)
                    mm_group(tensor, pg[0:asz, :],
                             [(w1_sb[:, k * DI2 + DI + aoff: k * DI2 + DI + aoff + asz],
                               xs[:, k * 512:(k + 1) * 512]) for k in range(NKT)], MM)
                block.tensor(t_a)

                def s_g(scalar, at=at, asz=asz, pg=pg):
                    wge(scalar, MM)
                    wge(scalar, EV)
                    AC.inc(scalar.activation(gg[0:asz, :], pg[0:asz, :], AF.Gelu,
                                             bias=b1g_sb[0:asz, at:at + 1]))
                block.scalar(s_g)

                hb_ = h_sb[at % 2]
                def v_h(vector, at=at, asz=asz, pa=pa, hb_=hb_):
                    wge(vector, AC)
                    wge(vector, ST)
                    EV.inc(vector.scalar_tensor_tensor(hb_[0:asz, :], pa[0:asz, :],
                                                       b1a_sb[0:asz, at:at + 1], gg[0:asz, :],
                                                       op0=ALU.add, op1=ALU.mult))
                block.vector(v_h)

                def sy_h(sync, at=at, asz=asz, blk=blk, hb_=hb_):
                    wge(sync, EV)
                    ST.inc(sync.dma_start(
                        out=d["hb"][at * 128: at * 128 + asz, blk * 512:(blk + 1) * 512],
                        in_=hb_[0:asz, :]), 16)
                block.sync(sy_h)

        def sy_fin(sync):
            wge(sync, ST)
        block.sync(sy_fin)


def emit_f2(p, d, li, xres_f, xout_f, xout_b):
    nc, block = p.nc, p.block
    p.barrier()
    LD, LD2, MM, EV, AC, ST = (p.s(x) for x in ["LD", "LD2", "MM", "EV", "AC", "ST"])
    es = ExitStack()
    pre = f"f2_{li}_"
    TB = 256
    with es:
        w2_sb = es.enter_context(nc.sbuf_tensor(pre + "w2", [128, NAT * 1024], BF16))
        b2_sb = es.enter_context(nc.sbuf_tensor(pre + "b2", [128, 8], F32))
        hs = [es.enter_context(nc.sbuf_tensor(pre + f"hs{i}", [128, NAT * TB], BF16)) for i in range(2)]
        xr = [es.enter_context(nc.sbuf_tensor(pre + f"xr{i}", [128, NKT * TB], F32)) for i in range(2)]
        xof = es.enter_context(nc.sbuf_tensor(pre + "xof", [128, NKT * TB], F32))
        ps2 = [es.enter_context(nc.psum_tensor(pre + f"ps{i}", [128, TB], F32)) for i in range(2)]

        def gp0(gpsimd):
            p.s("GLD").inc(gpsimd.dma_start(
                out=w2_sb.rearrange("p (k o) -> p k o", k=NAT)[:, 0:21, :],
                in_=d["w2"][li][0:2688].rearrange("(k p) o -> p k o", p=128)), 16)
            p.s("GLD").inc(gpsimd.dma_start(
                out=w2_sb[0:42, 21 * 1024: 22 * 1024],
                in_=d["w2"][li][2688:2730, :]), 16)
        block.gpsimd(gp0)

        def sy0(sync):
            LD.inc(sync.dma_start(out=b2_sb[:], in_=d["b2_pt"][li]), 16)
        block.sync(sy0)

        hs_cnt = {}

        def load_blk(sync, gpsimd_unused, blk):
            i = blk % 2
            LD2.inc(sync.dma_start(
                out=hs[i].rearrange("p (k t) -> p k t", k=NAT),
                in_=d["hb"].rearrange("(k p) t -> p k t", p=128)[:, :, blk * TB:(blk + 1) * TB]), 16)
            LD2.inc(sync.dma_start(out=rr8(xr[i], TB),
                                   in_=drview(xres_f, blk * TB, (blk + 1) * TB)), 16)
            hs_cnt[blk] = LD2.n

        block.sync(lambda s: load_blk(s, None, 0))

        for blk in range(32):
            i = blk % 2
            if blk + 1 < 32:
                def syn(s, blk=blk):
                    wge(s, MM)
                    wge(s, EV)
                    load_blk(s, None, blk + 1)
                block.sync(syn)

            for ot in range(8):
                ps = ps2[ot % 2]
                def t_m(tensor, ot=ot, ps=ps, i=i, blk=blk):
                    wge(tensor, LD)
                    wge(tensor, p.s("GLD"))
                    wge_val(tensor, LD2, hs_cnt[blk])
                    wge(tensor, EV, lag=1)
                    pairs = []
                    for at in range(NAT):
                        asz = ATILES[at][1]
                        pairs.append((w2_sb[0:asz, at * 1024 + ot * 128: at * 1024 + ot * 128 + 128],
                                      hs[i][0:asz, at * TB:(at + 1) * TB]))
                    mm_group(tensor, ps[:], pairs, MM)
                block.tensor(t_m)

                def v_o(vector, ot=ot, ps=ps, i=i):
                    wge(vector, MM)
                    wge(vector, ST)
                    EV.inc(vector.scalar_tensor_tensor(xof[:, ot * TB:(ot + 1) * TB], ps[:],
                                                       b2_sb[:, ot:ot + 1],
                                                       xr[i][:, ot * TB:(ot + 1) * TB],
                                                       op0=ALU.add, op1=ALU.add))
                block.vector(v_o)

            def sy_st(sync, blk=blk):
                wge(sync, EV)
                ST.inc(sync.dma_start(out=drview(xout_f, blk * TB, (blk + 1) * TB),
                                      in_=rr8(xof, TB)), 16)
            block.sync(sy_st)

            def gp_st(g, blk=blk):
                wge(g, EV)
                ST.inc(g.dma_start(out=drview(xout_b, blk * TB, (blk + 1) * TB),
                                   in_=rr8(xof, TB)), 16)
            block.gpsimd(gp_st)

        def sy_fin(sync):
            wge(sync, ST)
        block.sync(sy_fin)


def emit_fin(p, d, xlast_f):
    nc, block = p.nc, p.block
    p.barrier()
    LD, MM, EV, AC, ST = (p.s(x) for x in ["LD", "MM", "EV", "AC", "ST"])
    es = ExitStack()
    with es:
        onesc = es.enter_context(nc.sbuf_tensor("fin_ones", [128, 1], BF16))
        onesf = es.enter_context(nc.sbuf_tensor("fin_onesf", [1, 128], F32))
        a_sb = es.enter_context(nc.sbuf_tensor("fin_a", [128, NKT * 512], F32))
        t_f = es.enter_context(nc.sbuf_tensor("fin_t", [128, NKT * 512], F32))
        tb = es.enter_context(nc.sbuf_tensor("fin_tb", [128, NKT * 512], BF16))
        sqb = es.enter_context(nc.sbuf_tensor("fin_sqb", [128, NKT * 512], BF16))
        rows = es.enter_context(nc.sbuf_tensor("fin_rows", [1, 5 * 512], F32))
        outf = es.enter_context(nc.sbuf_tensor("fin_outf", [128, NKT * 512], F32))
        psR1 = es.enter_context(nc.psum_tensor("fin_psR1", [1, 512], F32))
        psBCa = es.enter_context(nc.psum_tensor("fin_psBCa", [128, 512], F32))
        psBCb = es.enter_context(nc.psum_tensor("fin_psBCb", [128, 512], F32))

        def v0(vector):
            EV.inc(vector.memset(onesc[:], 1.0))
            EV.inc(vector.memset(onesf[:], 1.0))
        block.vector(v0)

        bufs = dict(t_f=t_f, tb=tb, sqb=sqb, rows=rows, onesc=onesc, onesf=onesf,
                    ps_sum=psR1[:], ps_sumsq=psBCa[0:1, :], ps_bcmu=psBCa[:],
                    ps_bcr=psBCb[:], outf=outf, outb=tb)

        for c in range(16):
            def sy_l(sync, c=c):
                wge(sync, EV)
                wge(sync, AC)
                wge(sync, ST)
                LD.inc(sync.dma_start(out=rr8(t_f, 512),
                                      in_=drview(d["x1f"], c * 512, (c + 1) * 512)), 16)
                LD.inc(sync.dma_start(out=rr8(a_sb, 512),
                                      in_=drview(xlast_f, c * 512, (c + 1) * 512)), 16)
            block.sync(sy_l)

            def v_add(vector):
                wge(vector, LD)
                EV.inc(vector.tensor_add(t_f[:], t_f[:], a_sb[:]))
            block.vector(v_add)

            emit_ln(p, d, bufs, 512, make_bf=False)

            def sy_st(sync, c=c):
                wge(sync, EV)
                ST.inc(sync.dma_start(out=drview(d["outT"], c * 512, (c + 1) * 512),
                                      in_=rr8(outf, 512)), 16)
            block.sync(sy_st)

        def sy_fin(sync):
            wge(sync, ST)
        block.sync(sy_fin)


def build_program(n_layers=NL, outer_only=False, dbg=(), attn_only=False):
    nc = bass.Bass("TRN2", target_bir_lowering=False)
    d = {}
    def scratch(name, shape, dtype):
        kind = "ExternalOutput" if name in dbg else "Internal"
        return nc.dram_tensor(name, shape, dtype, kind=kind).ap()
    d["xT"] = nc.dram_tensor("xT", [D, S], F32, kind="ExternalInput").ap()
    for nm in ["wq", "wk", "wv", "wo"]:
        d[nm] = nc.dram_tensor(nm, [D, D], F32, kind="ExternalInput").ap()
    for nm in ["bq_pt", "bk_pt", "bo_pt"]:
        d[nm] = nc.dram_tensor(nm, [128, 8], F32, kind="ExternalInput").ap()
    d["bv_bc"] = nc.dram_tensor("bv_bc", [128, D], F32, kind="ExternalInput").ap()
    d["wqkv"] = nc.dram_tensor("wqkv", [NL, D, 3072], F32, kind="ExternalInput").ap()
    d["wo_i"] = nc.dram_tensor("wo_i", [NL, D, D], F32, kind="ExternalInput").ap()
    d["w1"] = nc.dram_tensor("w1", [NL, D, DI2], F32, kind="ExternalInput").ap()
    d["w2"] = nc.dram_tensor("w2", [NL, DI, D], F32, kind="ExternalInput").ap()
    d["qs_bc"] = nc.dram_tensor("qs_bc", [NL, 128, DH], F32, kind="ExternalInput").ap()
    d["ks_bc"] = nc.dram_tensor("ks_bc", [NL, 128, DH], F32, kind="ExternalInput").ap()
    d["qs_sw"] = nc.dram_tensor("qs_sw", [NL, 128, DH], F32, kind="ExternalInput").ap()
    d["ks_sw"] = nc.dram_tensor("ks_sw", [NL, 128, DH], F32, kind="ExternalInput").ap()
    d["g_pt"] = nc.dram_tensor("g_pt", [NL, 128, 8], F32, kind="ExternalInput").ap()
    d["lb_pt"] = nc.dram_tensor("lb_pt", [NL, 128, 8], F32, kind="ExternalInput").ap()
    d["b1a_pt"] = nc.dram_tensor("b1a_pt", [NL, 128, NAT], F32, kind="ExternalInput").ap()
    d["b1g_pt"] = nc.dram_tensor("b1g_pt", [NL, 128, NAT], F32, kind="ExternalInput").ap()
    d["b2_pt"] = nc.dram_tensor("b2_pt", [NL, 128, 8], F32, kind="ExternalInput").ap()
    for nm in ["cs_q", "ss_q", "cs_k", "ss_k"]:
        d[nm] = nc.dram_tensor(nm, [S, DH], F32, kind="ExternalInput").ap()
    for nm in ["triA", "triB", "ident"]:
        d[nm] = nc.dram_tensor(nm, [128, 128], BF16, kind="ExternalInput").ap()
    d["qt8"] = scratch("qt8", [D, S], BF16)
    d["kts8"] = scratch("kts8", [D, S], BF16)
    d["qaug"] = scratch("qaug", [2, S], BF16)
    d["kaug"] = scratch("kaug", [2, S], BF16)
    d["vb"] = scratch("vb", [S, D], BF16)
    d["x1f"] = scratch("x1f", [D, S], F32)
    d["x1b"] = scratch("x1b", [D, S], BF16)
    d["xf"] = [scratch(f"xf{i}", [D, S], F32) for i in range(2)]
    d["xb"] = [scratch(f"xb{i}", [D, S], BF16) for i in range(2)]
    d["xnb"] = scratch("xnb", [D, S], BF16)
    d["hb"] = scratch("hb", [NAT * 128, S], BF16)
    d["outT"] = nc.dram_tensor("outT", [D, S], F32, kind="ExternalOutput").ap()

    if "attn_dbg" in dbg:
        d["attn_dbg"] = nc.dram_tensor("attn_dbg", [D, S], BF16, kind="ExternalOutput").ap()
    if "aT_dbg" in dbg:
        d["aT_dbg"] = nc.dram_tensor("aT_dbg", [D, S], BF16, kind="ExternalOutput").ap()
    d["cvals"] = nc.dram_tensor("cvals", [128, 8], F32, kind="ExternalInput").ap()
    es = ExitStack()
    with es:
        block = es.enter_context(nc.Block())
        p = Prog(nc, es, block)
        p.cv = es.enter_context(nc.sbuf_tensor("cvals_sb", [128, 8], F32))

        def _cv_load(sync):
            p.s("LD").inc(sync.dma_start(out=p.cv[:], in_=d["cvals"]), 16)
        block.sync(_cv_load)
        emit_s0(p, d)
        emit_s1(p, d)
        if outer_only:
            emit_fin(p, d, d["x1f"])
            return nc
        cur_f, cur_b = d["x1f"], d["x1b"]
        for li in range(n_layers):
            nxt_f, nxt_b = d["xf"][li % 2], d["xb"][li % 2]
            emit_attn(p, d, li, cur_f, cur_b, nxt_f, nxt_b)
            if attn_only:
                return nc
            emit_f1(p, d, li)
            emit_f2(p, d, li, nxt_f, nxt_f, nxt_b)
            cur_f, cur_b = nxt_f, nxt_b
        emit_fin(p, d, cur_f)
    return nc


_CONSTS = None


def _inputs_for_core(b, args, consts):
    (x, wq, bq, wk, bk, wv, bv, wo, bo, wqkv, qs, ks, wo_i, g, lb, w1, b1, w2, b2) = args
    m = {
        "xT": np.ascontiguousarray(x[b].T),
        "wq": wq, "wk": wk, "wv": wv, "wo": wo,
        "bq_pt": _col(bq), "bk_pt": _col(bk), "bo_pt": _col(bo),
        "bv_bc": np.ascontiguousarray(np.broadcast_to(bv, (128, D))),
        "wqkv": wqkv, "wo_i": wo_i, "w1": w1, "w2": w2,
        "qs_bc": np.ascontiguousarray(np.broadcast_to(qs[:, None, :], (NL, 128, DH))),
        "ks_bc": np.ascontiguousarray(np.broadcast_to(ks[:, None, :], (NL, 128, DH))),
        "qs_sw": np.ascontiguousarray(np.broadcast_to(np.roll(qs, -32, -1)[:, None, :], (NL, 128, DH))),
        "ks_sw": np.ascontiguousarray(np.broadcast_to(np.roll(ks, -32, -1)[:, None, :], (NL, 128, DH))),
        "g_pt": np.stack([_col(g[i]) for i in range(NL)]),
        "lb_pt": np.stack([_col(lb[i]) for i in range(NL)]),
        "b1a_pt": np.stack([_colpad(b1[i, :DI], NAT) for i in range(NL)]),
        "b1g_pt": np.stack([_colpad(b1[i, DI:], NAT) for i in range(NL)]),
        "b2_pt": np.stack([_col(b2[i]) for i in range(NL)]),
    }
    m.update(consts)
    return m


def kernel(L0_attn_x, L0_attn_Wq, L0_attn_bq, L0_attn_Wk, L0_attn_bk, L0_attn_Wv,
           L0_attn_bv, L0_attn_Wo, L0_attn_bo, L0_ffn_Wqkv, L0_ffn_q_scale,
           L0_ffn_k_scale, L0_ffn_Wo, L0_ffn_ln_g, L0_ffn_ln_b, L0_ffn_W1,
           L0_ffn_b1, L0_ffn_W2, L0_ffn_b2):
    global _CONSTS
    if _CONSTS is None:
        _CONSTS = _build_consts()
    args = tuple(np.ascontiguousarray(np.asarray(a, dtype=np.float32)) for a in (
        L0_attn_x, L0_attn_Wq, L0_attn_bq, L0_attn_Wk, L0_attn_bk, L0_attn_Wv,
        L0_attn_bv, L0_attn_Wo, L0_attn_bo, L0_ffn_Wqkv, L0_ffn_q_scale,
        L0_ffn_k_scale, L0_ffn_Wo, L0_ffn_ln_g, L0_ffn_ln_b, L0_ffn_W1,
        L0_ffn_b1, L0_ffn_W2, L0_ffn_b2))
    nc = build_program()
    in_maps = [_inputs_for_core(b, args, _CONSTS) for b in range(8)]
    res = run_bass_kernel_spmd(nc, in_maps, core_ids=list(range(8)))
    out = np.stack([np.ascontiguousarray(np.asarray(r["outT"]).T) for r in res.results])
    return out.astype(np.float32)


# version 06 (final)
# speedup vs baseline: 1.9571x; 1.0051x over previous
import sys
sys.path.insert(0, '/opt/trn_rl_repo')
import numpy as np
import ml_dtypes
from contextlib import ExitStack

import concourse.bass as bass
import concourse.mybir as mybir
from concourse.bass_utils import run_bass_kernel_spmd

F32 = mybir.dt.float32
BF16 = mybir.dt.bfloat16
AF = mybir.ActivationFunctionType
ALU = mybir.AluOpType
AX = mybir.AxisListType

S = 8192
D = 1024
NKT = 8
H = 16
DH = 64
W = 256
NW = S // W
DI2 = 5460
DI = 2730
NL = 6
ATILES = [(i * 128, 128) for i in range(21)] + [(2688, 42)]
NAT = len(ATILES)

bfnp = ml_dtypes.bfloat16


def _build_consts():
    t = np.arange(S, dtype=np.float64)
    half = np.arange(0, DH, 2, dtype=np.float64)
    inv_freq = 1.0 / 10000.0 ** (half / DH)
    freqs = t[:, None] * inv_freq[None, :]
    freqs = np.concatenate([freqs, freqs], -1)
    base_scale = (half + 0.4 * DH) / (1.4 * DH)
    power = (t - S // 2) / (W // 2)
    scale = base_scale[None, :] ** power[:, None]
    scale = np.concatenate([scale, scale], -1)
    cos = np.cos(freqs); sin = np.sin(freqs)
    sgn = np.concatenate([-np.ones(32), np.ones(32)])[None, :]
    consts = {
        "cs_q": (8.0 * cos * scale).astype(np.float32),
        "ss_q": (8.0 * sin * scale * sgn).astype(np.float32),
        "cs_k": (cos / scale).astype(np.float32),
        "ss_k": (sin / scale * sgn).astype(np.float32),
    }
    kl = np.arange(128)[:, None]; ql = np.arange(128)[None, :]
    consts["triA"] = (ql <= kl).astype(bfnp)
    consts["triB"] = (ql >= kl).astype(bfnp)
    consts["ident"] = np.eye(128, dtype=bfnp)
    cv = np.zeros((128, 8), np.float32)
    cv[:, 0] = 1e-5; cv[:, 1] = -0.5; cv[:, 2] = -1.0; cv[:, 3] = -2.0
    cv[:, 4] = 1.0 / D; cv[:, 5] = 0.25; cv[:, 6] = 1.0
    consts["cvals"] = cv
    return consts


def _col(v):
    return np.ascontiguousarray(v.reshape(-1, 128).T)


def _colpad(v, n):
    out = np.zeros(n * 128, v.dtype)
    out[: v.shape[0]] = v
    return _col(out)


class Sem:
    def __init__(self, es, nc, name):
        self.h = es.enter_context(nc.semaphore(name))
        self.n = 0

    def inc(self, inst, k=1):
        inst.then_inc(self.h, k)
        self.n += k
        return inst


def wge(eng, sem, lag=0):
    v = sem.n - lag
    if v > 0:
        eng.wait_ge(sem.h, v)


class Prog:
    def __init__(self, nc, es, block):
        self.nc = nc
        self.es = es
        self.block = block
        self.sem = {}
        for name in ["LD", "LD2", "LD3", "MM", "EV", "AC", "ST", "GLD"]:
            self.sem[name] = Sem(es, nc, name)
        self.cv = None

    def s(self, name):
        if name not in self.sem:
            self.sem[name] = Sem(self.es, self.nc, name)
        return self.sem[name]

    def c(self, i):
        return self.cv[:, i:i + 1]

    def c1(self, i):
        return self.cv[0:1, i:i + 1]

    def barrier(self):
        for eng_open in [self.block.sync, self.block.tensor, self.block.vector,
                         self.block.scalar, self.block.gpsimd]:
            def _b(eng):
                for s in self.sem.values():
                    wge(eng, s)
            eng_open(_b)


def mm_group(tensor, out_ap, pairs, sem=None, k=1):
    inst = None
    n = len(pairs)
    for i, (l, r) in enumerate(pairs):
        inst = tensor.matmul(out_ap, l, r, start=(i == 0), stop=(i == n - 1))
    if sem is not None:
        sem.inc(inst, k)
    return inst


def rr8(ap, inner):
    return ap.rearrange("p (k t) -> p k t", t=inner)


def drview(dram, T0, T1):
    return dram.rearrange("(k p) t -> p k t", p=128)[:, :, T0:T1]


def emit_ln(p, d, bufs, T, g_ap=None, b_ap=None, make_f32=True, make_bf=False):
    block = p.block
    MM, EV, AC = p.s("MM"), p.s("EV"), p.s("AC")
    t_f, tb, sqb, rows = bufs["t_f"], bufs["tb"], bufs["sqb"], bufs["rows"]
    onesc, onesf = bufs["onesc"], bufs["onesf"]

    def s_cast(scalar):
        wge(scalar, EV)
        AC.inc(scalar.copy(tb[:, 0:NKT * T], t_f[:, 0:NKT * T]))
    block.scalar(s_cast)

    def v_sq(vector):
        wge(vector, AC)
        EV.inc(vector.tensor_mul(sqb[:, 0:NKT * T], tb[:, 0:NKT * T], tb[:, 0:NKT * T]))
    block.vector(v_sq)

    def t_rows(tensor):
        wge(tensor, EV)
        mm_group(tensor, bufs["ps_sum"],
                 [(onesc[:], tb[:, k * T:(k + 1) * T]) for k in range(NKT)], MM)
        mm_group(tensor, bufs["ps_sumsq"],
                 [(onesc[:], sqb[:, k * T:(k + 1) * T]) for k in range(NKT)], MM)
    block.tensor(t_rows)

    def s_r1(scalar):
        wge(scalar, MM)
        AC.inc(scalar.activation(rows[:, 0:T], bufs["ps_sum"], AF.Identity, scale=p.c1(4)))
        wge(scalar, AC)
        AC.inc(scalar.activation(rows[:, T:2 * T], bufs["ps_sumsq"], AF.Identity, scale=p.c1(4)))
    block.scalar(s_r1)

    def v_r(vector):
        wge(vector, AC)
        EV.inc(vector.tensor_mul(rows[:, 2 * T:3 * T], rows[:, 0:T], rows[:, 0:T]))
        wge(vector, EV)
        EV.inc(vector.tensor_sub(rows[:, 2 * T:3 * T], rows[:, T:2 * T], rows[:, 2 * T:3 * T]))
    block.vector(v_r)

    def s_r2(scalar):
        wge(scalar, EV)
        AC.inc(scalar.activation(rows[:, 3 * T:4 * T], rows[:, 2 * T:3 * T], AF.Ln, bias=p.c1(0)))
        wge(scalar, AC)
        AC.inc(scalar.activation(rows[:, 3 * T:4 * T], rows[:, 3 * T:4 * T], AF.Exp, scale=p.c1(1)))
        wge(scalar, AC)
        AC.inc(scalar.activation(rows[:, 4 * T:5 * T], rows[:, 0:T], AF.Identity, scale=p.c1(2)))
    block.scalar(s_r2)

    def t_bc(tensor):
        wge(tensor, AC)
        mm_group(tensor, bufs["ps_bcmu"], [(onesf[:], rows[:, 4 * T:5 * T])], MM)
        mm_group(tensor, bufs["ps_bcr"], [(onesf[:], rows[:, 3 * T:4 * T])], MM)
    block.tensor(t_bc)

    for k in range(NKT):
        def v_ap(vector, k=k):
            wge(vector, MM)
            EV.inc(vector.tensor_add(t_f[:, k * T:(k + 1) * T], t_f[:, k * T:(k + 1) * T],
                                     bufs["ps_bcmu"]))
            wge(vector, EV)
            if g_ap is None:
                EV.inc(vector.tensor_mul(bufs["outf"][:, k * T:(k + 1) * T],
                                         t_f[:, k * T:(k + 1) * T], bufs["ps_bcr"]))
            else:
                EV.inc(vector.tensor_mul(t_f[:, k * T:(k + 1) * T],
                                         t_f[:, k * T:(k + 1) * T], bufs["ps_bcr"]))
                wge(vector, EV)
                EV.inc(vector.tensor_scalar(bufs["outb"][:, k * T:(k + 1) * T],
                                            t_f[:, k * T:(k + 1) * T],
                                            g_ap[:, k:k + 1], b_ap[:, k:k + 1],
                                            op0=ALU.mult, op1=ALU.add))
        block.vector(v_ap)

        if g_ap is None and make_bf:
            def s_cb(scalar, k=k):
                wge(scalar, EV)
                AC.inc(scalar.copy(bufs["outb"][:, k * T:(k + 1) * T],
                                   bufs["outf"][:, k * T:(k + 1) * T]))
            block.scalar(s_cb)


def emit_s0(p, d):
    nc, block = p.nc, p.block
    p.barrier()
    LD, MM, EV, AC, ST = (p.s(x) for x in ["LD", "MM", "EV", "AC", "ST"])
    es = ExitStack()
    with es:
        wq_sb = es.enter_context(nc.sbuf_tensor("s0_wq", [128, NKT * 1024], BF16))
        wk_sb = es.enter_context(nc.sbuf_tensor("s0_wk", [128, NKT * 1024], BF16))
        wv_sb = es.enter_context(nc.sbuf_tensor("s0_wv", [128, NKT * 1024], BF16))
        bq_sb = es.enter_context(nc.sbuf_tensor("s0_bq", [128, 8], F32))
        bkm2_sb = es.enter_context(nc.sbuf_tensor("s0_bkm2", [128, 8], F32))
        bvbc_sb = es.enter_context(nc.sbuf_tensor("s0_bvbc", [128, 1024], F32))
        onesc = es.enter_context(nc.sbuf_tensor("s0_ones", [128, 1], BF16))
        xb = [es.enter_context(nc.sbuf_tensor(f"s0_xb{i}", [128, NKT * 512], BF16)) for i in range(2)]
        qt = es.enter_context(nc.sbuf_tensor("s0_qt", [128, NKT * 512], BF16))
        kt = es.enter_context(nc.sbuf_tensor("s0_kt", [128, NKT * 512], BF16))
        sq = es.enter_context(nc.sbuf_tensor("s0_sq", [128, NKT * 512], BF16))
        vt = es.enter_context(nc.sbuf_tensor("s0_vt", [128, 1024], BF16))
        rowq = es.enter_context(nc.sbuf_tensor("s0_rowq", [1, 512], BF16))
        rowk = es.enter_context(nc.sbuf_tensor("s0_rowk", [1, 512], BF16))
        rone = es.enter_context(nc.sbuf_tensor("s0_rone", [1, 512], BF16))
        psA = es.enter_context(nc.psum_tensor("s0_psA", [128, 512], F32))
        psB = es.enter_context(nc.psum_tensor("s0_psB", [128, 512], F32))
        psV = es.enter_context(nc.psum_tensor("s0_psV", [128, 512], F32))
        psR = es.enter_context(nc.psum_tensor("s0_psR", [1, 512], F32))
        psR2 = es.enter_context(nc.psum_tensor("s0_psR2", [1, 512], F32))

        def gp_loads(gpsimd):
            GLD = p.s("GLD")
            for w_sb, w_dram in [(wq_sb, d["wq"]), (wk_sb, d["wk"]), (wv_sb, d["wv"])]:
                GLD.inc(gpsimd.dma_start(
                    out=rr8(w_sb, 1024),
                    in_=w_dram.rearrange("(k p) o -> p k o", p=128)), 16)
        block.gpsimd(gp_loads)

        def sy_loads(sync):
            LD.inc(sync.dma_start(out=bq_sb[:], in_=d["bq_pt"]), 16)
            LD.inc(sync.dma_start(out=bkm2_sb[:], in_=d["bk_pt"]), 16)
            LD.inc(sync.dma_start(out=bvbc_sb[:], in_=d["bv_bc"]), 16)
        block.sync(sy_loads)

        def v_init(vector):
            wge(vector, LD)
            EV.inc(vector.memset(onesc[:], 1.0))
            EV.inc(vector.memset(rone[:], 1.0))
            wge(vector, EV)
            EV.inc(vector.tensor_scalar_mul(bkm2_sb[:], bkm2_sb[:], -2.0))
        block.vector(v_init)

        def sy_aug(sync):
            wge(sync, EV)
            for blk in range(16):
                ST.inc(sync.dma_start(out=d["qaug"][0:1, blk * 512:(blk + 1) * 512], in_=rone[:]), 16)
                ST.inc(sync.dma_start(out=d["kaug"][1:2, blk * 512:(blk + 1) * 512], in_=rone[:]), 16)
        block.sync(sy_aug)

        x_cnt = {}
        XB = [p.s("XB0"), p.s("XB1")]

        def load_blk(gpsimd, blk):
            XB[blk % 2].inc(gpsimd.dma_start(
                out=rr8(xb[blk % 2], 512),
                in_=drview(d["xT"], blk * 512, (blk + 1) * 512)), 16)
            x_cnt[blk] = XB[blk % 2].n

        block.gpsimd(lambda g: load_blk(g, 0))

        for blk in range(16):
            x_sb = xb[blk % 2]
            if blk + 1 < 16:
                def gp(g, blk=blk):
                    wge(g, MM)
                    load_blk(g, blk + 1)
                block.gpsimd(gp)

            for proj in range(2):
                w_sb = [wq_sb, wk_sb][proj]
                dst = [d["qt8"], d["kts8"]][proj]
                b_ap = [bq_sb, bkm2_sb][proj]
                scale_ap = [1.0, p.c(3)][proj]
                tgt_sb = [qt, kt][proj]
                for ot in range(8):
                    ps = psA if ot % 2 == 0 else psB
                    def t_mm(tensor, ot=ot, ps=ps, w_sb=w_sb, x_sb=x_sb, blk=blk):
                        wge(tensor, LD)
                        wge(tensor, p.s("GLD"))
                        wge_val(tensor, XB[blk % 2], x_cnt[blk])
                        wge(tensor, AC)
                        mm_group(tensor, ps[:],
                                 [(w_sb[:, k * 1024 + ot * 128: k * 1024 + ot * 128 + 128],
                                   x_sb[:, k * 512:(k + 1) * 512]) for k in range(NKT)], MM)
                    block.tensor(t_mm)

                    def s_ev(scalar, ot=ot, ps=ps, b_ap=b_ap, scale_ap=scale_ap, tgt_sb=tgt_sb):
                        wge(scalar, MM)
                        wge(scalar, ST)
                        AC.inc(scalar.activation(tgt_sb[:, ot * 512:(ot + 1) * 512], ps[:],
                                                 AF.Identity, bias=b_ap[:, ot:ot + 1], scale=scale_ap))
                    block.scalar(s_ev)

                    def v_sq(vector, ot=ot, tgt_sb=tgt_sb):
                        wge(vector, AC)
                        EV.inc(vector.tensor_mul(sq[:, ot * 512:(ot + 1) * 512],
                                                 tgt_sb[:, ot * 512:(ot + 1) * 512],
                                                 tgt_sb[:, ot * 512:(ot + 1) * 512]))
                    block.vector(v_sq)

                def sy_st(sync, dst=dst, tgt_sb=tgt_sb, blk=blk):
                    wge(sync, AC)
                    ST.inc(sync.dma_start(
                        out=drview(dst, blk * 512, (blk + 1) * 512),
                        in_=rr8(tgt_sb, 512)), 16)
                block.sync(sy_st)

                psrow = psR if proj == 0 else psR2
                def t_row(tensor, psrow=psrow):
                    wge(tensor, EV)
                    mm_group(tensor, psrow[:],
                             [(onesc[:], sq[:, k * 512:(k + 1) * 512]) for k in range(NKT)], MM)
                block.tensor(t_row)

                rowsb = [rowq, rowk][proj]
                rsc_ap = [1.0, p.c1(5)][proj]
                def s_row(scalar, psrow=psrow, rowsb=rowsb, rsc_ap=rsc_ap):
                    wge(scalar, MM)
                    wge(scalar, ST)
                    AC.inc(scalar.activation(rowsb[:], psrow[:], AF.Identity, scale=rsc_ap))
                block.scalar(s_row)

                aug_dst = d["qaug"][1:2] if proj == 0 else d["kaug"][0:1]
                def sy_row(sync, aug_dst=aug_dst, rowsb=rowsb, blk=blk):
                    wge(sync, AC)
                    ST.inc(sync.dma_start(out=aug_dst[:, blk * 512:(blk + 1) * 512], in_=rowsb[:]), 16)
                block.sync(sy_row)

            for tt in range(4):
                for oc in range(2):
                    def t_v(tensor, tt=tt, oc=oc, x_sb=x_sb, blk=blk):
                        wge(tensor, p.s("GLD"))
                        wge_val(tensor, XB[blk % 2], x_cnt[blk])
                        wge(tensor, EV)
                        mm_group(tensor, psV[:],
                                 [(x_sb[:, k * 512 + tt * 128: k * 512 + tt * 128 + 128],
                                   wv_sb[:, k * 1024 + oc * 512: k * 1024 + oc * 512 + 512])
                                  for k in range(NKT)], MM)
                    block.tensor(t_v)

                    def v_ev(vector, oc=oc):
                        wge(vector, MM)
                        wge(vector, ST)
                        EV.inc(vector.tensor_add(vt[:, oc * 512:(oc + 1) * 512], psV[:],
                                                 bvbc_sb[:, oc * 512:(oc + 1) * 512]))
                    block.vector(v_ev)

                def sy_v(sync, tt=tt, blk=blk):
                    wge(sync, EV)
                    ST.inc(sync.dma_start(
                        out=d["vb"][blk * 512 + tt * 128: blk * 512 + tt * 128 + 128, :],
                        in_=vt[:]), 16)
                block.sync(sy_v)

        def sy_fin(sync):
            wge(sync, ST)
        block.sync(sy_fin)


def emit_s1(p, d):
    nc, block = p.nc, p.block
    p.barrier()
    LD, LD2, LD3, MM, EV, AC, ST = (p.s(x) for x in ["LD", "LD2", "LD3", "MM", "EV", "AC", "ST"])
    es = ExitStack()
    with es:
        wo_sb = es.enter_context(nc.sbuf_tensor("s1_wo", [128, NKT * 1024], BF16))
        bo_sb = es.enter_context(nc.sbuf_tensor("s1_bo", [128, 8], F32))
        onesc = es.enter_context(nc.sbuf_tensor("s1_ones", [128, 1], BF16))
        onesf = es.enter_context(nc.sbuf_tensor("s1_onesf", [1, 128], F32))
        q8 = [es.enter_context(nc.sbuf_tensor(f"s1_q8_{i}", [128, NKT * 512], BF16)) for i in range(2)]
        qa2 = [es.enter_context(nc.sbuf_tensor(f"s1_qa2_{i}", [2, 512], BF16)) for i in range(2)]
        ktile = [es.enter_context(nc.sbuf_tensor(f"s1_kt{i}", [128, NKT * 128], BF16)) for i in range(2)]
        ka2 = [es.enter_context(nc.sbuf_tensor(f"s1_ka2_{i}", [2, 128], BF16)) for i in range(2)]
        fcol = es.enter_context(nc.sbuf_tensor("s1_fcol", [128, 64 * 512], BF16))

        vcol = [es.enter_context(nc.sbuf_tensor(f"s1_vcol{i}", [128, 64 * 128], BF16)) for i in range(2)]
        x_sb = es.enter_context(nc.sbuf_tensor("s1_x", [128, NKT * 512], F32))
        attnT = es.enter_context(nc.sbuf_tensor("s1_attnT", [128, NKT * 512], BF16))
        rden = es.enter_context(nc.sbuf_tensor("s1_rden", [1, 512], F32))
        rb = es.enter_context(nc.sbuf_tensor("s1_rb", [128, 512], F32))
        tb = es.enter_context(nc.sbuf_tensor("s1_tb", [128, NKT * 512], BF16))
        rows = es.enter_context(nc.sbuf_tensor("s1_rows", [1, 5 * 512], F32))
        x1f = es.enter_context(nc.sbuf_tensor("s1_x1f", [128, NKT * 512], F32))
        psD = [es.enter_context(nc.psum_tensor(f"s1_psD{i}", [128, 512], F32)) for i in range(2)]
        psDen = es.enter_context(nc.psum_tensor("s1_psDen", [1, 512], F32))
        psO = [es.enter_context(nc.psum_tensor(f"s1_psO{i}", [128, 512], F32)) for i in range(2)]
        psBCa = es.enter_context(nc.psum_tensor("s1_psBCa", [128, 512], F32))
        psBCb = es.enter_context(nc.psum_tensor("s1_psBCb", [128, 512], F32))
        psET = es.enter_context(nc.psum_tensor("s1_psET", [128, 512], F32))
        sD = p.s("sD"); sE1 = p.s("sE1"); sE2 = p.s("sE2"); sOV = p.s("sOV")

        ln_bufs = dict(t_f=x_sb, tb=tb, sqb=attnT, rows=rows, onesc=onesc, onesf=onesf,
                       ps_sum=psDen[:], ps_sumsq=psBCa[0:1, :], ps_bcmu=psBCa[:],
                       ps_bcr=psBCb[:], outf=x1f, outb=None)

        def gp0(gpsimd):
            p.s("GLD").inc(gpsimd.dma_start(
                out=rr8(wo_sb, 1024),
                in_=d["wo"].rearrange("(k p) o -> p k o", p=128)), 16)
        block.gpsimd(gp0)

        def sy0(sync):
            LD.inc(sync.dma_start(out=bo_sb[:], in_=d["bo_pt"]), 16)
        block.sync(sy0)

        def v0(vector):
            wge(vector, LD)
            EV.inc(vector.memset(onesc[:], 1.0))
            EV.inc(vector.memset(onesf[:], 1.0))
        block.vector(v0)

        q8_cnt = {}
        Q8S = [p.s("Q80"), p.s("Q81")]

        def load_chunk(sync, c):
            i = c % 2
            Q8S[i].inc(sync.dma_start(out=rr8(q8[i], 512),
                                      in_=drview(d["qt8"], c * 512, (c + 1) * 512)), 16)
            Q8S[i].inc(sync.dma_start(out=qa2[i][:], in_=d["qaug"][:, c * 512:(c + 1) * 512]), 16)
            q8_cnt[c] = Q8S[i].n

        block.sync(lambda s: load_chunk(s, 0))

        kt_cnt = {}
        KTS = [p.s("KT0"), p.s("KT1")]

        def load_ktile(sync, kt_i, g):
            i = kt_i % 2
            KTS[i].inc(sync.dma_start(out=rr8(ktile[i], 128),
                                      in_=drview(d["kts8"], kt_i * 128, (kt_i + 1) * 128)), 16)
            KTS[i].inc(sync.dma_start(out=ka2[i][:], in_=d["kaug"][:, kt_i * 128:(kt_i + 1) * 128]), 16)
            kt_cnt[g] = KTS[i].n

        vc_cnt = {}
        VCS = [p.s("VC0"), p.s("VC1")]

        def load_vcol(gpsimd, dt, g):
            VCS[dt % 2].inc(gpsimd.dma_start(
                out=vcol[dt % 2].rearrange("p (k c) -> p k c", k=64),
                in_=d["vb"].rearrange("(k p) c -> p k c", p=128)[:, :, dt * 128:(dt + 1) * 128]), 16)
            vc_cnt[g] = VCS[dt % 2].n

        e1_hist = {}
        mm_o_hist = {}

        for c in range(16):
            ci = c % 2
            if c + 1 < 16:
                def syn(s, c=c):
                    wge(s, MM)
                    load_chunk(s, c + 1)
                block.sync(syn)

            def syx(s, c=c):
                wge(s, EV)
                wge(s, AC)
                LD2.inc(s.dma_start(out=rr8(x_sb, 512),
                                    in_=drview(d["xT"], c * 512, (c + 1) * 512)), 16)
            block.sync(syx)

            def sy_k0(s, c=c):
                wge(s, MM)
                load_ktile(s, 0, c * 64)
            block.sync(sy_k0)

            def gp_v01(g, c=c):
                for dt in range(2):
                    if c > 0:
                        wge_val(g, MM, mm_o_hist[(c - 1, 6 + dt)])
                    load_vcol(g, dt, (c, dt))
            block.gpsimd(gp_v01)

            for ktp in range(32):
                for kt_i in (2 * ktp, 2 * ktp + 1):
                    g = c * 64 + kt_i
                    kti = kt_i % 2
                    if kt_i + 1 < 64:
                        def syk(s, kt_i=kt_i, c=c):
                            wge(s, MM)
                            load_ktile(s, kt_i + 1, c * 64 + kt_i + 1)
                        block.sync(syk)

                    ps = psD[kt_i % 2]
                    def t_d(tensor, kti=kti, ps=ps, ci=ci, g=g, c=c):
                        wge(tensor, LD)
                        wge_val(tensor, KTS[kti], kt_cnt[g])
                        wge_val(tensor, Q8S[ci], q8_cnt[c])
                        if g >= 2:
                            wge_val(tensor, sE1, e1_hist[g - 2])
                        pairs = [(ktile[kti][:, k * 128:(k + 1) * 128], q8[ci][:, k * 512:(k + 1) * 512])
                                 for k in range(NKT)]
                        pairs.append((ka2[kti][:], qa2[ci][:]))
                        mm_group(tensor, ps[:], pairs, MM)
                    block.tensor(t_d)

                def s_e(scalar, ktp=ktp, c=c):
                    wge(scalar, MM)
                    for kt_i in (2 * ktp, 2 * ktp + 1):
                        ps = psD[kt_i % 2]
                        sE1.inc(scalar.activation(psET[:], ps[:], AF.Exp, scale=p.c(2)))
                        e1_hist[c * 64 + kt_i] = sE1.n
                        sE2.inc(scalar.activation(fcol[:, kt_i * 512:(kt_i + 1) * 512], psET[:], AF.Exp))
                block.scalar(s_e)

            def t_den(tensor):
                wge(tensor, sE2)
                mm_group(tensor, psDen[:],
                         [(onesc[:], fcol[:, k * 512:(k + 1) * 512]) for k in range(64)], MM)
            block.tensor(t_den)

            def v_den(vector):
                wge(vector, MM)
                EV.inc(vector.reciprocal(rden[:], psDen[:]))
            block.vector(v_den)

            def t_bc(tensor):
                wge(tensor, EV)
                mm_group(tensor, psBCa[:], [(onesf[:], rden[:])], MM)
            block.tensor(t_bc)

            def s_bc(scalar):
                wge(scalar, MM)
                AC.inc(scalar.copy(rb[:], psBCa[:]))
            block.scalar(s_bc)

            for dt in range(8):
                ps = psO[dt % 2]
                def t_o(tensor, dt=dt, ps=ps, c=c):
                    wge_val(tensor, VCS[dt % 2], vc_cnt[(c, dt)])
                    wge(tensor, sOV, lag=1)
                    mm_group(tensor, ps[:],
                             [(vcol[dt % 2][:, k * 128:(k + 1) * 128], fcol[:, k * 512:(k + 1) * 512])
                              for k in range(64)], MM)
                block.tensor(t_o)
                mm_o_hist[(c, dt)] = MM.n
                if dt + 2 < 8:
                    def gpv(g, dt=dt, c=c):
                        wge_val(g, MM, mm_o_hist[(c, dt)])
                        load_vcol(g, dt + 2, (c, dt + 2))
                    block.gpsimd(gpv)

                def v_o(vector, dt=dt, ps=ps):
                    wge(vector, MM)
                    wge(vector, AC)
                    sOV.inc(vector.tensor_mul(attnT[:, dt * 512:(dt + 1) * 512], ps[:], rb[:]))
                block.vector(v_o)

            if "attn_dbg" in d:
                def sy_dbg(sync, c=c):
                    wge(sync, EV)
                    ST.inc(sync.dma_start(out=drview(d["attn_dbg"], c * 512, (c + 1) * 512),
                                          in_=rr8(attnT, 512)), 16)
                block.sync(sy_dbg)

            for ot in range(8):
                ps = psD[ot % 2]
                def t_w(tensor, ot=ot, ps=ps):
                    wge(tensor, p.s("GLD"))
                    wge(tensor, EV)
                    wge(tensor, sOV)
                    wge(tensor, sE1)
                    mm_group(tensor, ps[:],
                             [(wo_sb[:, k * 1024 + ot * 128: k * 1024 + ot * 128 + 128],
                               attnT[:, k * 512:(k + 1) * 512]) for k in range(NKT)], MM)
                block.tensor(t_w)

                def s_w(scalar, ot=ot, ps=ps):
                    wge(scalar, MM)
                    wge(scalar, AC)
                    wge(scalar, ST)
                    AC.inc(scalar.activation(x1f[:, ot * 512:(ot + 1) * 512], ps[:],
                                             AF.Identity, bias=bo_sb[:, ot:ot + 1]))
                block.scalar(s_w)

                def v_w(vector, ot=ot):
                    wge(vector, AC)
                    wge(vector, LD2)
                    EV.inc(vector.tensor_add(x_sb[:, ot * 512:(ot + 1) * 512],
                                             x1f[:, ot * 512:(ot + 1) * 512],
                                             x_sb[:, ot * 512:(ot + 1) * 512]))
                block.vector(v_w)

            emit_ln(p, d, ln_bufs, 512, make_bf=False)

            def sy_out(sync, c=c):
                wge(sync, EV)
                wge(sync, AC)
                ST.inc(sync.dma_start(out=drview(d["x1f"], c * 512, (c + 1) * 512),
                                      in_=rr8(x1f, 512)), 16)
            block.sync(sy_out)

            def gp_out(g, c=c):
                wge(g, EV)
                wge(g, AC)
                ST.inc(g.dma_start(out=drview(d["x1b"], c * 512, (c + 1) * 512),
                                   in_=rr8(x1f, 512)), 16)
            block.gpsimd(gp_out)

        def sy_fin(sync):
            wge(sync, ST)
        block.sync(sy_fin)


def wge_val(eng, sem, v):
    if v > 0:
        eng.wait_ge(sem.h, v)


def emit_attn(p, d, li, xin_f, xin_b, xout_f, xout_b):
    nc, block = p.nc, p.block
    p.barrier()
    LD, GLD, ST = p.s("LD"), p.s("GLD"), p.s("ST")
    names = ["aX", "aQK", "aQE", "aRot", "aTr", "aTrE", "aSim", "aExp", "aMsk",
             "aOut", "aRcp", "aNrmE", "aTr2", "aTr2E", "aWo", "aWoE", "aPc",
             "aLnS", "aLnR", "aLnR1", "aLnB", "aLnA"]
    sm = {n: p.s(n) for n in names}
    (aX, aQK, aQE, aRot, aTr, aTrE, aSim, aExp, aMsk, aOut, aRcp, aNrmE, aTr2,
     aTr2E, aWo, aWoE, aPc, aLnS, aLnR, aLnR1, aLnB, aLnA) = (sm[n] for n in names)
    M = [dict((n, 0) for n in names + ["ST"]) for _ in range(NW + 2)]
    loads_after = {}
    qe_after = {}
    exp_after = {}
    msk_after = {}

    def mark(i):
        for n in names:
            M[i + 1][n] = sm[n].n
        M[i + 1]["ST"] = ST.n

    es = ExitStack()
    pre = f"a{li}_"
    with es:
        wqkv_sb = es.enter_context(nc.sbuf_tensor(pre + "wqkv", [128, NKT * 3072], BF16))
        woi_sb = es.enter_context(nc.sbuf_tensor(pre + "woi", [128, NKT * 1024], BF16))
        qs_sb = es.enter_context(nc.sbuf_tensor(pre + "qs", [128, DH], F32))
        ks_sb = es.enter_context(nc.sbuf_tensor(pre + "ks", [128, DH], F32))
        qsw_sb = es.enter_context(nc.sbuf_tensor(pre + "qsw", [128, DH], F32))
        ksw_sb = es.enter_context(nc.sbuf_tensor(pre + "ksw", [128, DH], F32))
        g_sb = es.enter_context(nc.sbuf_tensor(pre + "g", [128, 8], F32))
        lb_sb = es.enter_context(nc.sbuf_tensor(pre + "lb", [128, 8], F32))
        ident = es.enter_context(nc.sbuf_tensor(pre + "id", [128, 128], BF16))
        triA = es.enter_context(nc.sbuf_tensor(pre + "triA", [128, 128], BF16))
        triB = es.enter_context(nc.sbuf_tensor(pre + "triB", [128, 128], BF16))
        tab_t = es.enter_context(nc.sbuf_tensor(pre + "tab", [128, 4 * 2 * DH], F32))
        tab = [tab_t, tab_t]
        x_sb = [es.enter_context(nc.sbuf_tensor(pre + f"x{i}", [128, NKT * 257], F32)) for i in range(2)]
        xb_sb = [es.enter_context(nc.sbuf_tensor(pre + f"xbw{i}", [128, NKT * 257], BF16)) for i in range(2)]
        q_tm = es.enter_context(nc.sbuf_tensor(pre + "qtm", [128, 1024], BF16))
        k_tm = es.enter_context(nc.sbuf_tensor(pre + "ktm", [128, 1024], BF16))
        qn_sb = es.enter_context(nc.sbuf_tensor(pre + "qn", [128, 1024], BF16))
        tmp1 = es.enter_context(nc.sbuf_tensor(pre + "tmp1", [128, 1024], BF16))
        qrot = es.enter_context(nc.sbuf_tensor(pre + "qrot", [128, 1024], BF16))
        krot = es.enter_context(nc.sbuf_tensor(pre + "krot", [128, 1024], BF16))
        ssq = es.enter_context(nc.sbuf_tensor(pre + "ssq", [128, 32], F32))
        rr_ = es.enter_context(nc.sbuf_tensor(pre + "rr", [128, 32], F32))
        csq = es.enter_context(nc.sbuf_tensor(pre + "csq", [128, 4 * DH], F32))
        qT = es.enter_context(nc.sbuf_tensor(pre + "qT", [64, H * 256], BF16))
        kT = [es.enter_context(nc.sbuf_tensor(pre + f"kT{i}", [64, H * 256], BF16)) for i in range(2)]
        vA = [es.enter_context(nc.sbuf_tensor(pre + f"vA{i}", [128, 2 * H * 65], BF16)) for i in range(3)]
        emT = es.enter_context(nc.sbuf_tensor(pre + "emT", [128, H * 768], BF16))
        attnout = es.enter_context(nc.sbuf_tensor(pre + "ao", [128, 2 * 1024], BF16))
        f_sb = es.enter_context(nc.sbuf_tensor(pre + "f", [128, 2 * H], F32))
        aT = es.enter_context(nc.sbuf_tensor(pre + "aT", [128, NKT * 256], BF16))
        xo_t = es.enter_context(nc.sbuf_tensor(pre + "xo", [128, NKT * 258], F32))
        xo = [xo_t, xo_t]
        tb_a = es.enter_context(nc.sbuf_tensor(pre + "tba", [128, NKT * 256], BF16))
        sqb_a = es.enter_context(nc.sbuf_tensor(pre + "sqba", [128, NKT * 256], BF16))
        xn_sb = es.enter_context(nc.sbuf_tensor(pre + "xn", [128, NKT * 256], BF16))
        u_sb = es.enter_context(nc.sbuf_tensor(pre + "u", [128, 256], F32))
        rows = es.enter_context(nc.sbuf_tensor(pre + "rows", [1, 5 * 256], F32))
        onesc = es.enter_context(nc.sbuf_tensor(pre + "onesc", [128, 1], BF16))
        onesf = es.enter_context(nc.sbuf_tensor(pre + "onesf", [1, 128], F32))
        psMM = [es.enter_context(nc.psum_tensor(pre + f"psMM{i}", [128, 512], F32)) for i in range(2)]
        psT = es.enter_context(nc.psum_tensor(pre + "psT", [128, 1024], BF16))
        psS = [es.enter_context(nc.psum_tensor(pre + f"psS{i}", [128, 512], F32)) for i in range(2)]
        psG = [es.enter_context(nc.psum_tensor(pre + f"psG{i}", [128, 512], F32)) for i in range(2)]
        psT2 = es.enter_context(nc.psum_tensor(pre + "psT2", [128, 1024], BF16))
        psTs = [psT, psT2]

        def gp0(gpsimd):
            GLD.inc(gpsimd.dma_start(out=rr8(wqkv_sb, 3072),
                                     in_=d["wqkv"][li].rearrange("(k p) o -> p k o", p=128)), 16)
            GLD.inc(gpsimd.dma_start(out=rr8(woi_sb, 1024),
                                     in_=d["wo_i"][li].rearrange("(k p) o -> p k o", p=128)), 16)
        block.gpsimd(gp0)

        def sy0(sync):
            for dst, src_ in [(qs_sb, d["qs_bc"][li]), (ks_sb, d["ks_bc"][li]),
                              (qsw_sb, d["qs_sw"][li]), (ksw_sb, d["ks_sw"][li]),
                              (g_sb, d["g_pt"][li]), (lb_sb, d["lb_pt"][li]),
                              (ident, d["ident"]), (triA, d["triA"]), (triB, d["triB"])]:
                LD.inc(sync.dma_start(out=dst[:], in_=src_), 16)
        block.sync(sy0)

        def v0(vector):
            wge(vector, LD)
            aPc.inc(vector.memset(onesc[:], 1.0))
            wge(vector, aPc)
            aPc.inc(vector.memset(onesf[:], 1.0))
            for i in range(3):
                wge(vector, aPc)
                aPc.inc(vector.memset(
                    vA[i].rearrange("p (c h e) -> p c h e", c=2, h=H)[:, :, :, 64:65], 1.0))
            wge(vector, aPc)
            aPc.inc(vector.memset(rr8(x_sb[0], 257)[:, :, 0:1], 0.0))
            wge(vector, aPc)
            aPc.inc(vector.memset(rr8(xb_sb[0], 257)[:, :, 0:1], 0.0))
            wge(vector, aPc)
            aPc.inc(vector.memset(rr8(xo[0], 258)[:, :, 0:1], 0.0))
        block.vector(v0)
        init_pc = aPc.n

        aXP = [p.s(f"aX{li}_0"), p.s(f"aX{li}_1")]

        def ph_loads(j):
            if j >= NW:
                return

            def sy_x(sync, j=j):
                aX = aXP[j % 2]
                w0 = j * W
                wge(sync, aQK)
                wge(sync, aWoE)
                wge(sync, aRot)
                wge_val(sync, aPc, init_pc)
                if j == 0:
                    aX.inc(sync.dma_start(out=rr8(x_sb[0], 257)[:, :, 1:257],
                                          in_=drview(xin_f, 0, 256)), 16)
                    aX.inc(sync.dma_start(out=rr8(xb_sb[0], 257)[:, :, 1:257],
                                          in_=drview(xin_b, 0, 256)), 16)
                else:
                    aX.inc(sync.dma_start(out=rr8(x_sb[j % 2], 257),
                                          in_=drview(xin_f, w0 - 1, w0 + 256)), 16)
                    aX.inc(sync.dma_start(out=rr8(xb_sb[j % 2], 257),
                                          in_=drview(xin_b, w0 - 1, w0 + 256)), 16)
                for ti, nm in enumerate(["cs_q", "ss_q", "cs_k", "ss_k"]):
                    aX.inc(sync.dma_start(
                        out=tab[j % 2].rearrange("p (ti tt e) -> p ti tt e", ti=4, tt=2)[:, ti],
                        in_=d[nm].rearrange("(tt p) e -> p tt e", p=128)[:, 2 * j:2 * j + 2, :]), 16)
            block.sync(sy_x)
            loads_after[j] = aXP[j % 2].n

        def ph_qkv(j):
            if j >= NW:
                return
            qlvl = globals().get("_DBG_QLVL", 99)
            xbw = xb_sb[j % 2]
            for tt in range(2):
                def tview(kt, tt=tt, xbw=xbw):
                    off = (1 if kt < 4 else 0) + tt * 128
                    return xbw[:, kt * 257 + off: kt * 257 + off + 128]

                for ch in range(6):
                    ps = psMM[ch % 2]

                    def t_qkv(tensor, ps=ps, ch=ch, tt=tt, j=j):
                        wge(tensor, GLD)
                        wge_val(tensor, aXP[j % 2], loads_after[j])
                        wge(tensor, aWoE)
                        wge(tensor, aQE, lag=1)
                        mm_group(tensor, ps[:],
                                 [(tview(k), wqkv_sb[:, k * 3072 + ch * 512: k * 3072 + ch * 512 + 512])
                                  for k in range(NKT)], aQK)
                    block.tensor(t_qkv)

                    def v_ev(vector, ps=ps, ch=ch, tt=tt, j=j):
                        wge(vector, aQK)
                        if ch < 2:
                            aQE.inc(vector.tensor_copy(q_tm[:, ch * 512:(ch + 1) * 512], ps[:]))
                        elif ch < 4:
                            aQE.inc(vector.tensor_copy(k_tm[:, (ch - 2) * 512:(ch - 1) * 512], ps[:]))
                        else:
                            if j >= 2:
                                wge_val(vector, aOut, M[j - 1]["aOut"])
                            h0 = (ch - 4) * 8
                            aQE.inc(vector.tensor_copy(
                                vA[j % 3].rearrange("p (c h e) -> p c h e", c=2, h=H)[:, tt, h0:h0 + 8, 0:64],
                                ps.rearrange("p (h e) -> p h e", e=64)))
                    block.vector(v_ev)

                if qlvl < 2:
                    continue
                def v_ss(vector, tt=tt, j=j):
                    wge(vector, aQE)
                    aRot.inc(vector.tensor_mul(tmp1[:], q_tm[:], q_tm[:]))
                    wge(vector, aRot)
                    aRot.inc(vector.tensor_reduce(ssq[:, 0:16],
                                                  tmp1.rearrange("p (h e) -> p h e", e=64),
                                                  axis=AX.X, op=ALU.add))
                    wge(vector, aRot)
                    aRot.inc(vector.tensor_mul(tmp1[:], k_tm[:], k_tm[:]))
                    wge(vector, aRot)
                    aRot.inc(vector.tensor_reduce(ssq[:, 16:32],
                                                  tmp1.rearrange("p (h e) -> p h e", e=64),
                                                  axis=AX.X, op=ALU.add))
                    wge_val(vector, aXP[j % 2], loads_after[j])
                    t3 = tab[j % 2].rearrange("p (ti tt e) -> p ti tt e", ti=4, tt=2)
                    c4 = csq.rearrange("p (ti e) -> p ti e", ti=4)
                    aRot.inc(vector.tensor_mul(c4[:, 0], t3[:, 0, tt], qs_sb[:]))
                    aRot.inc(vector.tensor_mul(c4[:, 1], t3[:, 1, tt], qsw_sb[:]))
                    aRot.inc(vector.tensor_mul(c4[:, 2], t3[:, 2, tt], ks_sb[:]))
                    aRot.inc(vector.tensor_mul(c4[:, 3], t3[:, 3, tt], ksw_sb[:]))
                block.vector(v_ss)

                def s_rr(scalar):
                    wge(scalar, aRot)
                    aRcp.inc(scalar.activation(rr_[:], ssq[:], AF.Ln))
                    wge(scalar, aRcp)
                    aRcp.inc(scalar.activation(rr_[:], rr_[:], AF.Exp, scale=p.c(1)))
                block.scalar(s_rr)

                def v_rrmin(vector):
                    wge(vector, aRcp)
                    aRot.inc(vector.tensor_scalar_min(rr_[:], rr_[:], 1.0e12))
                block.vector(v_rrmin)

                for which in (range(2) if qlvl >= 3 else []):
                    src = q_tm if which == 0 else k_tm
                    dst = qrot if which == 0 else krot
                    rr_sl = rr_[:, 0:16] if which == 0 else rr_[:, 16:32]
                    cs_i, ss_i = (0, 1) if which == 0 else (2, 3)

                    def v_rot(vector, src=src, dst=dst, rr_sl=rr_sl, cs_i=cs_i, ss_i=ss_i):
                        wge(vector, aRot)
                        wge(vector, aTr)
                        c4 = csq.rearrange("p (ti e) -> p ti e", ti=4)
                        h3 = lambda t: t.rearrange("p (h e) -> p h e", e=64)
                        h4 = lambda t: t.rearrange("p (h a c) -> p h a c", a=2, c=32)
                        aRot.inc(vector.tensor_mul(
                            h3(qn_sb), h3(src),
                            rr_sl.rearrange("p (h e) -> p h e", e=1).broadcast_to((128, 16, 64))))
                        wge(vector, aRot)
                        aRot.inc(vector.tensor_mul(
                            h3(tmp1), h3(qn_sb),
                            c4[:, cs_i].rearrange("p (o e) -> p o e", o=1).broadcast_to((128, 16, 64))))
                        wge(vector, aRot)
                        ss_t = c4[:, ss_i].rearrange("p (a c) -> p a c", a=2)
                        aRot.inc(vector.tensor_mul(
                            h4(dst)[:, :, 0, :], h4(qn_sb)[:, :, 1, :],
                            ss_t[:, 0:1, :].broadcast_to((128, 16, 32))))
                        wge(vector, aRot)
                        aRot.inc(vector.tensor_mul(
                            h4(dst)[:, :, 1, :], h4(qn_sb)[:, :, 0, :],
                            ss_t[:, 1:2, :].broadcast_to((128, 16, 32))))
                        wge(vector, aRot)
                        aRot.inc(vector.tensor_add(dst[:], dst[:], tmp1[:]))
                    block.vector(v_rot)

                for which in (range(2) if qlvl >= 4 else []):
                    src = qrot if which == 0 else krot
                    dstT = qT if which == 0 else kT[j % 2]
                    for bb in range(4):
                        def t_tr(tensor, src=src, bb=bb, j=j):
                            wge(tensor, aRot)
                            wge(tensor, aSim)
                            wge(tensor, aTr2E)
                            wge(tensor, aTrE, lag=1)
                            inst = None
                            for hh in range(4):
                                h = bb * 4 + hh
                                inst = tensor.transpose(psTs[bb % 2][0:64, hh * 128:(hh + 1) * 128],
                                                        src[:, h * 64:(h + 1) * 64], ident[:])
                            aTr.inc(inst)
                        block.tensor(t_tr)

                        def v_trev(vector, dstT=dstT, bb=bb, tt=tt):
                            wge(vector, aTr)
                            aTrE.inc(vector.tensor_copy(
                                dstT[0:64, tt * 2048 + bb * 512: tt * 2048 + (bb + 1) * 512],
                                psTs[bb % 2][0:64, 0:512]))
                        block.vector(v_trev)
            qe_after[j] = aQE.n

        exp_cnt = {}

        def ph_sims(j):
            for h in range(H):
                def t_sim(tensor, h=h, j=j):
                    wge(tensor, aTrE)
                    if h >= 2:
                        wge_val(tensor, aExp, exp_cnt[(j, h - 2)])
                    elif j > 0:
                        wge_val(tensor, aExp, exp_cnt[(j - 1, 14 + h)])
                    if h < 2:
                        wge(tensor, aLnR1)
                        wge(tensor, aNrmE)
                        wge(tensor, aLnA)
                    qh_full = qT.rearrange("p (t h c) -> p t h c", t=2, c=128)[:, :, h, :]
                    qh0 = qT[0:64, h * 128: h * 128 + 128]
                    qh1 = qT[0:64, 2048 + h * 128: 2048 + h * 128 + 128]
                    pw_ = 1 - (j % 2)
                    cw_ = j % 2
                    if j > 0:
                        tensor.matmul(psS[h % 2][:, 0:128], kT[pw_][0:64, h * 128: h * 128 + 128],
                                      qh0, start=True, stop=True)
                        tensor.matmul(psS[h % 2][:, 128:384], kT[pw_][0:64, 2048 + h * 128: 2048 + h * 128 + 128],
                                      qh_full, start=True, stop=True)
                    tensor.matmul(psG[h % 2][:, 0:256], kT[cw_][0:64, h * 128: h * 128 + 128],
                                  qh_full, start=True, stop=True)
                    aSim.inc(tensor.matmul(psG[h % 2][:, 256:384], kT[cw_][0:64, 2048 + h * 128: 2048 + h * 128 + 128],
                                           qh1, start=True, stop=True))
                block.tensor(t_sim)

                if h % 2 == 0:
                    continue

                def s_exp(scalar, h=h, j=j):
                    wge(scalar, aSim)
                    if h == 1:
                        wge_val(scalar, aOut, M[j]["aOut"])
                    for hh in (h - 1, h):
                        base = hh * 768
                        if j > 0:
                            aExp.inc(scalar.activation(emT[:, base: base + 384],
                                                       psS[hh % 2][:, 0:384], AF.Exp))
                        aExp.inc(scalar.activation(emT[:, base + 384: base + 768],
                                                   psG[hh % 2][:, 0:384], AF.Exp))
                        exp_cnt[(j, hh)] = aExp.n
                block.scalar(s_exp)

                def g_mask(vector, h=h, j=j):
                    wge(vector, aExp)
                    e6 = emT.rearrange("p (s c) -> p s c", c=128)
                    for hh in (h - 1, h):
                        b6 = hh * 6
                        if j > 0:
                            vector.tensor_mul(
                                e6[:, b6:b6 + 3:2, :], e6[:, b6:b6 + 3:2, :],
                                triA.rearrange("p (o c) -> p o c", o=1).broadcast_to((128, 2, 128)))
                            vector.wait_ge(aMsk.h, aMsk.n)
                        aMsk.inc(vector.tensor_mul(
                            e6[:, b6 + 3:b6 + 6:2, :], e6[:, b6 + 3:b6 + 6:2, :],
                            triB.rearrange("p (o c) -> p o c", o=1).broadcast_to((128, 2, 128))))
                block.vector(g_mask)
            exp_after[j] = aExp.n
            msk_after[j] = aMsk.n

        def ph_outs(j):
            for h in range(H):
                for qt in range(2):
                    ps_att = psS[h % 2] if qt == 0 else psG[h % 2]

                    def t_out(tensor, h=h, qt=qt, ps_att=ps_att, j=j):
                        if h < 2:
                            wge(tensor, aLnA)
                        wge_val(tensor, aMsk, msk_after[j])
                        wge_val(tensor, aExp, exp_after[j])
                        wge_val(tensor, aQE, qe_after.get(j, aQE.n))
                        wge(tensor, aNrmE, lag=2)
                        base = h * 768
                        pairs = []
                        va_c = vA[j % 3].rearrange("p (c h e) -> p c h e", c=2, h=H)
                        va_p = vA[(j - 1) % 3].rearrange("p (c h e) -> p c h e", c=2, h=H)
                        if qt == 0:
                            if j > 0:
                                pairs.append((emT[:, base + 0: base + 128], va_p[:, 0, h, :]))
                                pairs.append((emT[:, base + 128: base + 256], va_p[:, 1, h, :]))
                            pairs.append((emT[:, base + 384: base + 512], va_c[:, 0, h, :]))
                        else:
                            if j > 0:
                                pairs.append((emT[:, base + 256: base + 384], va_p[:, 1, h, :]))
                            pairs.append((emT[:, base + 512: base + 640], va_c[:, 0, h, :]))
                            pairs.append((emT[:, base + 640: base + 768], va_c[:, 1, h, :]))
                        mm_group(tensor, ps_att[:, 384:449], pairs, aOut)
                    block.tensor(t_out)

                def v_rcp(vector, h=h):
                    wge(vector, aOut)
                    wge(vector, aNrmE)
                    aRcp.inc(vector.reciprocal(f_sb[:, 2 * h:2 * h + 1], psS[h % 2][:, 448:449]))
                    wge(vector, aRcp)
                    aRcp.inc(vector.reciprocal(f_sb[:, 2 * h + 1:2 * h + 2], psG[h % 2][:, 448:449]))
                block.vector(v_rcp)

                def s_nrm(vector, h=h):
                    wge(vector, aRcp)
                    wge(vector, aTr2)
                    ao = attnout.rearrange("p (q h e) -> p q h e", q=2, e=64)
                    aNrmE.inc(vector.tensor_scalar_mul(ao[:, 0, h, :], psS[h % 2][:, 384:448],
                                                       f_sb[:, 2 * h:2 * h + 1]))
                    wge(vector, aNrmE)
                    aNrmE.inc(vector.tensor_scalar_mul(ao[:, 1, h, :], psG[h % 2][:, 384:448],
                                                       f_sb[:, 2 * h + 1:2 * h + 2]))
                block.vector(s_nrm)

        def ph_tr2(j):
            for qt in range(2):
                for bb in range(2):
                    g2 = qt * 2 + bb
                    def t_tr2(tensor, qt=qt, bb=bb, g2=g2):
                        wge(tensor, aNrmE)
                        wge(tensor, aTrE)
                        wge(tensor, aTr2E, lag=1)
                        inst = None
                        for cc in range(4):
                            ct = bb * 4 + cc
                            inst = tensor.transpose(psTs[g2 % 2][:, cc * 128:(cc + 1) * 128],
                                                    attnout[:, qt * 1024 + ct * 128: qt * 1024 + ct * 128 + 128],
                                                    ident[:])
                        aTr2.inc(inst)
                    block.tensor(t_tr2)

                    def v_tr2e(vector, qt=qt, bb=bb, g2=g2):
                        wge(vector, aTr2)
                        aTr2E.inc(vector.tensor_copy(
                            aT[:, qt * 1024 + bb * 512: qt * 1024 + (bb + 1) * 512],
                            psTs[g2 % 2][:, 0:512]))
                    block.vector(v_tr2e)

        def ph_wo(j):
            cw_ = j % 2

            def v_pc(vector, j=j, cw_=cw_):
                if j > 0:
                    wge(vector, aWoE)
                    wge(vector, aLnS)
                    aPc.inc(vector.tensor_copy(rr8(xo[cw_], 258)[:, :, 0:1],
                                               rr8(xo[cw_], 258)[:, :, 256:257]))
                else:
                    aPc.inc(vector.engine_nop())
            block.vector(v_pc)

            for ot in range(8):
                ps = psMM[ot % 2]

                def t_wo(tensor, ot=ot, ps=ps, j=j):
                    wge(tensor, aTr2E)
                    wge_val(tensor, aQE, qe_after.get(j + 1, aQE.n))
                    wge(tensor, aWoE, lag=1)
                    aT4 = aT.rearrange("p (q c e) -> p q c e", q=2, e=128)
                    mm_group(tensor, ps[:, 0:256],
                             [(woi_sb[:, k * 1024 + ot * 128: k * 1024 + ot * 128 + 128],
                               aT4[:, :, k, :]) for k in range(NKT)], aWo)
                block.tensor(t_wo)

                def v_wo(vector, ot=ot, ps=ps, j=j, cw_=cw_):
                    wge(vector, aWo)
                    wge(vector, aPc)
                    wge_val(vector, ST, M[j]["ST"])
                    wge(vector, aLnA)
                    aWoE.inc(vector.tensor_add(xo[cw_][:, ot * 258 + 1: ot * 258 + 257],
                                               ps[:, 0:256],
                                               x_sb[j % 2][:, ot * 257 + 1: ot * 257 + 257]))
                block.vector(v_wo)

        def ph_ln(j):
            cw_ = j % 2
            T = 256

            def v_casts(vector, j=j, cw_=cw_):
                wge(vector, aWoE)
                wge_val(vector, ST, M[j]["ST"])
                aLnS.inc(vector.tensor_copy(rr8(tb_a, 256)[:, 0:4, :],
                                            rr8(xo[cw_], 258)[:, 0:4, 1:257]))
                wge(vector, aLnS)
                aLnS.inc(vector.tensor_copy(rr8(tb_a, 256)[:, 4:8, :],
                                            rr8(xo[cw_], 258)[:, 4:8, 0:256]))
                wge(vector, aLnS)
                aLnS.inc(vector.tensor_mul(sqb_a[:], tb_a[:], tb_a[:]))
            block.vector(v_casts)

            def t_lnrows(tensor, j=j):
                wge(tensor, aLnS)
                wge_val(tensor, aExp, exp_after[j])
                wge(tensor, aNrmE)
                wge(tensor, aRcp)
                mm_group(tensor, psS[0][0:1, 0:256],
                         [(onesc[:], tb_a[:, k * 256:(k + 1) * 256]) for k in range(NKT)], aLnR)
                mm_group(tensor, psS[1][0:1, 0:256],
                         [(onesc[:], sqb_a[:, k * 256:(k + 1) * 256]) for k in range(NKT)], aLnR)
            block.tensor(t_lnrows)

            def s_lr(scalar):
                wge(scalar, aLnR)
                aLnR1.inc(scalar.activation(rows[:, 0:T], psS[0][0:1, 0:256], AF.Identity, scale=p.c1(4)))
                wge(scalar, aLnR1)
                aLnR1.inc(scalar.activation(rows[:, T:2 * T], psS[1][0:1, 0:256], AF.Identity, scale=p.c1(4)))
            block.scalar(s_lr)

            def v_lr(vector):
                wge(vector, aLnR1)
                aLnB.inc(vector.tensor_mul(rows[:, 2 * T:3 * T], rows[:, 0:T], rows[:, 0:T]))
                wge(vector, aLnB)
                aLnB.inc(vector.tensor_sub(rows[:, 2 * T:3 * T], rows[:, T:2 * T], rows[:, 2 * T:3 * T]))
            block.vector(v_lr)

            def s_lr2(scalar):
                wge(scalar, aLnB)
                aLnB.inc(scalar.activation(rows[:, 3 * T:4 * T], rows[:, 2 * T:3 * T], AF.Ln, bias=p.c1(0)))
                wge(scalar, aLnB)
                aLnB.inc(scalar.activation(rows[:, 3 * T:4 * T], rows[:, 3 * T:4 * T], AF.Exp, scale=p.c1(1)))
                wge(scalar, aLnB)
                aLnB.inc(scalar.activation(rows[:, 4 * T:5 * T], rows[:, 0:T], AF.Identity, scale=p.c1(2)))
            block.scalar(s_lr2)

            def t_lnbc(tensor, j=j):
                wge(tensor, aLnB)
                wge_val(tensor, aLnA, M[j]["aLnA"])
                mm_group(tensor, psS[0][:, 256:512], [(onesf[:], rows[:, 4 * T:5 * T])], aLnB)
                mm_group(tensor, psS[1][:, 256:512], [(onesf[:], rows[:, 3 * T:4 * T])], aLnB)
            block.tensor(t_lnbc)

            for kk in range(NKT):
                def v_lap(vector, kk=kk, cw_=cw_, j=j):
                    wge(vector, aLnB)
                    wge(vector, aLnA)
                    if kk == 0:
                        wge_val(vector, ST, M[j]["ST"])
                    off = (1 if kk < 4 else 0)
                    src = xo[cw_][:, kk * 258 + off: kk * 258 + off + 256]
                    aLnA.inc(vector.tensor_add(u_sb[:], src, psS[0][:, 256:512]))
                    wge(vector, aLnA)
                    aLnA.inc(vector.tensor_mul(u_sb[:], u_sb[:], psS[1][:, 256:512]))
                    wge(vector, aLnA)
                    aLnA.inc(vector.tensor_scalar(xn_sb[:, kk * 256:(kk + 1) * 256], u_sb[:],
                                                  g_sb[:, kk:kk + 1], lb_sb[:, kk:kk + 1],
                                                  op0=ALU.mult, op1=ALU.add))
                block.vector(v_lap)

            def sy_st(sync, j=j, cw_=cw_):
                w0 = j * W
                wge(sync, aLnA)
                wge(sync, aLnS)
                ST.inc(sync.dma_start(out=drview(xout_f, w0, w0 + 256),
                                      in_=rr8(xo[cw_], 258)[:, :, 1:257]), 16)
                ST.inc(sync.dma_start(out=drview(d["xnb"], w0, w0 + 256), in_=rr8(xn_sb, 256)), 16)
            block.sync(sy_st)

            def gp_st(g, j=j, cw_=cw_):
                w0 = j * W
                wge(g, aLnA)
                wge(g, aLnS)
                ST.inc(g.dma_start(out=drview(xout_b, w0, w0 + 256),
                                   in_=rr8(xo[cw_], 258)[:, :, 1:257]), 16)
            block.gpsimd(gp_st)

        nwin = globals().get("_DBG_NWIN", NW)
        nph = globals().get("_DBG_NPH", 99)
        if nph >= -1:
            ph_loads(0)
        for w in range(nwin):
            if w == 0 and nph >= 0:
                ph_qkv(0)
                ph_loads(1)
            if nph >= 1: ph_sims(w)
            if nph >= 2: ph_qkv(w + 1)
            if nph >= 3: ph_outs(w)
            if nph >= 4: ph_tr2(w)
            if nph >= 5: ph_wo(w)
            if nph >= 6: ph_loads(w + 2)
            if nph >= 7: ph_ln(w)
            mark(w)

        def sy_fin(sync):
            wge(sync, ST)
        block.sync(sy_fin)


def emit_f1(p, d, li):
    nc, block = p.nc, p.block
    p.barrier()
    LD, LD2, MM, EV, AC, ST = (p.s(x) for x in ["LD", "LD2", "MM", "EV", "AC", "ST"])
    es = ExitStack()
    pre = f"f1_{li}_"
    with es:
        w1_sb = es.enter_context(nc.sbuf_tensor(pre + "w1", [128, NKT * DI2], BF16))
        b1a_sb = es.enter_context(nc.sbuf_tensor(pre + "b1a", [128, NAT], F32))
        b1g_sb = es.enter_context(nc.sbuf_tensor(pre + "b1g", [128, NAT], F32))
        xn = [es.enter_context(nc.sbuf_tensor(pre + f"xn{i}", [128, NKT * 512], BF16)) for i in range(2)]
        gg = es.enter_context(nc.sbuf_tensor(pre + "gg", [128, 512], F32))
        h_sb = [es.enter_context(nc.sbuf_tensor(pre + f"h{i}", [128, 512], BF16)) for i in range(2)]
        psa = [es.enter_context(nc.psum_tensor(pre + f"psa{i}", [128, 512], F32)) for i in range(2)]
        psg = [es.enter_context(nc.psum_tensor(pre + f"psg{i}", [128, 512], F32)) for i in range(2)]

        def gp0(gpsimd):
            p.s("GLD").inc(gpsimd.dma_start(out=rr8(w1_sb, DI2),
                                    in_=d["w1"][li].rearrange("(k p) o -> p k o", p=128)), 16)
        block.gpsimd(gp0)

        def sy0(sync):
            LD.inc(sync.dma_start(out=b1a_sb[:], in_=d["b1a_pt"][li]), 16)
            LD.inc(sync.dma_start(out=b1g_sb[:], in_=d["b1g_pt"][li]), 16)
        block.sync(sy0)

        xn_cnt = {}
        XNS = [p.s("XN0"), p.s("XN1")]

        def load_blk(sync, blk):
            XNS[blk % 2].inc(sync.dma_start(out=rr8(xn[blk % 2], 512),
                                            in_=drview(d["xnb"], blk * 512, (blk + 1) * 512)), 16)
            xn_cnt[blk] = XNS[blk % 2].n

        block.sync(lambda s: load_blk(s, 0))

        for blk in range(16):
            xs = xn[blk % 2]
            if blk + 1 < 16:
                def syn(s, blk=blk):
                    wge(s, MM)
                    load_blk(s, blk + 1)
                block.sync(syn)

            for at in range(NAT):
                aoff, asz = ATILES[at]
                pa = psa[at % 2]
                pg = psg[at % 2]
                def t_a(tensor, aoff=aoff, asz=asz, pa=pa, pg=pg, xs=xs, blk=blk):
                    wge(tensor, LD)
                    wge(tensor, p.s("GLD"))
                    wge_val(tensor, XNS[blk % 2], xn_cnt[blk])
                    wge(tensor, EV, lag=1)
                    wge(tensor, AC, lag=1)
                    mm_group(tensor, pa[0:asz, :],
                             [(w1_sb[:, k * DI2 + aoff: k * DI2 + aoff + asz],
                               xs[:, k * 512:(k + 1) * 512]) for k in range(NKT)], MM)
                    mm_group(tensor, pg[0:asz, :],
                             [(w1_sb[:, k * DI2 + DI + aoff: k * DI2 + DI + aoff + asz],
                               xs[:, k * 512:(k + 1) * 512]) for k in range(NKT)], MM)
                block.tensor(t_a)

                def s_g(scalar, at=at, asz=asz, pg=pg):
                    wge(scalar, MM)
                    wge(scalar, EV)
                    AC.inc(scalar.activation(gg[0:asz, :], pg[0:asz, :], AF.Gelu,
                                             bias=b1g_sb[0:asz, at:at + 1]))
                block.scalar(s_g)

                hb_ = h_sb[at % 2]
                def v_h(vector, at=at, asz=asz, pa=pa, hb_=hb_):
                    wge(vector, AC)
                    wge(vector, ST)
                    EV.inc(vector.scalar_tensor_tensor(hb_[0:asz, :], pa[0:asz, :],
                                                       b1a_sb[0:asz, at:at + 1], gg[0:asz, :],
                                                       op0=ALU.add, op1=ALU.mult))
                block.vector(v_h)

                def sy_h(sync, at=at, asz=asz, blk=blk, hb_=hb_):
                    wge(sync, EV)
                    ST.inc(sync.dma_start(
                        out=d["hb"][at * 128: at * 128 + asz, blk * 512:(blk + 1) * 512],
                        in_=hb_[0:asz, :]), 16)
                block.sync(sy_h)

        def sy_fin(sync):
            wge(sync, ST)
        block.sync(sy_fin)


def emit_f2(p, d, li, xres_f, xout_f, xout_b):
    nc, block = p.nc, p.block
    p.barrier()
    LD, LD2, MM, EV, AC, ST = (p.s(x) for x in ["LD", "LD2", "MM", "EV", "AC", "ST"])
    es = ExitStack()
    pre = f"f2_{li}_"
    TB = 256
    with es:
        w2_sb = es.enter_context(nc.sbuf_tensor(pre + "w2", [128, NAT * 1024], BF16))
        b2_sb = es.enter_context(nc.sbuf_tensor(pre + "b2", [128, 8], F32))
        hs = [es.enter_context(nc.sbuf_tensor(pre + f"hs{i}", [128, NAT * TB], BF16)) for i in range(2)]
        xr = [es.enter_context(nc.sbuf_tensor(pre + f"xr{i}", [128, NKT * TB], F32)) for i in range(2)]
        xof = es.enter_context(nc.sbuf_tensor(pre + "xof", [128, NKT * TB], F32))
        ps2 = [es.enter_context(nc.psum_tensor(pre + f"ps{i}", [128, TB], F32)) for i in range(2)]

        def gp0(gpsimd):
            p.s("GLD").inc(gpsimd.dma_start(
                out=w2_sb.rearrange("p (k o) -> p k o", k=NAT)[:, 0:21, :],
                in_=d["w2"][li][0:2688].rearrange("(k p) o -> p k o", p=128)), 16)
            p.s("GLD").inc(gpsimd.dma_start(
                out=w2_sb[0:42, 21 * 1024: 22 * 1024],
                in_=d["w2"][li][2688:2730, :]), 16)
        block.gpsimd(gp0)

        def sy0(sync):
            LD.inc(sync.dma_start(out=b2_sb[:], in_=d["b2_pt"][li]), 16)
        block.sync(sy0)

        hs_cnt = {}
        HSS = [p.s("HS0"), p.s("HS1")]

        def load_blk(sync, gpsimd_unused, blk):
            i = blk % 2
            HSS[i].inc(sync.dma_start(
                out=hs[i].rearrange("p (k t) -> p k t", k=NAT),
                in_=d["hb"].rearrange("(k p) t -> p k t", p=128)[:, :, blk * TB:(blk + 1) * TB]), 16)
            HSS[i].inc(sync.dma_start(out=rr8(xr[i], TB),
                                      in_=drview(xres_f, blk * TB, (blk + 1) * TB)), 16)
            hs_cnt[blk] = HSS[i].n

        block.sync(lambda s: load_blk(s, None, 0))

        for blk in range(32):
            i = blk % 2
            if blk + 1 < 32:
                def syn(s, blk=blk):
                    wge(s, MM)
                    wge(s, EV)
                    load_blk(s, None, blk + 1)
                block.sync(syn)

            for ot in range(8):
                ps = ps2[ot % 2]
                def t_m(tensor, ot=ot, ps=ps, i=i, blk=blk):
                    wge(tensor, LD)
                    wge(tensor, p.s("GLD"))
                    wge_val(tensor, HSS[i], hs_cnt[blk])
                    wge(tensor, EV, lag=1)
                    pairs = []
                    for at in range(NAT):
                        asz = ATILES[at][1]
                        pairs.append((w2_sb[0:asz, at * 1024 + ot * 128: at * 1024 + ot * 128 + 128],
                                      hs[i][0:asz, at * TB:(at + 1) * TB]))
                    mm_group(tensor, ps[:], pairs, MM)
                block.tensor(t_m)

                def v_o(vector, ot=ot, ps=ps, i=i):
                    wge(vector, MM)
                    wge(vector, ST)
                    EV.inc(vector.scalar_tensor_tensor(xof[:, ot * TB:(ot + 1) * TB], ps[:],
                                                       b2_sb[:, ot:ot + 1],
                                                       xr[i][:, ot * TB:(ot + 1) * TB],
                                                       op0=ALU.add, op1=ALU.add))
                block.vector(v_o)

            def sy_st(sync, blk=blk):
                wge(sync, EV)
                ST.inc(sync.dma_start(out=drview(xout_f, blk * TB, (blk + 1) * TB),
                                      in_=rr8(xof, TB)), 16)
            block.sync(sy_st)

            def gp_st(g, blk=blk):
                wge(g, EV)
                ST.inc(g.dma_start(out=drview(xout_b, blk * TB, (blk + 1) * TB),
                                   in_=rr8(xof, TB)), 16)
            block.gpsimd(gp_st)

        def sy_fin(sync):
            wge(sync, ST)
        block.sync(sy_fin)


def emit_fin(p, d, xlast_f):
    nc, block = p.nc, p.block
    p.barrier()
    LD, MM, EV, AC, ST = (p.s(x) for x in ["LD", "MM", "EV", "AC", "ST"])
    es = ExitStack()
    with es:
        onesc = es.enter_context(nc.sbuf_tensor("fin_ones", [128, 1], BF16))
        onesf = es.enter_context(nc.sbuf_tensor("fin_onesf", [1, 128], F32))
        a_sb = es.enter_context(nc.sbuf_tensor("fin_a", [128, NKT * 512], F32))
        t_f = es.enter_context(nc.sbuf_tensor("fin_t", [128, NKT * 512], F32))
        tb = es.enter_context(nc.sbuf_tensor("fin_tb", [128, NKT * 512], BF16))
        sqb = es.enter_context(nc.sbuf_tensor("fin_sqb", [128, NKT * 512], BF16))
        rows = es.enter_context(nc.sbuf_tensor("fin_rows", [1, 5 * 512], F32))
        outf = es.enter_context(nc.sbuf_tensor("fin_outf", [128, NKT * 512], F32))
        psR1 = es.enter_context(nc.psum_tensor("fin_psR1", [1, 512], F32))
        psBCa = es.enter_context(nc.psum_tensor("fin_psBCa", [128, 512], F32))
        psBCb = es.enter_context(nc.psum_tensor("fin_psBCb", [128, 512], F32))

        def v0(vector):
            EV.inc(vector.memset(onesc[:], 1.0))
            EV.inc(vector.memset(onesf[:], 1.0))
        block.vector(v0)

        bufs = dict(t_f=t_f, tb=tb, sqb=sqb, rows=rows, onesc=onesc, onesf=onesf,
                    ps_sum=psR1[:], ps_sumsq=psBCa[0:1, :], ps_bcmu=psBCa[:],
                    ps_bcr=psBCb[:], outf=outf, outb=tb)

        for c in range(16):
            def sy_l(sync, c=c):
                wge(sync, EV)
                wge(sync, AC)
                wge(sync, ST)
                LD.inc(sync.dma_start(out=rr8(t_f, 512),
                                      in_=drview(d["x1f"], c * 512, (c + 1) * 512)), 16)
                LD.inc(sync.dma_start(out=rr8(a_sb, 512),
                                      in_=drview(xlast_f, c * 512, (c + 1) * 512)), 16)
            block.sync(sy_l)

            def v_add(vector):
                wge(vector, LD)
                EV.inc(vector.tensor_add(t_f[:], t_f[:], a_sb[:]))
            block.vector(v_add)

            emit_ln(p, d, bufs, 512, make_bf=False)

            def sy_st(sync, c=c):
                wge(sync, EV)
                ST.inc(sync.dma_start(out=drview(d["outT"], c * 512, (c + 1) * 512),
                                      in_=rr8(outf, 512)), 16)
            block.sync(sy_st)

        def sy_fin(sync):
            wge(sync, ST)
        block.sync(sy_fin)


def build_program(n_layers=NL, outer_only=False, dbg=(), attn_only=False):
    nc = bass.Bass("TRN2", target_bir_lowering=False)
    d = {}
    def scratch(name, shape, dtype):
        kind = "ExternalOutput" if name in dbg else "Internal"
        return nc.dram_tensor(name, shape, dtype, kind=kind).ap()
    d["xT"] = nc.dram_tensor("xT", [D, S], F32, kind="ExternalInput").ap()
    for nm in ["wq", "wk", "wv", "wo"]:
        d[nm] = nc.dram_tensor(nm, [D, D], F32, kind="ExternalInput").ap()
    for nm in ["bq_pt", "bk_pt", "bo_pt"]:
        d[nm] = nc.dram_tensor(nm, [128, 8], F32, kind="ExternalInput").ap()
    d["bv_bc"] = nc.dram_tensor("bv_bc", [128, D], F32, kind="ExternalInput").ap()
    d["wqkv"] = nc.dram_tensor("wqkv", [NL, D, 3072], F32, kind="ExternalInput").ap()
    d["wo_i"] = nc.dram_tensor("wo_i", [NL, D, D], F32, kind="ExternalInput").ap()
    d["w1"] = nc.dram_tensor("w1", [NL, D, DI2], F32, kind="ExternalInput").ap()
    d["w2"] = nc.dram_tensor("w2", [NL, DI, D], F32, kind="ExternalInput").ap()
    d["qs_bc"] = nc.dram_tensor("qs_bc", [NL, 128, DH], F32, kind="ExternalInput").ap()
    d["ks_bc"] = nc.dram_tensor("ks_bc", [NL, 128, DH], F32, kind="ExternalInput").ap()
    d["qs_sw"] = nc.dram_tensor("qs_sw", [NL, 128, DH], F32, kind="ExternalInput").ap()
    d["ks_sw"] = nc.dram_tensor("ks_sw", [NL, 128, DH], F32, kind="ExternalInput").ap()
    d["g_pt"] = nc.dram_tensor("g_pt", [NL, 128, 8], F32, kind="ExternalInput").ap()
    d["lb_pt"] = nc.dram_tensor("lb_pt", [NL, 128, 8], F32, kind="ExternalInput").ap()
    d["b1a_pt"] = nc.dram_tensor("b1a_pt", [NL, 128, NAT], F32, kind="ExternalInput").ap()
    d["b1g_pt"] = nc.dram_tensor("b1g_pt", [NL, 128, NAT], F32, kind="ExternalInput").ap()
    d["b2_pt"] = nc.dram_tensor("b2_pt", [NL, 128, 8], F32, kind="ExternalInput").ap()
    for nm in ["cs_q", "ss_q", "cs_k", "ss_k"]:
        d[nm] = nc.dram_tensor(nm, [S, DH], F32, kind="ExternalInput").ap()
    for nm in ["triA", "triB", "ident"]:
        d[nm] = nc.dram_tensor(nm, [128, 128], BF16, kind="ExternalInput").ap()
    d["qt8"] = scratch("qt8", [D, S], BF16)
    d["kts8"] = scratch("kts8", [D, S], BF16)
    d["qaug"] = scratch("qaug", [2, S], BF16)
    d["kaug"] = scratch("kaug", [2, S], BF16)
    d["vb"] = scratch("vb", [S, D], BF16)
    d["x1f"] = scratch("x1f", [D, S], F32)
    d["x1b"] = scratch("x1b", [D, S], BF16)
    d["xf"] = [scratch(f"xf{i}", [D, S], F32) for i in range(2)]
    d["xb"] = [scratch(f"xb{i}", [D, S], BF16) for i in range(2)]
    d["xnb"] = scratch("xnb", [D, S], BF16)
    d["hb"] = scratch("hb", [NAT * 128, S], BF16)
    d["outT"] = nc.dram_tensor("outT", [D, S], F32, kind="ExternalOutput").ap()

    if "attn_dbg" in dbg:
        d["attn_dbg"] = nc.dram_tensor("attn_dbg", [D, S], BF16, kind="ExternalOutput").ap()
    if "aT_dbg" in dbg:
        d["aT_dbg"] = nc.dram_tensor("aT_dbg", [D, S], BF16, kind="ExternalOutput").ap()
    d["cvals"] = nc.dram_tensor("cvals", [128, 8], F32, kind="ExternalInput").ap()
    es = ExitStack()
    with es:
        block = es.enter_context(nc.Block())
        p = Prog(nc, es, block)
        p.cv = es.enter_context(nc.sbuf_tensor("cvals_sb", [128, 8], F32))

        def _cv_load(sync):
            p.s("LD").inc(sync.dma_start(out=p.cv[:], in_=d["cvals"]), 16)
        block.sync(_cv_load)
        emit_s0(p, d)
        emit_s1(p, d)
        if outer_only:
            emit_fin(p, d, d["x1f"])
            return nc
        cur_f, cur_b = d["x1f"], d["x1b"]
        for li in range(n_layers):
            nxt_f, nxt_b = d["xf"][li % 2], d["xb"][li % 2]
            emit_attn(p, d, li, cur_f, cur_b, nxt_f, nxt_b)
            if attn_only:
                return nc
            emit_f1(p, d, li)
            emit_f2(p, d, li, nxt_f, nxt_f, nxt_b)
            cur_f, cur_b = nxt_f, nxt_b
        emit_fin(p, d, cur_f)
    return nc


_CONSTS = None


def _inputs_for_core(b, args, consts):
    (x, wq, bq, wk, bk, wv, bv, wo, bo, wqkv, qs, ks, wo_i, g, lb, w1, b1, w2, b2) = args
    m = {
        "xT": np.ascontiguousarray(x[b].T),
        "wq": wq, "wk": wk, "wv": wv, "wo": wo,
        "bq_pt": _col(bq), "bk_pt": _col(bk), "bo_pt": _col(bo),
        "bv_bc": np.ascontiguousarray(np.broadcast_to(bv, (128, D))),
        "wqkv": wqkv, "wo_i": wo_i, "w1": w1, "w2": w2,
        "qs_bc": np.ascontiguousarray(np.broadcast_to(qs[:, None, :], (NL, 128, DH))),
        "ks_bc": np.ascontiguousarray(np.broadcast_to(ks[:, None, :], (NL, 128, DH))),
        "qs_sw": np.ascontiguousarray(np.broadcast_to(np.roll(qs, -32, -1)[:, None, :], (NL, 128, DH))),
        "ks_sw": np.ascontiguousarray(np.broadcast_to(np.roll(ks, -32, -1)[:, None, :], (NL, 128, DH))),
        "g_pt": np.stack([_col(g[i]) for i in range(NL)]),
        "lb_pt": np.stack([_col(lb[i]) for i in range(NL)]),
        "b1a_pt": np.stack([_colpad(b1[i, :DI], NAT) for i in range(NL)]),
        "b1g_pt": np.stack([_colpad(b1[i, DI:], NAT) for i in range(NL)]),
        "b2_pt": np.stack([_col(b2[i]) for i in range(NL)]),
    }
    m.update(consts)
    return m


def kernel(L0_attn_x, L0_attn_Wq, L0_attn_bq, L0_attn_Wk, L0_attn_bk, L0_attn_Wv,
           L0_attn_bv, L0_attn_Wo, L0_attn_bo, L0_ffn_Wqkv, L0_ffn_q_scale,
           L0_ffn_k_scale, L0_ffn_Wo, L0_ffn_ln_g, L0_ffn_ln_b, L0_ffn_W1,
           L0_ffn_b1, L0_ffn_W2, L0_ffn_b2):
    global _CONSTS
    if _CONSTS is None:
        _CONSTS = _build_consts()
    args = tuple(np.ascontiguousarray(np.asarray(a, dtype=np.float32)) for a in (
        L0_attn_x, L0_attn_Wq, L0_attn_bq, L0_attn_Wk, L0_attn_bk, L0_attn_Wv,
        L0_attn_bv, L0_attn_Wo, L0_attn_bo, L0_ffn_Wqkv, L0_ffn_q_scale,
        L0_ffn_k_scale, L0_ffn_Wo, L0_ffn_ln_g, L0_ffn_ln_b, L0_ffn_W1,
        L0_ffn_b1, L0_ffn_W2, L0_ffn_b2))
    nc = build_program()
    in_maps = [_inputs_for_core(b, args, _CONSTS) for b in range(8)]
    res = run_bass_kernel_spmd(nc, in_maps, core_ids=list(range(8)))
    out = np.stack([np.ascontiguousarray(np.asarray(r["outT"]).T) for r in res.results])
    return out.astype(np.float32)
